# version 05
# baseline (speedup 1.0000x reference)
import math
from contextlib import ExitStack

import numpy as np
import ml_dtypes

import concourse.bass as bass
import concourse.mybir as mybir
from concourse.bass_utils import run_bass_kernel_spmd

F32 = mybir.dt.float32
BF16 = mybir.dt.bfloat16
FP16 = mybir.dt.float16
I32 = mybir.dt.int32
AF = mybir.ActivationFunctionType
ALU = mybir.AluOpType
AX = mybir.AxisListType

D = 4096
S = 8192
H = 32
DH = 128
HFF = 11008
BS = 512
NCORES = 8
TOK = 1024
DC = 32
GFF = 86
NKC = 96
NPAN = 43
SCALE = 1.0 / math.sqrt(192.0)
EXPB = -2.0
EPS = 1e-5

PHASES = 6


def make_tables(core):
    b0, b1 = core, 15 - core
    pb = np.zeros((128, NKC), dtype=np.float32)
    for m in range(NKC):
        if m < 4 or m >= 92:
            real = True
        elif m < 32:
            real = (m - 4) < 4 * b0
        else:
            real = (m - 32) < 4 * b1
        pb[:, m] = 1.0 if real else 0.0
    return pb


def make_consts():
    ident_bf = np.eye(128, dtype=ml_dtypes.bfloat16)
    ident_f32 = np.eye(128, dtype=np.float32)
    masks = np.zeros((128, 4 * 512), dtype=ml_dtypes.bfloat16)
    kr = np.arange(128)[:, None]
    q = np.arange(512)[None, :]
    for w in range(4):
        masks[:, w * 512:(w + 1) * 512] = (q >= w * 128 + kr).astype(ml_dtypes.bfloat16)
    oh32 = np.zeros((32, 32 * 128), dtype=np.float16)
    for g in range(32):
        oh32[g, g * 128:(g + 1) * 128] = 1.0
    ohD = np.zeros((GFF, GFF * 128), dtype=np.float16)
    for g in range(GFF):
        ohD[g, g * 128:(g + 1) * 128] = 1.0
    ones1r = np.ones((1, 128), dtype=np.float16)
    return ident_bf, ident_f32, masks, oh32, ohD, ones1r


def build_nc():
    nc = bass.Bass("TRN2", target_bir_lowering=False)

    x_in = nc.dram_tensor("x", [TOK, D], F32, kind="ExternalInput").ap()
    wq = nc.dram_tensor("wq", [D, 6144], F32, kind="ExternalInput").ap()
    wkv = nc.dram_tensor("wkv", [D, 8256], F32, kind="ExternalInput").ap()
    wo = nc.dram_tensor("wo", [D, D], F32, kind="ExternalInput").ap()
    qwg = nc.dram_tensor("qwg", [D, HFF], I32, kind="ExternalInput").ap()
    qwu = nc.dram_tensor("qwu", [D, HFF], I32, kind="ExternalInput").ap()
    qwd = nc.dram_tensor("qwd", [HFF, D], I32, kind="ExternalInput").ap()
    zg = nc.dram_tensor("zg", [32, HFF], I32, kind="ExternalInput").ap()
    zu = nc.dram_tensor("zu", [32, HFF], I32, kind="ExternalInput").ap()
    zd = nc.dram_tensor("zd", [GFF, D], I32, kind="ExternalInput").ap()
    sg = nc.dram_tensor("sg", [32, HFF], F32, kind="ExternalInput").ap()
    su = nc.dram_tensor("su", [32, HFF], F32, kind="ExternalInput").ap()
    sd = nc.dram_tensor("sd", [GFF, D], F32, kind="ExternalInput").ap()
    tbl_in = nc.dram_tensor("tbl", [128, NKC], F32, kind="ExternalInput").ap()
    y_out = nc.dram_tensor("y", [TOK, D], F32, kind="ExternalOutput").ap()

    kvK_in = nc.dram_tensor("kvK_in", [H * 8 * 128, 128], BF16).ap()
    kvV_in = nc.dram_tensor("kvV_in", [H * TOK, DH], BF16).ap()
    kvK_out = nc.dram_tensor("kvK_out", [NCORES * H * 8 * 128, 128], BF16, addr_space="Shared").ap()
    kvV_out = nc.dram_tensor("kvV_out", [NCORES * H * TOK, DH], BF16, addr_space="Shared").ap()
    qT_dram = nc.dram_tensor("qT_dram", [H * DH, TOK], BF16).ap()
    x1_dram = nc.dram_tensor("x1_dram", [TOK, D], F32).ap()
    hact_dram = nc.dram_tensor("hact_dram", [HFF, TOK], BF16).ap()
    wd_dram = nc.dram_tensor("wd_dram", [DC, HFF * 128], BF16).ap()
    nzg_dram = nc.dram_tensor("nzg_dram", [32, HFF], BF16).ap()
    nzu_dram = nc.dram_tensor("nzu_dram", [32, HFF], BF16).ap()
    nzd_dram = nc.dram_tensor("nzd_dram", [GFF, D], BF16).ap()
    qwg_bf = nc.dram_tensor("qwg_bf", [D, HFF], BF16).ap()
    qwu_bf = nc.dram_tensor("qwu_bf", [D, HFF], BF16).ap()
    qwd_bf = nc.dram_tensor("qwd_bf", [HFF, D], BF16).ap()

    ident_bf_np, ident_f32_np, masks_np, oh32_np, ohD_np, ones1r_np = make_consts()
    ident_bf_d = nc.inline_tensor(ident_bf_np, "c_ident_bf").ap()
    ident_f32_d = nc.inline_tensor(ident_f32_np, "c_ident_f32").ap()
    masks_d = nc.inline_tensor(masks_np, "c_masks").ap()
    oh32_d = nc.inline_tensor(oh32_np, "c_oh32").ap()
    ohD_d = nc.inline_tensor(ohD_np, "c_ohD").ap()
    ones1r_d = nc.inline_tensor(ones1r_np, "c_ones1r").ap()

    es = ExitStack()

    ident_bf = nc.alloc_sbuf_tensor("ident_bf_sb", [128, 128], BF16)
    ident_f32 = nc.alloc_sbuf_tensor("ident_f32_sb", [128, 128], F32)
    masks_sb = nc.alloc_sbuf_tensor("masks_sb", [128, 2048], BF16)
    oh32_sb = nc.alloc_sbuf_tensor("oh32_sb", [32, 4096], FP16)
    ones1r_sb = nc.alloc_sbuf_tensor("ones1r_sb", [1, 128], FP16)
    tbl_sb = nc.alloc_sbuf_tensor("tbl_sb", [128, NKC], F32)
    ones_col = nc.alloc_sbuf_tensor("ones_col", [128, 1], BF16)
    zero_col = nc.alloc_sbuf_tensor("zero_col", [128, 1], BF16)
    zero_t = nc.alloc_sbuf_tensor("zero_t", [128, 128], BF16)
    expb_sb = nc.alloc_sbuf_tensor("expb_sb", [128, 1], F32)

    single = [
        "PH", "ld_c", "ld_z", "ld_s", "s_nz", "st_nz",
        "ln_sq", "ln_v1", "ln_sqr", "ln_n", "ln_t", "ln_e",
        "p2mm", "p2ev", "p2tr", "p2ve", "cc",
        "se", "ex", "ep", "pa", "siv", "p3bc", "sbe", "sat",
        "wo_mm", "wo_ev", "wo_tr", "wo_add",
        "xg_mm", "xg_ev",
        "g_bc", "g_dq", "g_mm", "s_silu", "s_hm",
        "d_bc", "d_dq", "ld_sd",
        "ag_mm", "ag_ev", "ld_ha",
        "d6_mm", "d6_ev", "d6_tr", "d6_add",
        "vv", "pcast", "pcast2", "dbg",
    ]
    paired = [
        "ln_ld", "ld_w", "st_k", "st_qq", "st_v", "ld_kv",
        "ld_wo", "ld_xp", "st_x1", "st_xg",
        "ld_ff", "ld_nz", "ld_qd", "ld_oh", "st_ha", "st_wd",
        "ld_wd", "ld_nzd", "ld_x1", "st_ag", "st_out",
    ]
    sems = {}
    for n in single:
        sems[n] = es.enter_context(nc.semaphore(n))
    for n in paired:
        sems[n] = [es.enter_context(nc.semaphore(n + "0")),
                   es.enter_context(nc.semaphore(n + "1"))]

    class SM:
        pass
    for n, h_ in sems.items():
        setattr(SM, n, h_)

    blk = es.enter_context(nc.Block())

    ph_target = [0]
    vvc = [0]

    def barrier():
        ph_target[0] += 5
        tgt = ph_target[0]
        for eng in ("sync", "scalar", "vector", "tensor", "gpsimd"):
            @getattr(blk, eng)
            def _(e):
                e.drain()
                e.nop().then_inc(SM.PH, 1)
                e.wait_ge(SM.PH, tgt)

    def phase0():
        @blk.sync
        def _(sy):
            sy.dma_start(out=ident_bf[:, :], in_=ident_bf_d[:, :]).then_inc(SM.ld_c, 16)
            sy.dma_start(out=ident_f32[:, :], in_=ident_f32_d[:, :]).then_inc(SM.ld_c, 16)
            sy.dma_start(out=masks_sb[:, :], in_=masks_d[:, :]).then_inc(SM.ld_c, 16)
            sy.dma_start(out=oh32_sb[:, :], in_=oh32_d[:, :]).then_inc(SM.ld_c, 16)
            sy.dma_start(out=ones1r_sb[:, :], in_=ones1r_d[:, :]).then_inc(SM.ld_c, 16)
            sy.dma_start(out=tbl_sb[:, :], in_=tbl_in[:, :]).then_inc(SM.ld_c, 16)
            sy.wait_ge(SM.ld_c, 96)

        @blk.vector
        def _(v):
            v.memset(ones_col[:, :], 1.0)
            v.memset(zero_col[:, :], 0.0)
            v.memset(zero_t[:, :], 0.0)
            v.memset(expb_sb[:, :], EXPB)

        specs = [(zg, sg, nzg_dram, 32, HFF), (zu, su, nzu_dram, 32, HFF),
                 (zd, sd, nzd_dram, GFF, D)]
        with (
            nc.sbuf_tensor("nz_z", [GFF, HFF], BF16) as z_sb,
            nc.sbuf_tensor("nz_s", [GFF, HFF], F32) as s_sb,
            nc.sbuf_tensor("nz_o", [GFF, HFF], BF16) as o_sb,
        ):
            @blk.gpsimd
            def _(g):
                for j, (z_d, s_d, nz_d, P, F) in enumerate(specs):
                    if j >= 1:
                        g.wait_ge(SM.s_nz, j)
                    g.dma_start(out=z_sb[0:P, 0:F], in_=z_d[:, :]).then_inc(SM.ld_z, 16)

            @blk.sync
            def _(sy):
                for j, (z_d, s_d, nz_d, P, F) in enumerate(specs):
                    if j >= 1:
                        sy.wait_ge(SM.s_nz, j)
                    sy.dma_start(out=s_sb[0:P, 0:F], in_=s_d[:, :]).then_inc(SM.ld_s, 16)

            @blk.vector
            def _(v):
                for j, (z_d, s_d, nz_d, P, F) in enumerate(specs):
                    v.wait_ge(SM.ld_z, 16 * (j + 1))
                    v.wait_ge(SM.ld_s, 16 * (j + 1))
                    if j >= 1:
                        v.wait_ge(SM.st_nz, 16 * j)
                    v.scalar_tensor_tensor(o_sb[0:P, 0:F], z_sb[0:P, 0:F], -1.0,
                                           s_sb[0:P, 0:F], ALU.mult, ALU.mult
                                           ).then_inc(SM.s_nz, 1)

            @blk.scalar
            def _(s):
                for j, (z_d, s_d, nz_d, P, F) in enumerate(specs):
                    s.wait_ge(SM.s_nz, j + 1)
                    s.dma_start(out=nz_d[:, :], in_=o_sb[0:P, 0:F]).then_inc(SM.st_nz, 16)
                s.wait_ge(SM.st_nz, 48)
        barrier()

    ln_ctr = {"ld0": 0, "ld1": 0, "sq": 0, "v1": 0, "sqr": 0, "n": 0, "t": 0, "e": 0}

    def emit_ln(src_dram, xnT, tag):
        c = dict(ln_ctr)
        with (
            nc.sbuf_tensor(f"ln_xt_{tag}", [128, 2 * D], F32) as xt,
            nc.sbuf_tensor(f"ln_dump_{tag}", [128, D], BF16) as dump,
            nc.sbuf_tensor(f"ln_xn_{tag}", [128, 2 * D], BF16) as xn,
            nc.sbuf_tensor(f"ln_st_{tag}", [128, 8], F32) as st,
            nc.psum_tensor(f"ln_ps_{tag}", [128, 2048], BF16) as ptr,
        ):
            @blk.sync
            def _(sy):
                for tc in range(8):
                    if tc >= 2:
                        sy.wait_ge(SM.ln_n, c["n"] + tc - 1)
                        sy.wait_ge(SM.ln_sq, c["sq"] + tc - 1)
                    sy.dma_start(out=xt[:, (tc % 2) * D:(tc % 2 + 1) * D],
                                 in_=src_dram[tc * 128:(tc + 1) * 128, :]
                                 ).then_inc(SM.ln_ld[tc % 2], 16)

            @blk.scalar
            def _(s):
                for tc in range(8):
                    xv = xt[:, (tc % 2) * D:(tc % 2 + 1) * D]
                    s.wait_ge(SM.ln_ld[tc % 2], c["ld%d" % (tc % 2)] + 16 * (tc // 2 + 1))
                    s.activation(dump[:, :], xv, AF.Square,
                                 accum_out=st[:, 1:2]).then_inc(SM.ln_sq, 1)
                    s.wait_ge(SM.ln_v1, c["v1"] + tc + 1)
                    s.activation(st[:, 7:8], st[:, 6:7], AF.Sqrt).then_inc(SM.ln_sqr, 1)

            @blk.vector
            def _(v):
                for tc in range(9):
                    if tc < 8:
                        xv = xt[:, (tc % 2) * D:(tc % 2 + 1) * D]
                        xnv = xn[:, (tc % 2) * D:(tc % 2 + 1) * D]
                        v.wait_ge(SM.ln_ld[tc % 2], c["ld%d" % (tc % 2)] + 16 * (tc // 2 + 1))
                        v.reduce_sum(st[:, 0:1], xv, axis=AX.X).then_inc(SM.vv, 1)
                        vvc[0] += 1
                        v.wait_ge(SM.ln_sq, c["sq"] + tc + 1)
                        v.wait_ge(SM.vv, vvc[0])
                        v.tensor_scalar_mul(st[:, 2:3], st[:, 0:1], 1.0 / D).then_inc(SM.vv, 1)
                        vvc[0] += 1
                        v.tensor_scalar_mul(st[:, 3:4], st[:, 1:2], 1.0 / D).then_inc(SM.vv, 1)
                        vvc[0] += 1
                        v.wait_ge(SM.vv, vvc[0])
                        v.tensor_tensor(st[:, 5:6], st[:, 2:3], st[:, 2:3], ALU.mult).then_inc(SM.vv, 1)
                        vvc[0] += 1
                        v.wait_ge(SM.vv, vvc[0])
                        v.tensor_tensor(st[:, 4:5], st[:, 3:4], st[:, 5:6], ALU.subtract).then_inc(SM.vv, 1)
                        vvc[0] += 1
                        v.wait_ge(SM.vv, vvc[0])
                        v.tensor_scalar_add(st[:, 6:7], st[:, 4:5], EPS).then_inc(SM.ln_v1, 1)
                        v.wait_ge(SM.ln_sqr, c["sqr"] + tc + 1)
                        v.reciprocal(st[:, 5:6], st[:, 7:8]).then_inc(SM.vv, 1)
                        vvc[0] += 1
                        if tc >= 2:
                            v.wait_ge(SM.ln_t, c["t"] + 4 * (tc - 1))
                        v.wait_ge(SM.vv, vvc[0])
                        v.tensor_scalar(xnv, xv, st[:, 2:3], st[:, 5:6],
                                        ALU.subtract, ALU.mult).then_inc(SM.ln_n, 1)
                    if tc >= 1:
                        for grp in range(4):
                            gidx = (tc - 1) * 4 + grp
                            v.wait_ge(SM.ln_t, c["t"] + gidx + 1)
                            v.tensor_copy(
                                bass.AP(xnT, (grp * 8) * TOK + (tc - 1) * 128,
                                        [[DC * TOK, 128], [TOK, 8], [1, 128]]),
                                ptr[:, (gidx % 2) * 1024:(gidx % 2 + 1) * 1024].rearrange(
                                    "p (j e) -> p j e", j=8)
                            ).then_inc(SM.ln_e, 1)

            @blk.tensor
            def _(t):
                for tc in range(8):
                    off = (tc % 2) * D
                    t.wait_ge(SM.ln_n, c["n"] + tc + 1)
                    for grp in range(4):
                        gidx = tc * 4 + grp
                        if gidx >= 2:
                            t.wait_ge(SM.ln_e, c["e"] + gidx - 1)
                        for jj in range(8):
                            j = grp * 8 + jj
                            ins = t.transpose(
                                ptr[:, (gidx % 2) * 1024 + jj * 128:(gidx % 2) * 1024 + (jj + 1) * 128],
                                xn[:, off + j * 128:off + (j + 1) * 128],
                                ident_bf[:, :])
                            if jj == 7:
                                ins.then_inc(SM.ln_t, 1)

        ln_ctr["ld0"] += 64
        ln_ctr["ld1"] += 64
        ln_ctr["sq"] += 8
        ln_ctr["v1"] += 8
        ln_ctr["sqr"] += 8
        ln_ctr["n"] += 8
        ln_ctr["t"] += 32
        ln_ctr["e"] += 32

    def phase12():
        es12 = ExitStack()
        xn1T = es12.enter_context(nc.sbuf_tensor("xn1T", [128, DC * TOK], BF16))
        emit_ln(x_in, xn1T, "ln1")
        barrier()

        if PHASES == 1:
            @blk.gpsimd
            def _(g):
                for j in range(8):
                    g.dma_start(out=bass.AP(y_out.tensor, j * 128 * D, [[D, 128], [1, D]]),
                                in_=xn1T[:, j * 4096:(j + 1) * 4096]).then_inc(SM.dbg, 16)
                g.wait_ge(SM.dbg, 128)
            es12.close()
            return

        with (
            nc.sbuf_tensor("wpan", [128, 2 * 4096], BF16) as wpan,
            nc.sbuf_tensor("qkst", [128, 2 * 1024], BF16) as qkst,
            nc.sbuf_tensor("vtst", [128, 2 * 1024], BF16) as vtst,
            nc.psum_tensor("pQ", [128, 2048], F32) as pQ,
            nc.psum_tensor("pVt", [128, 2048], BF16) as pVt,
        ):
            def w_src(oc):
                if oc < 32:
                    return bass.AP(wkv.tensor, oc * 128,
                                   [[8256, 128], [8256 * 128, 32], [1, 128]])
                if oc < 64:
                    return bass.AP(wkv.tensor, 4096 + (oc - 32) * 128,
                                   [[8256, 128], [8256 * 128, 32], [1, 128]])
                return bass.AP(wq.tensor, (oc - 64) * 192,
                               [[6144, 128], [6144 * 128, 32], [1, 128]])

            @blk.gpsimd
            def _(g):
                for oc in range(96):
                    if oc >= 2:
                        g.wait_ge(SM.p2mm, oc - 1)
                    g.dma_start(out=wpan[:, (oc % 2) * 4096:(oc % 2 + 1) * 4096]
                                .rearrange("p (cc e) -> p cc e", cc=32),
                                in_=w_src(oc)).then_inc(SM.ld_w[oc % 2], 16)

            @blk.tensor
            def _(t):
                for oc in range(96):
                    t.wait_ge(SM.ld_w[oc % 2], 16 * (oc // 2 + 1))
                    if oc >= 2:
                        t.wait_ge(SM.p2ev, oc - 1)
                    pb = (oc % 2) * 1024
                    for dcc in range(DC):
                        lhs = wpan[:, (oc % 2) * 4096 + dcc * 128:(oc % 2) * 4096 + (dcc + 1) * 128]
                        for tt in range(2):
                            ins = t.matmul(pQ[:, pb + tt * 512:pb + (tt + 1) * 512],
                                           lhs,
                                           xn1T[:, dcc * TOK + tt * 512:dcc * TOK + (tt + 1) * 512],
                                           start=(dcc == 0), stop=(dcc == DC - 1))
                            if dcc == DC - 1 and tt == 1:
                                ins.then_inc(SM.p2mm, 1)
                    if 33 <= oc <= 64:
                        hh = oc - 33
                        t.wait_ge(SM.p2ev, (32 + hh) + 1)
                        if hh >= 2:
                            t.wait_ge(SM.p2ve, hh - 1)
                        for tc2 in range(8):
                            ins = t.transpose(
                                pVt[:, (hh % 2) * 1024 + tc2 * 128:(hh % 2) * 1024 + (tc2 + 1) * 128],
                                qkst[:, ((32 + hh) % 2) * 1024 + tc2 * 128:
                                     ((32 + hh) % 2) * 1024 + (tc2 + 1) * 128],
                                ident_bf[:, :])
                            if tc2 == 7:
                                ins.then_inc(SM.p2tr, 1)

            @blk.scalar
            def _(s):
                for oc in range(96):
                    s.wait_ge(SM.p2mm, oc + 1)
                    if oc >= 2:
                        prev = oc - 2
                        if prev < 32:
                            s.wait_ge(SM.st_k[prev % 2], 16 * (prev // 2 + 1))
                        elif prev < 64:
                            s.wait_ge(SM.p2tr, prev - 32 + 1)
                        else:
                            s.wait_ge(SM.st_qq[prev % 2], 16 * ((prev - 64) // 2 + 1))
                    s.activation(qkst[:, (oc % 2) * 1024:(oc % 2 + 1) * 1024],
                                 pQ[:, (oc % 2) * 1024:(oc % 2 + 1) * 1024],
                                 AF.Copy).then_inc(SM.p2ev, 1)

            @blk.sync
            def _(sy):
                for oc in range(96):
                    src = qkst[:, (oc % 2) * 1024:(oc % 2 + 1) * 1024]
                    if 32 <= oc < 64:
                        hh = oc - 32
                        sy.wait_ge(SM.p2ve, hh + 1)
                        sy.dma_start(out=bass.AP(kvV_in.tensor, hh * TOK * DH,
                                                 [[DH, 128], [128 * DH, 8], [1, DH]]),
                                     in_=vtst[:, (hh % 2) * 1024:(hh % 2 + 1) * 1024]
                                     .rearrange("p (tc e) -> p tc e", tc=8)
                                     ).then_inc(SM.st_v[hh % 2], 16)
                    if oc < 32:
                        sy.wait_ge(SM.p2ev, oc + 1)
                        sy.dma_start(out=bass.AP(kvK_in.tensor, oc * 8 * 16384,
                                                 [[128, 128], [16384, 8], [1, 128]]),
                                     in_=src.rearrange("p (c e) -> p c e", c=8)
                                     ).then_inc(SM.st_k[oc % 2], 16)
                    elif oc >= 64:
                        sy.wait_ge(SM.p2ev, oc + 1)
                        sy.dma_start(out=qT_dram[(oc - 64) * 128:(oc - 63) * 128, :], in_=src
                                     ).then_inc(SM.st_qq[oc % 2], 16)

            @blk.vector
            def _(v):
                for hh in range(32):
                    v.wait_ge(SM.p2tr, hh + 1)
                    if hh >= 2:
                        v.wait_ge(SM.st_v[hh % 2], 16 * (hh // 2))
                    v.tensor_copy(vtst[:, (hh % 2) * 1024:(hh % 2 + 1) * 1024],
                                  pVt[:, (hh % 2) * 1024:(hh % 2 + 1) * 1024]
                                  ).then_inc(SM.p2ve, 1)

            @blk.gpsimd
            def _(g):
                g.wait_ge(SM.st_k[0], 16 * 16)
                g.wait_ge(SM.st_k[1], 16 * 16)
                g.wait_ge(SM.st_v[0], 16 * 16)
                g.wait_ge(SM.st_v[1], 16 * 16)
                g.collective_compute("AllGather", ALU.bypass,
                                     replica_groups=[list(range(NCORES))],
                                     ins=[kvK_in], outs=[kvK_out]).then_inc(SM.cc, 1)
                g.collective_compute("AllGather", ALU.bypass,
                                     replica_groups=[list(range(NCORES))],
                                     ins=[kvV_in], outs=[kvV_out]).then_inc(SM.cc, 1)

            @blk.sync
            def _(sy):
                sy.wait_ge(SM.st_qq[0], 16 * 16)
                sy.wait_ge(SM.st_qq[1], 16 * 16)
                sy.wait_ge(SM.cc, 2)
        es12.close()
        barrier()

        if PHASES == 2:
            @blk.gpsimd
            def _(g):
                g.dma_start(out=bass.AP(y_out.tensor, 0, [[1024, 4096], [1, 1024]]),
                            in_=bass.AP(qT_dram.tensor, 0, [[1024, 4096], [1, 1024]])
                            ).then_inc(SM.dbg, 16)
                g.wait_ge(SM.dbg, 16)

    def phase3(attnT):
        DIAG = set(range(4)) | set(range(92, 96))
        SLOTS = NKC
        FK = 2 * SLOTS * 128
        with (
            nc.sbuf_tensor("K_sb", [128, FK], BF16) as K_sb,
            nc.sbuf_tensor("V_sb", [128, FK], BF16) as V_sb,
            nc.sbuf_tensor("Q_sb", [128, 2 * TOK], BF16) as Q_sb,
            nc.sbuf_tensor("probs", [128, 4 * 512], BF16) as probs,
            nc.sbuf_tensor("invs", [1, 1024], FP16) as invs,
            nc.sbuf_tensor("invs32", [1, 1024], F32) as invs32,
            nc.sbuf_tensor("bcs", [128, 1024], F32) as bcs,
            nc.psum_tensor("s_ps", [128, 2048], F32) as s_ps,
            nc.psum_tensor("a_ps", [128, 1024], F32) as a_ps,
            nc.psum_tensor("z_ps", [33, 512], F32) as z_ps,
            nc.psum_tensor("b_ps", [128, 512], F32) as b_ps,
        ):
            @blk.gpsimd
            def _(g):
                jobs = ([(qwg, qwg_bf, j, 128) for j in range(32)] +
                        [(qwu, qwu_bf, j, 128) for j in range(32)] +
                        [(qwd, qwd_bf, j, 128) for j in range(GFF)])
                for n, (qw_d, qw_b, j, rws) in enumerate(jobs):
                    g.wait_ge(SM.pa, SLOTS * min(n // 5, 31))
                    g.dma_start(out=qw_b[j * rws:(j + 1) * rws, :],
                                in_=qw_d[j * rws:(j + 1) * rws, :]
                                ).then_inc(SM.pcast, 16)

            @blk.sync
            def _(sy):
                for h in range(H):
                    if h >= 2:
                        sy.wait_ge(SM.pa, SLOTS * (h - 1))
                    hk = (h % 2) * SLOTS * 128
                    for kv_out, kv_loc, sbuf_t in ((kvK_out, kvK_in, K_sb),
                                                   (kvV_out, kvV_in, V_sb)):
                        sy.dma_start(
                            out=bass.AP(sbuf_t, hk, [[FK, 128], [128, 4], [1, 128]]),
                            in_=bass.AP(kv_loc.tensor, (h * 8) * 16384,
                                        [[128, 128], [16384, 4], [1, 128]]),
                        ).then_inc(SM.ld_kv[h % 2], 16)
                        for gb in range(7):
                            sy.dma_start(
                                out=bass.AP(sbuf_t, hk + (4 + gb * 4) * 128,
                                            [[FK, 128], [128, 4], [1, 128]]),
                                in_=bass.AP(kv_out.tensor, (gb * H * 8 + h * 8) * 16384,
                                            [[128, 128], [16384, 4], [1, 128]]),
                            ).then_inc(SM.ld_kv[h % 2], 16)
                        for gb in range(15):
                            if gb < 8:
                                soff = (gb * H * 8 + h * 8) * 16384
                            else:
                                soff = ((15 - gb) * H * 8 + h * 8 + 4) * 16384
                            sy.dma_start(
                                out=bass.AP(sbuf_t, hk + (32 + gb * 4) * 128,
                                            [[FK, 128], [128, 4], [1, 128]]),
                                in_=bass.AP(kv_out.tensor, soff,
                                            [[128, 128], [16384, 4], [1, 128]]),
                            ).then_inc(SM.ld_kv[h % 2], 16)
                        sy.dma_start(
                            out=bass.AP(sbuf_t, hk + 92 * 128, [[FK, 128], [128, 4], [1, 128]]),
                            in_=bass.AP(kv_loc.tensor, (h * 8 + 4) * 16384,
                                        [[128, 128], [16384, 4], [1, 128]]),
                        ).then_inc(SM.ld_kv[h % 2], 16)
                    sy.dma_start(out=Q_sb[:, (h % 2) * TOK:(h % 2 + 1) * TOK],
                                 in_=qT_dram[h * 128:(h + 1) * 128, :]
                                 ).then_inc(SM.ld_kv[h % 2], 16)

            @blk.tensor
            def _(t):
                for h in range(H):
                    hk = (h % 2) * SLOTS * 128
                    hq = (h % 2) * TOK
                    t.wait_ge(SM.ld_kv[h % 2], 16 * 49 * (h // 2 + 1))
                    if h >= 1:
                        t.wait_ge(SM.sat, 2 * h)
                        t.wait_ge(SM.siv, 2 * h)
                    for qb in range(2):
                        t.matmul(bass.AP(a_ps, qb * 512, [[1024, 128], [1, 512]]),
                                 zero_t[:, :], masks_sb[:, 0:512],
                                 start=True, stop=True, skip_group_check=True)
                    t.matmul(z_ps[0:33, :], zero_t[:, 0:33], masks_sb[:, 0:512],
                             start=True, stop=True, skip_group_check=True)

                    def emit_s(m):
                        qb = 0 if m < 32 else 1
                        if m >= 4:
                            t.wait_ge(SM.ex, SLOTS * h + m - 3)
                        t.matmul(s_ps[:, (m % 4) * 512:(m % 4 + 1) * 512],
                                 K_sb[:, hk + m * 128:hk + (m + 1) * 128],
                                 Q_sb[:, hq + qb * 512:hq + (qb + 1) * 512],
                                 start=True, stop=True, skip_group_check=True
                                 ).then_inc(SM.se, 1)

                    emit_s(0)
                    emit_s(1)
                    for m in range(SLOTS):
                        qb = 0 if m < 32 else 1
                        if m + 2 < SLOTS:
                            emit_s(m + 2)
                        t.wait_ge(SM.ep, SLOTS * h + m + 1)
                        t.matmul(bass.AP(z_ps, qb * 32 * 512, [[512, 1], [1, 512]]),
                                 ones_col[:, :],
                                 probs[:, (m % 4) * 512:(m % 4 + 1) * 512],
                                 start=False, stop=False, skip_group_check=True)
                        t.matmul(bass.AP(a_ps, qb * 512, [[1024, 128], [1, 512]]),
                                 V_sb[:, hk + m * 128:hk + (m + 1) * 128],
                                 probs[:, (m % 4) * 512:(m % 4 + 1) * 512],
                                 start=False, stop=False, skip_group_check=True
                                 ).then_inc(SM.pa, 1)
                    for qb in range(2):
                        t.wait_ge(SM.siv, 2 * h + qb + 1)
                        if 2 * h + qb >= 1:
                            t.wait_ge(SM.sbe, 2 * h + qb)
                        t.matmul(b_ps[:, :], ones1r_sb[:, :],
                                 invs[0:1, qb * 512:(qb + 1) * 512],
                                 start=True, stop=True, skip_group_check=True
                                 ).then_inc(SM.p3bc, 1)

            @blk.scalar
            def _(s):
                for h in range(H):
                    for m in range(SLOTS):
                        s.wait_ge(SM.se, SLOTS * h + m + 1)
                        if SLOTS * h + m >= 4:
                            s.wait_ge(SM.pa, SLOTS * h + m - 3)
                        s.activation(probs[:, (m % 4) * 512:(m % 4 + 1) * 512],
                                     s_ps[:, (m % 4) * 512:(m % 4 + 1) * 512],
                                     AF.Exp, bias=expb_sb[:, :], scale=SCALE
                                     ).then_inc(SM.ex, 1)

            @blk.vector
            def _(v):
                for h in range(H):
                    for m in range(SLOTS):
                        v.wait_ge(SM.ex, SLOTS * h + m + 1)
                        pr = probs[:, (m % 4) * 512:(m % 4 + 1) * 512]
                        if m in DIAG:
                            w = m if m < 4 else m - 92
                            v.tensor_tensor(pr, pr, masks_sb[:, w * 512:(w + 1) * 512],
                                            ALU.mult).then_inc(SM.ep, 1)
                        else:
                            v.tensor_scalar_mul(pr, pr, tbl_sb[:, m:m + 1]
                                                ).then_inc(SM.ep, 1)
                    v.wait_ge(SM.pa, SLOTS * (h + 1))
                    for qb in range(2):
                        v.reciprocal(invs32[0:1, qb * 512:(qb + 1) * 512],
                                     z_ps[32 * qb:32 * qb + 1, :]).then_inc(SM.vv, 1)
                        vvc[0] += 1
                        v.wait_ge(SM.vv, vvc[0])
                        v.tensor_copy(invs[0:1, qb * 512:(qb + 1) * 512],
                                      invs32[0:1, qb * 512:(qb + 1) * 512]
                                      ).then_inc(SM.siv, 1)
                    for qb in range(2):
                        v.wait_ge(SM.p3bc, 2 * h + qb + 1)
                        v.tensor_copy(bcs[:, qb * 512:(qb + 1) * 512], b_ps[:, :]
                                      ).then_inc(SM.sbe, 1)
                        v.wait_ge(SM.sbe, 2 * h + qb + 1)
                        v.tensor_tensor(attnT[:, h * TOK + qb * 512:h * TOK + (qb + 1) * 512],
                                        a_ps[:, qb * 512:(qb + 1) * 512],
                                        bcs[:, qb * 512:(qb + 1) * 512],
                                        ALU.mult).then_inc(SM.sat, 1)
        barrier()

    def phase4(attnT):
        with (
            nc.sbuf_tensor("wopan", [128, 2 * 4096], BF16) as wopan,
            nc.sbuf_tensor("wsb", [128, 2 * 1024], F32) as wsb,
            nc.sbuf_tensor("xp4", [128, 2 * 512], F32) as xp,
            nc.sbuf_tensor("x1p", [128, 2 * 512], F32) as x1p,
            nc.psum_tensor("pW", [128, 2048], F32) as pW,
            nc.psum_tensor("pT4", [128, 1024], F32) as pT,
        ):
            @blk.gpsimd
            def _(g):
                for dc in range(DC):
                    if dc >= 2:
                        g.wait_ge(SM.wo_mm, dc - 1)
                    g.dma_start(out=wopan[:, (dc % 2) * 4096:(dc % 2 + 1) * 4096]
                                .rearrange("p (cc e) -> p cc e", cc=32),
                                in_=bass.AP(wo.tensor, dc * 128,
                                            [[D, 128], [D * 128, 32], [1, 128]])
                                ).then_inc(SM.ld_wo[dc % 2], 16)

            @blk.sync
            def _(sy):
                for gidx in range(64):
                    dc, gg = gidx // 2, gidx % 2
                    if gidx >= 2:
                        sy.wait_ge(SM.wo_add, gidx - 1)
                    sy.dma_start(out=xp[:, (gidx % 2) * 512:(gidx % 2 + 1) * 512]
                                 .rearrange("p (tc e) -> p tc e", tc=4),
                                 in_=bass.AP(x_in.tensor, gg * 512 * D + dc * 128,
                                             [[D, 128], [128 * D, 4], [1, 128]])
                                 ).then_inc(SM.ld_xp[gidx % 2], 16)

            @blk.tensor
            def _(t):
                for dc in range(DC):
                    t.wait_ge(SM.ld_wo[dc % 2], 16 * (dc // 2 + 1))
                    if dc >= 2:
                        t.wait_ge(SM.wo_ev, dc - 1)
                    pb = (dc % 2) * 1024
                    for hc in range(32):
                        lhs = wopan[:, (dc % 2) * 4096 + hc * 128:(dc % 2) * 4096 + (hc + 1) * 128]
                        for tt in range(2):
                            ins = t.matmul(pW[:, pb + tt * 512:pb + (tt + 1) * 512],
                                           lhs,
                                           attnT[:, hc * TOK + tt * 512:hc * TOK + (tt + 1) * 512],
                                           start=(hc == 0), stop=(hc == 31))
                            if hc == 31 and tt == 1:
                                ins.then_inc(SM.wo_mm, 1)
                    t.wait_ge(SM.wo_ev, dc + 1)
                    for gg in range(2):
                        gidx = dc * 2 + gg
                        if gidx >= 2:
                            t.wait_ge(SM.wo_add, gidx - 1)
                        for tc2 in range(4):
                            ins = t.transpose(
                                pT[:, (gidx % 2) * 512 + tc2 * 128:(gidx % 2) * 512 + (tc2 + 1) * 128],
                                wsb[:, (dc % 2) * 1024 + (gg * 4 + tc2) * 128:
                                    (dc % 2) * 1024 + (gg * 4 + tc2 + 1) * 128],
                                ident_f32[:, :])
                            if tc2 == 3:
                                ins.then_inc(SM.wo_tr, 1)

            @blk.vector
            def _(v):
                for dc in range(DC):
                    v.wait_ge(SM.wo_mm, dc + 1)
                    if dc >= 2:
                        v.wait_ge(SM.wo_tr, 2 * (dc - 1))
                    v.tensor_copy(wsb[:, (dc % 2) * 1024:(dc % 2 + 1) * 1024],
                                  pW[:, (dc % 2) * 1024:(dc % 2 + 1) * 1024]
                                  ).then_inc(SM.wo_ev, 1)
                    for gg in range(2):
                        gidx = dc * 2 + gg
                        v.wait_ge(SM.wo_tr, gidx + 1)
                        v.wait_ge(SM.ld_xp[gidx % 2], 16 * (gidx // 2 + 1))
                        if gidx >= 2:
                            v.wait_ge(SM.st_x1[gidx % 2], 16 * (gidx // 2))
                        v.tensor_tensor(x1p[:, (gidx % 2) * 512:(gidx % 2 + 1) * 512],
                                        pT[:, (gidx % 2) * 512:(gidx % 2 + 1) * 512],
                                        xp[:, (gidx % 2) * 512:(gidx % 2 + 1) * 512],
                                        ALU.add).then_inc(SM.wo_add, 1)

            @blk.scalar
            def _(s):
                for gidx in range(64):
                    dc, gg = gidx // 2, gidx % 2
                    s.wait_ge(SM.wo_add, gidx + 1)
                    s.dma_start(out=bass.AP(x1_dram.tensor, gg * 512 * D + dc * 128,
                                            [[D, 128], [128 * D, 4], [1, 128]]),
                                in_=x1p[:, (gidx % 2) * 512:(gidx % 2 + 1) * 512]
                                .rearrange("p (tc e) -> p tc e", tc=4)
                                ).then_inc(SM.st_x1[gidx % 2], 16)
                s.wait_ge(SM.st_x1[0], 16 * 32)
                s.wait_ge(SM.st_x1[1], 16 * 32)
        barrier()

    def phase5():
        es5 = ExitStack()
        xn2T = es5.enter_context(nc.sbuf_tensor("xn2T", [128, DC * TOK], BF16))
        Xg_sb = es5.enter_context(nc.sbuf_tensor("Xg_sb", [32, TOK], BF16))
        sd_sb = es5.enter_context(nc.sbuf_tensor("sd_sb", [GFF, D], FP16))
        emit_ln(x1_dram, xn2T, "ln2")
        barrier()

        with (
            nc.sbuf_tensor("xgst", [1, 1024], BF16) as xgst,
            nc.psum_tensor("pxg", [1, 1024], F32) as pxg,
        ):
            @blk.tensor
            def _(t):
                for i in range(64):
                    dcc, tt = i // 2, i % 2
                    if i >= 2:
                        t.wait_ge(SM.xg_ev, i - 1)
                    t.matmul(pxg[0:1, (i % 2) * 512:(i % 2 + 1) * 512],
                             ones_col[:, :],
                             xn2T[:, dcc * TOK + tt * 512:dcc * TOK + (tt + 1) * 512],
                             start=True, stop=True, skip_group_check=True
                             ).then_inc(SM.xg_mm, 1)

            @blk.vector
            def _(v):
                for i in range(64):
                    v.wait_ge(SM.xg_mm, i + 1)
                    if i >= 2:
                        v.wait_ge(SM.st_xg[i % 2], 16 * (i // 2))
                    v.tensor_copy(xgst[0:1, (i % 2) * 512:(i % 2 + 1) * 512],
                                  pxg[0:1, (i % 2) * 512:(i % 2 + 1) * 512]
                                  ).then_inc(SM.xg_ev, 1)

            @blk.sync
            def _(sy):
                for i in range(64):
                    dcc, tt = i // 2, i % 2
                    sy.wait_ge(SM.xg_ev, i + 1)
                    sy.dma_start(out=Xg_sb[dcc:dcc + 1, tt * 512:(tt + 1) * 512],
                                 in_=xgst[0:1, (i % 2) * 512:(i % 2 + 1) * 512]
                                 ).then_inc(SM.st_xg[i % 2], 16)
                sy.wait_ge(SM.st_xg[0], 16 * 32)
                sy.wait_ge(SM.st_xg[1], 16 * 32)
        barrier()

        @blk.gpsimd
        def _(g):
            g.dma_start(out=sd_sb[:, :], in_=sd[:, :]).then_inc(SM.ld_sd, 16)

        with (
            nc.sbuf_tensor("gbuf", [128, 2 * 8192], BF16) as gbuf,
            nc.sbuf_tensor("ubuf", [128, 2 * 8192], BF16) as ubuf,
            nc.sbuf_tensor("sgp", [32, 2 * 512], FP16) as sgp,
            nc.sbuf_tensor("nzs", [32, 8 * 128], BF16) as nzs,
            nc.sbuf_tensor("sgact", [128, 1024], BF16) as sgact,
            nc.sbuf_tensor("hstage", [128, 2 * 1024], BF16) as hstage,
            nc.sbuf_tensor("qwdb", [128, 2 * 4096], BF16) as qwdb,
            nc.sbuf_tensor("ohdb", [GFF, 2 * 128], FP16) as ohdb,
            nc.psum_tensor("pG", [128, 1024], F32) as pG,
            nc.psum_tensor("pU", [128, 1024], F32) as pU,
            nc.psum_tensor("pbc", [128, 1024], F32) as pbc,
            nc.psum_tensor("pbc2", [128, 1024], F32) as pbc2,
        ):
            @blk.gpsimd
            def _(g):
                g.wait_ge(SM.pcast, 16 * (64 + GFF))
                for q in range(NPAN):
                    if q >= 2:
                        g.wait_ge(SM.g_mm, 4 * (q - 1))
                    for qw_d, buf in ((qwg_bf, gbuf), (qwu_bf, ubuf)):
                        g.dma_start(out=buf[:, (q % 2) * 8192:(q % 2 + 1) * 8192]
                                    .rearrange("p (cc e) -> p cc e", cc=32),
                                    in_=bass.AP(qw_d.tensor, q * 256,
                                                [[HFF, 128], [HFF * 128, 32], [1, 256]])
                                    ).then_inc(SM.ld_ff[q % 2], 16)
                    g.dma_start(out=sgp[0:32, (q % 2) * 512:(q % 2) * 512 + 256],
                                in_=sg[:, q * 256:(q + 1) * 256]).then_inc(SM.ld_ff[q % 2], 16)
                    g.dma_start(out=sgp[0:32, (q % 2) * 512 + 256:(q % 2 + 1) * 512],
                                in_=su[:, q * 256:(q + 1) * 256]).then_inc(SM.ld_ff[q % 2], 16)
                    for hh in range(2):
                        hcd = 2 * q + hh
                        if hcd >= 2:
                            g.wait_ge(SM.st_wd[hcd % 2], 16 * (hcd // 2))
                        g.dma_start(out=qwdb[:, (hcd % 2) * 4096:(hcd % 2 + 1) * 4096],
                                    in_=qwd_bf[hcd * 128:(hcd + 1) * 128, :]
                                    ).then_inc(SM.ld_qd[hcd % 2], 16)

            @blk.sync
            def _(sy):
                for q in range(NPAN + 1):
                    if q < NPAN:
                        if q >= 2:
                            sy.wait_ge(SM.g_mm, 4 * (q - 1))
                        for mat, nz_d in enumerate([nzg_dram, nzu_dram]):
                            for hc in range(2):
                                slot = (q % 2) * 4 + mat * 2 + hc
                                sy.dma_start(out=nzs[0:32, slot * 128:(slot + 1) * 128],
                                             in_=nz_d[:, (2 * q + hc) * 128:(2 * q + hc + 1) * 128]
                                             ).then_inc(SM.ld_nz[q % 2], 16)
                        for hh in range(2):
                            hcd = 2 * q + hh
                            if hcd >= 2:
                                sy.wait_ge(SM.d_bc, 8 * (hcd - 1))
                            sy.dma_start(out=ohdb[0:GFF, (hcd % 2) * 128:(hcd % 2 + 1) * 128],
                                         in_=bass.AP(ohD_d.tensor, hcd * 128,
                                                     [[GFF * 128, GFF], [1, 128]])
                                         ).then_inc(SM.ld_oh[hcd % 2], 16)
                    if q >= 1:
                        for hc in range(2):
                            hcg = 2 * (q - 1) + hc
                            sy.wait_ge(SM.s_hm, hcg + 1)
                            sy.dma_start(out=hact_dram[hcg * 128:(hcg + 1) * 128, :],
                                         in_=hstage[:, (hcg % 2) * 1024:(hcg % 2 + 1) * 1024]
                                         ).then_inc(SM.st_ha[hcg % 2], 16)
                        for hh in range(2):
                            hcd = 2 * (q - 1) + hh
                            sy.wait_ge(SM.d_dq, 8 * (hcd + 1))
                            sy.dma_start(out=bass.AP(wd_dram.tensor, hcd * 128 * 128,
                                                     [[128, 128], [HFF * 128, 32], [1, 128]]),
                                         in_=qwdb[:, (hcd % 2) * 4096:(hcd % 2 + 1) * 4096]
                                         .rearrange("p (cc e) -> p cc e", cc=32)
                                         ).then_inc(SM.st_wd[hcd % 2], 16)
                sy.wait_ge(SM.st_ha[0], 16 * 43)
                sy.wait_ge(SM.st_ha[1], 16 * 43)
                sy.wait_ge(SM.st_wd[0], 16 * 43)
                sy.wait_ge(SM.st_wd[1], 16 * 43)

            @blk.tensor
            def _(t):
                nbc = 0
                nbc2 = 0
                for q in range(NPAN + 1):
                    if q < NPAN:
                        t.wait_ge(SM.ld_ff[q % 2], 16 * 4 * (q // 2 + 1))
                        for mat in range(2):
                            for dcc in range(32):
                                nbc += 1
                                if nbc >= 3:
                                    t.wait_ge(SM.g_dq, nbc - 2)
                                t.matmul(pbc[:, (nbc % 2) * 512:(nbc % 2) * 512 + 256],
                                         oh32_sb[0:32, dcc * 128:(dcc + 1) * 128],
                                         sgp[0:32, (q % 2) * 512 + mat * 256:
                                             (q % 2) * 512 + (mat + 1) * 256],
                                         start=True, stop=True, skip_group_check=True
                                         ).then_inc(SM.g_bc, 1)
                    if q >= 1:
                        qq = q - 1
                        t.wait_ge(SM.g_dq, 64 * (qq + 1))
                        t.wait_ge(SM.ld_nz[qq % 2], 16 * 4 * (qq // 2 + 1))
                        for hc in range(2):
                            for mat, (buf, ps) in enumerate([(gbuf, pG), (ubuf, pU)]):
                                if 4 * qq + 2 * hc + mat >= 2:
                                    if mat == 0:
                                        t.wait_ge(SM.s_silu, 2 * qq + hc)
                                    else:
                                        t.wait_ge(SM.s_hm, 2 * qq + hc)
                                for tt in range(2):
                                    for dcc in range(32):
                                        t.matmul(ps[:, tt * 512:(tt + 1) * 512],
                                                 buf[:, (qq % 2) * 8192 + dcc * 256 + hc * 128:
                                                     (qq % 2) * 8192 + dcc * 256 + hc * 128 + 128],
                                                 xn2T[:, dcc * TOK + tt * 512:dcc * TOK + (tt + 1) * 512],
                                                 start=(dcc == 0), stop=False,
                                                 skip_group_check=True)
                                    slot = (qq % 2) * 4 + mat * 2 + hc
                                    ins = t.matmul(ps[:, tt * 512:(tt + 1) * 512],
                                                   nzs[0:32, slot * 128:(slot + 1) * 128],
                                                   Xg_sb[0:32, tt * 512:(tt + 1) * 512],
                                                   start=False, stop=True,
                                                   skip_group_check=True)
                                    if tt == 1:
                                        ins.then_inc(SM.g_mm, 1)
                    if q < NPAN:
                        t.wait_ge(SM.ld_sd, 16)
                        for hh in range(2):
                            hcd = 2 * q + hh
                            t.wait_ge(SM.ld_oh[hcd % 2], 16 * (hcd // 2 + 1))
                            for dt in range(8):
                                nbc2 += 1
                                if nbc2 >= 3:
                                    t.wait_ge(SM.d_dq, nbc2 - 2)
                                t.matmul(pbc2[:, (nbc2 % 2) * 512:(nbc2 % 2 + 1) * 512],
                                         ohdb[0:GFF, (hcd % 2) * 128:(hcd % 2 + 1) * 128],
                                         sd_sb[0:GFF, dt * 512:(dt + 1) * 512],
                                         start=True, stop=True, skip_group_check=True
                                         ).then_inc(SM.d_bc, 1)

            @blk.scalar
            def _(s):
                for qq in range(NPAN):
                    for hc in range(2):
                        hcg = 2 * qq + hc
                        s.wait_ge(SM.g_mm, 4 * qq + 2 * hc + 1)
                        if hcg >= 1:
                            s.wait_ge(SM.s_hm, hcg)
                        s.activation(sgact[:, :], pG[:, :], AF.Silu).then_inc(SM.s_silu, 1)

            @blk.vector
            def _(v):
                ndq = 0
                ndq2 = 0
                for q in range(NPAN + 1):
                    if q < NPAN:
                        v.wait_ge(SM.ld_ff[q % 2], 16 * 4 * (q // 2 + 1))
                        for mat, buf in enumerate([gbuf, ubuf]):
                            for dcc in range(32):
                                ndq += 1
                                v.wait_ge(SM.g_bc, ndq)
                                v.tensor_tensor(
                                    buf[:, (q % 2) * 8192 + dcc * 256:(q % 2) * 8192 + (dcc + 1) * 256],
                                    buf[:, (q % 2) * 8192 + dcc * 256:(q % 2) * 8192 + (dcc + 1) * 256],
                                    pbc[:, (ndq % 2) * 512:(ndq % 2) * 512 + 256],
                                    ALU.mult).then_inc(SM.g_dq, 1)
                    if q >= 1:
                        qq = q - 1
                        for hc in range(2):
                            hcg = 2 * qq + hc
                            v.wait_ge(SM.s_silu, hcg + 1)
                            v.wait_ge(SM.g_mm, 4 * qq + 2 * hc + 2)
                            if hcg >= 2:
                                v.wait_ge(SM.st_ha[hcg % 2], 16 * (hcg // 2))
                            v.tensor_tensor(hstage[:, (hcg % 2) * 1024:(hcg % 2 + 1) * 1024],
                                            pU[:, :], sgact[:, :],
                                            ALU.mult).then_inc(SM.s_hm, 1)
                    if q < NPAN:
                        for hh in range(2):
                            hcd = 2 * q + hh
                            v.wait_ge(SM.ld_qd[hcd % 2], 16 * (hcd // 2 + 1))
                            for dt in range(8):
                                ndq2 += 1
                                v.wait_ge(SM.d_bc, ndq2)
                                v.tensor_tensor(
                                    qwdb[:, (hcd % 2) * 4096 + dt * 512:(hcd % 2) * 4096 + (dt + 1) * 512],
                                    qwdb[:, (hcd % 2) * 4096 + dt * 512:(hcd % 2) * 4096 + (dt + 1) * 512],
                                    pbc2[:, (ndq2 % 2) * 512:(ndq2 % 2 + 1) * 512],
                                    ALU.mult).then_inc(SM.d_dq, 1)
        es5.close()
        barrier()

        if PHASES == 5:
            @blk.gpsimd
            def _(g):
                g.dma_start(out=bass.AP(y_out.tensor, 0, [[1024, 4096], [1, 1024]]),
                            in_=bass.AP(hact_dram.tensor, 0, [[1024, 4096], [1, 1024]])
                            ).then_inc(SM.dbg, 16)
                g.wait_ge(SM.dbg, 16)

    def phase6():
        with (
            nc.sbuf_tensor("hact_sb", [128, GFF * 512], BF16) as hact_sb,
            nc.sbuf_tensor("wdc", [128, 2 * GFF * 128], BF16) as wdc,
            nc.sbuf_tensor("nzd6", [GFF, 2 * 128], BF16) as nzd6,
            nc.sbuf_tensor("Ag_sb", [GFF, 512], BF16) as Ag_sb,
            nc.sbuf_tensor("agst", [1, 1024], BF16) as agst,
            nc.sbuf_tensor("dsb", [128, 2 * 512], F32) as dsb,
            nc.sbuf_tensor("xp6", [128, 2 * 512], F32) as xp6,
            nc.sbuf_tensor("yp6", [128, 2 * 512], F32) as yp6,
            nc.psum_tensor("pD", [128, 1024], F32) as pD,
            nc.psum_tensor("pT6", [128, 1024], F32) as pT6,
            nc.psum_tensor("pag", [1, 1024], F32) as pag,
        ):
            @blk.sync
            def _(sy):
                for t_ in range(2):
                    if t_ == 1:
                        sy.wait_ge(SM.d6_mm, 32)
                    sy.dma_start(out=hact_sb[:].rearrange("p (gg e) -> p gg e", gg=GFF),
                                 in_=bass.AP(hact_dram.tensor, t_ * 512,
                                             [[TOK, 128], [128 * TOK, GFF], [1, 512]])
                                 ).then_inc(SM.ld_ha, 16)
                    for dc in range(DC):
                        i = t_ * 32 + dc
                        if i >= 2:
                            sy.wait_ge(SM.d6_mm, i - 1)
                        sy.dma_start(out=wdc[:, (i % 2) * GFF * 128:(i % 2 + 1) * GFF * 128],
                                     in_=bass.AP(wd_dram.tensor, dc * HFF * 128,
                                                 [[128, 128], [128 * 128, GFF], [1, 128]])
                                     ).then_inc(SM.ld_wd[i % 2], 16)
                        if i >= 2:
                            sy.wait_ge(SM.d6_add, i - 1)
                        sy.dma_start(out=xp6[:, (i % 2) * 512:(i % 2 + 1) * 512]
                                     .rearrange("p (tc e) -> p tc e", tc=4),
                                     in_=bass.AP(x1_dram.tensor, t_ * 512 * D + dc * 128,
                                                 [[D, 128], [128 * D, 4], [1, 128]])
                                     ).then_inc(SM.ld_x1[i % 2], 16)

            @blk.scalar
            def _(s):
                for t_ in range(2):
                    if t_ == 1:
                        s.wait_ge(SM.d6_mm, 32)
                    for gg in range(GFF):
                        i2 = t_ * GFF + gg
                        s.wait_ge(SM.ag_ev, i2 + 1)
                        s.dma_start(out=Ag_sb[gg:gg + 1, :],
                                    in_=agst[0:1, (i2 % 2) * 512:(i2 % 2) * 512 + 512]
                                    ).then_inc(SM.st_ag[i2 % 2], 16)
                    def emit_out(i_):
                        dc_ = i_ - t_ * 32
                        s.wait_ge(SM.d6_add, i_ + 1)
                        s.dma_start(out=bass.AP(y_out.tensor, t_ * 512 * D + dc_ * 128,
                                                [[D, 128], [128 * D, 4], [1, 128]]),
                                    in_=yp6[:, (i_ % 2) * 512:(i_ % 2 + 1) * 512]
                                    .rearrange("p (tc e) -> p tc e", tc=4)
                                    ).then_inc(SM.st_out[i_ % 2], 16)

                    for dc in range(DC):
                        i = t_ * 32 + dc
                        if i >= 2:
                            s.wait_ge(SM.d6_mm, i - 1)
                        s.dma_start(out=nzd6[0:GFF, (i % 2) * 128:(i % 2 + 1) * 128],
                                    in_=nzd_dram[:, dc * 128:(dc + 1) * 128]
                                    ).then_inc(SM.ld_nzd[i % 2], 16)
                        if dc >= 2:
                            emit_out(i - 2)
                    emit_out(t_ * 32 + 30)
                    emit_out(t_ * 32 + 31)
                s.wait_ge(SM.st_out[0], 16 * 32)
                s.wait_ge(SM.st_out[1], 16 * 32)

            @blk.tensor
            def _(t):
                def emit_tr(ip):
                    t.wait_ge(SM.d6_ev, ip + 1)
                    if ip >= 2:
                        t.wait_ge(SM.d6_add, ip - 1)
                    for tc2 in range(4):
                        ins = t.transpose(
                            pT6[:, (ip % 2) * 512 + tc2 * 128:(ip % 2) * 512 + (tc2 + 1) * 128],
                            dsb[:, (ip % 2) * 512 + tc2 * 128:(ip % 2) * 512 + (tc2 + 1) * 128],
                            ident_f32[:, :])
                        if tc2 == 3:
                            ins.then_inc(SM.d6_tr, 1)

                for t_ in range(2):
                    t.wait_ge(SM.ld_ha, 16 * (t_ + 1))
                    for gg in range(GFF):
                        i2 = t_ * GFF + gg
                        if i2 >= 2:
                            t.wait_ge(SM.ag_ev, i2 - 1)
                        t.matmul(pag[0:1, (i2 % 2) * 512:(i2 % 2) * 512 + 512],
                                 ones_col[:, :],
                                 hact_sb[:, gg * 512:(gg + 1) * 512],
                                 start=True, stop=True, skip_group_check=True
                                 ).then_inc(SM.ag_mm, 1)
                    t.wait_ge(SM.st_ag[0], 16 * 43 * (t_ + 1))
                    t.wait_ge(SM.st_ag[1], 16 * 43 * (t_ + 1))
                    for dc in range(DC):
                        i = t_ * 32 + dc
                        t.wait_ge(SM.ld_wd[i % 2], 16 * (i // 2 + 1))
                        t.wait_ge(SM.ld_nzd[i % 2], 16 * (i // 2 + 1))
                        if i >= 2:
                            t.wait_ge(SM.d6_ev, i - 1)
                        for gg in range(GFF):
                            t.matmul(pD[:, (i % 2) * 512:(i % 2 + 1) * 512],
                                     wdc[:, (i % 2) * GFF * 128 + gg * 128:
                                         (i % 2) * GFF * 128 + (gg + 1) * 128],
                                     hact_sb[:, gg * 512:(gg + 1) * 512],
                                     start=(gg == 0), stop=False, skip_group_check=True)
                        t.matmul(pD[:, (i % 2) * 512:(i % 2 + 1) * 512],
                                 nzd6[0:GFF, (i % 2) * 128:(i % 2 + 1) * 128],
                                 Ag_sb[0:GFF, :],
                                 start=False, stop=True, skip_group_check=True
                                 ).then_inc(SM.d6_mm, 1)
                        if dc >= 1:
                            emit_tr(i - 1)
                    emit_tr(t_ * 32 + 31)

            @blk.vector
            def _(v):
                for t_ in range(2):
                    for gg in range(GFF):
                        i2 = t_ * GFF + gg
                        v.wait_ge(SM.ag_mm, i2 + 1)
                        if i2 >= 2:
                            v.wait_ge(SM.st_ag[i2 % 2], 16 * (i2 // 2))
                        v.tensor_copy(agst[0:1, (i2 % 2) * 512:(i2 % 2) * 512 + 512],
                                      pag[0:1, (i2 % 2) * 512:(i2 % 2) * 512 + 512]
                                      ).then_inc(SM.ag_ev, 1)
                    for dc in range(DC):
                        i = t_ * 32 + dc
                        v.wait_ge(SM.d6_mm, i + 1)
                        if i >= 2:
                            v.wait_ge(SM.d6_tr, i - 1)
                        v.tensor_copy(dsb[:, (i % 2) * 512:(i % 2 + 1) * 512],
                                      pD[:, (i % 2) * 512:(i % 2 + 1) * 512]
                                      ).then_inc(SM.d6_ev, 1)
                        v.wait_ge(SM.d6_tr, i + 1)
                        v.wait_ge(SM.ld_x1[i % 2], 16 * (i // 2 + 1))
                        if i >= 2:
                            v.wait_ge(SM.st_out[i % 2], 16 * (i // 2))
                        v.tensor_tensor(yp6[:, (i % 2) * 512:(i % 2 + 1) * 512],
                                        pT6[:, (i % 2) * 512:(i % 2 + 1) * 512],
                                        xp6[:, (i % 2) * 512:(i % 2 + 1) * 512],
                                        ALU.add).then_inc(SM.d6_add, 1)

    phase0()
    phase12()
    if PHASES >= 3:
        es34 = ExitStack()
        attnT = es34.enter_context(nc.sbuf_tensor("attnT", [128, H * TOK], BF16))
        phase3(attnT)
        if PHASES == 3:
            @blk.gpsimd
            def _(g):
                for j in range(8):
                    g.dma_start(out=bass.AP(y_out.tensor, j * 128 * D, [[D, 128], [1, D]]),
                                in_=attnT[:, j * 4096:(j + 1) * 4096]).then_inc(SM.dbg, 16)
                g.wait_ge(SM.dbg, 128)
    if PHASES >= 4:
        phase4(attnT)
        if PHASES == 4:
            @blk.gpsimd
            def _(g):
                g.dma_start(out=y_out[:, :], in_=x1_dram[:, :]).then_inc(SM.dbg, 16)
                g.wait_ge(SM.dbg, 16)
    if PHASES >= 3:
        es34.close()
    if PHASES >= 5:
        phase5()
    if PHASES >= 6:
        phase6()
        barrier()

    es.close()
    return nc


def kernel(**inputs):
    x = np.ascontiguousarray(np.asarray(inputs["L0_attn_hidden_states"])[0], dtype=np.float32)
    common = {
        "wq": np.ascontiguousarray(inputs["L0_attn_wq"], dtype=np.float32),
        "wkv": np.ascontiguousarray(inputs["L0_attn_wkv"], dtype=np.float32),
        "wo": np.ascontiguousarray(inputs["L0_attn_wo"], dtype=np.float32),
        "qwg": np.ascontiguousarray(inputs["L0_ffn_qw_gate"], dtype=np.int32),
        "qwu": np.ascontiguousarray(inputs["L0_ffn_qw_up"], dtype=np.int32),
        "qwd": np.ascontiguousarray(inputs["L0_ffn_qw_down"], dtype=np.int32),
        "zg": np.ascontiguousarray(inputs["L0_ffn_z_gate"], dtype=np.int32),
        "zu": np.ascontiguousarray(inputs["L0_ffn_z_up"], dtype=np.int32),
        "zd": np.ascontiguousarray(inputs["L0_ffn_z_down"], dtype=np.int32),
        "sg": np.ascontiguousarray(inputs["L0_ffn_s_gate"], dtype=np.float32),
        "su": np.ascontiguousarray(inputs["L0_ffn_s_up"], dtype=np.float32),
        "sd": np.ascontiguousarray(inputs["L0_ffn_s_down"], dtype=np.float32),
    }
    nc = build_nc()
    in_maps = []
    for c in range(NCORES):
        b0, b1 = c, 15 - c
        xs = np.concatenate([x[b0 * BS:(b0 + 1) * BS], x[b1 * BS:(b1 + 1) * BS]], axis=0)
        m = dict(common)
        m["x"] = np.ascontiguousarray(xs)
        m["tbl"] = make_tables(c)
        in_maps.append(m)
    res = run_bass_kernel_spmd(nc, in_maps, core_ids=list(range(NCORES)))
    out = np.empty((1, S, D), dtype=np.float32)
    for c in range(NCORES):
        b0, b1 = c, 15 - c
        yc = np.asarray(res.results[c]["y"])
        out[0, b0 * BS:(b0 + 1) * BS] = yc[:BS]
        out[0, b1 * BS:(b1 + 1) * BS] = yc[BS:]
    return out


# version 06
# speedup vs baseline: 1.0074x; 1.0074x over previous
import math
from contextlib import ExitStack

import numpy as np
import ml_dtypes

import concourse.bass as bass
import concourse.mybir as mybir
from concourse.bass_utils import run_bass_kernel_spmd

F32 = mybir.dt.float32
BF16 = mybir.dt.bfloat16
FP16 = mybir.dt.float16
I32 = mybir.dt.int32
AF = mybir.ActivationFunctionType
ALU = mybir.AluOpType
AX = mybir.AxisListType

D = 4096
S = 8192
H = 32
DH = 128
HFF = 11008
BS = 512
NCORES = 8
TOK = 1024
DC = 32
GFF = 86
NKC = 96
NPAN = 43
SCALE = 1.0 / math.sqrt(192.0)
EXPB = -2.0
EPS = 1e-5

PHASES = 6


def make_tables(core):
    b0, b1 = core, 15 - core
    pb = np.zeros((128, NKC), dtype=np.float32)
    for m in range(NKC):
        if m < 4 or m >= 92:
            real = True
        elif m < 32:
            real = (m - 4) < 4 * b0
        else:
            real = (m - 32) < 4 * b1
        pb[:, m] = 1.0 if real else 0.0
    return pb


def make_consts():
    ident_bf = np.eye(128, dtype=ml_dtypes.bfloat16)
    ident_f32 = np.eye(128, dtype=np.float32)
    masks = np.zeros((128, 4 * 512), dtype=ml_dtypes.bfloat16)
    kr = np.arange(128)[:, None]
    q = np.arange(512)[None, :]
    for w in range(4):
        masks[:, w * 512:(w + 1) * 512] = (q >= w * 128 + kr).astype(ml_dtypes.bfloat16)
    oh32 = np.zeros((32, 32 * 128), dtype=np.float16)
    for g in range(32):
        oh32[g, g * 128:(g + 1) * 128] = 1.0
    ohD = np.zeros((GFF, GFF * 128), dtype=np.float16)
    for g in range(GFF):
        ohD[g, g * 128:(g + 1) * 128] = 1.0
    ones1r = np.ones((1, 128), dtype=np.float16)
    return ident_bf, ident_f32, masks, oh32, ohD, ones1r


def build_nc():
    nc = bass.Bass("TRN2", target_bir_lowering=False)

    x_in = nc.dram_tensor("x", [TOK, D], F32, kind="ExternalInput").ap()
    wq = nc.dram_tensor("wq", [D, 6144], F32, kind="ExternalInput").ap()
    wkv = nc.dram_tensor("wkv", [D, 8256], F32, kind="ExternalInput").ap()
    wo = nc.dram_tensor("wo", [D, D], F32, kind="ExternalInput").ap()
    qwg = nc.dram_tensor("qwg", [D, HFF], I32, kind="ExternalInput").ap()
    qwu = nc.dram_tensor("qwu", [D, HFF], I32, kind="ExternalInput").ap()
    qwd = nc.dram_tensor("qwd", [HFF, D], I32, kind="ExternalInput").ap()
    zg = nc.dram_tensor("zg", [32, HFF], I32, kind="ExternalInput").ap()
    zu = nc.dram_tensor("zu", [32, HFF], I32, kind="ExternalInput").ap()
    zd = nc.dram_tensor("zd", [GFF, D], I32, kind="ExternalInput").ap()
    sg = nc.dram_tensor("sg", [32, HFF], F32, kind="ExternalInput").ap()
    su = nc.dram_tensor("su", [32, HFF], F32, kind="ExternalInput").ap()
    sd = nc.dram_tensor("sd", [GFF, D], F32, kind="ExternalInput").ap()
    tbl_in = nc.dram_tensor("tbl", [128, NKC], F32, kind="ExternalInput").ap()
    y_out = nc.dram_tensor("y", [TOK, D], F32, kind="ExternalOutput").ap()

    kvK_in = nc.dram_tensor("kvK_in", [H * 8 * 128, 128], BF16).ap()
    kvV_in = nc.dram_tensor("kvV_in", [H * TOK, DH], BF16).ap()
    kvK_out = nc.dram_tensor("kvK_out", [NCORES * H * 8 * 128, 128], BF16, addr_space="Shared").ap()
    kvV_out = nc.dram_tensor("kvV_out", [NCORES * H * TOK, DH], BF16, addr_space="Shared").ap()
    qT_dram = nc.dram_tensor("qT_dram", [H * DH, TOK], BF16).ap()
    x1_dram = nc.dram_tensor("x1_dram", [TOK, D], F32).ap()
    hact_dram = nc.dram_tensor("hact_dram", [HFF, TOK], BF16).ap()
    wd_dram = nc.dram_tensor("wd_dram", [DC, HFF * 128], BF16).ap()
    nzg_dram = nc.dram_tensor("nzg_dram", [32, HFF], BF16).ap()
    nzu_dram = nc.dram_tensor("nzu_dram", [32, HFF], BF16).ap()
    nzd_dram = nc.dram_tensor("nzd_dram", [GFF, D], BF16).ap()
    qwg_bf = nc.dram_tensor("qwg_bf", [D, HFF], BF16).ap()
    qwu_bf = nc.dram_tensor("qwu_bf", [D, HFF], BF16).ap()

    ident_bf_np, ident_f32_np, masks_np, oh32_np, ohD_np, ones1r_np = make_consts()
    ident_bf_d = nc.inline_tensor(ident_bf_np, "c_ident_bf").ap()
    ident_f32_d = nc.inline_tensor(ident_f32_np, "c_ident_f32").ap()
    masks_d = nc.inline_tensor(masks_np, "c_masks").ap()
    oh32_d = nc.inline_tensor(oh32_np, "c_oh32").ap()
    ohD_d = nc.inline_tensor(ohD_np, "c_ohD").ap()
    ones1r_d = nc.inline_tensor(ones1r_np, "c_ones1r").ap()

    es = ExitStack()

    ident_bf = nc.alloc_sbuf_tensor("ident_bf_sb", [128, 128], BF16)
    ident_f32 = nc.alloc_sbuf_tensor("ident_f32_sb", [128, 128], F32)
    masks_sb = nc.alloc_sbuf_tensor("masks_sb", [128, 2048], BF16)
    oh32_sb = nc.alloc_sbuf_tensor("oh32_sb", [32, 4096], FP16)
    ones1r_sb = nc.alloc_sbuf_tensor("ones1r_sb", [1, 128], FP16)
    tbl_sb = nc.alloc_sbuf_tensor("tbl_sb", [128, NKC], F32)
    ones_col = nc.alloc_sbuf_tensor("ones_col", [128, 1], BF16)
    zero_col = nc.alloc_sbuf_tensor("zero_col", [128, 1], BF16)
    zero_t = nc.alloc_sbuf_tensor("zero_t", [128, 128], BF16)
    expb_sb = nc.alloc_sbuf_tensor("expb_sb", [128, 1], F32)

    single = [
        "PH", "ld_c", "ld_z", "ld_s", "s_nz", "st_nz",
        "ln_sq", "ln_v1", "ln_sqr", "ln_n", "ln_t", "ln_e",
        "p2mm", "p2ev", "p2tr", "p2ve", "cc",
        "se", "ex", "ep", "pa", "siv", "p3bc", "sbe", "sat",
        "wo_mm", "wo_ev", "wo_tr", "wo_add",
        "xg_mm", "xg_ev",
        "g_bc", "g_dq", "g_mm", "s_silu", "s_hm",
        "d_bc", "d_dq", "ld_sd",
        "ag_mm", "ag_ev", "ld_ha",
        "d6_mm", "d6_ev", "d6_tr", "d6_add",
        "vv", "pcast", "dbg",
    ]
    paired = [
        "ln_ld", "ld_w", "st_k", "st_qq", "st_v", "ld_kv",
        "ld_wo", "ld_xp", "st_x1", "st_xg",
        "ld_ff", "ld_nz", "ld_qd", "ld_oh", "st_ha", "st_wd",
        "ld_wd", "ld_nzd", "ld_x1", "st_ag", "st_out",
    ]
    sems = {}
    for n in single:
        sems[n] = es.enter_context(nc.semaphore(n))
    for n in paired:
        sems[n] = [es.enter_context(nc.semaphore(n + "0")),
                   es.enter_context(nc.semaphore(n + "1"))]

    class SM:
        pass
    for n, h_ in sems.items():
        setattr(SM, n, h_)

    blk = es.enter_context(nc.Block())

    ph_target = [0]
    vvc = [0]

    def barrier():
        ph_target[0] += 5
        tgt = ph_target[0]
        for eng in ("sync", "scalar", "vector", "tensor", "gpsimd"):
            @getattr(blk, eng)
            def _(e):
                e.drain()
                e.nop().then_inc(SM.PH, 1)
                e.wait_ge(SM.PH, tgt)

    def phase0():
        @blk.sync
        def _(sy):
            sy.dma_start(out=ident_bf[:, :], in_=ident_bf_d[:, :]).then_inc(SM.ld_c, 16)
            sy.dma_start(out=ident_f32[:, :], in_=ident_f32_d[:, :]).then_inc(SM.ld_c, 16)
            sy.dma_start(out=masks_sb[:, :], in_=masks_d[:, :]).then_inc(SM.ld_c, 16)
            sy.dma_start(out=oh32_sb[:, :], in_=oh32_d[:, :]).then_inc(SM.ld_c, 16)
            sy.dma_start(out=ones1r_sb[:, :], in_=ones1r_d[:, :]).then_inc(SM.ld_c, 16)
            sy.dma_start(out=tbl_sb[:, :], in_=tbl_in[:, :]).then_inc(SM.ld_c, 16)
            sy.wait_ge(SM.ld_c, 96)

        @blk.vector
        def _(v):
            v.memset(ones_col[:, :], 1.0)
            v.memset(zero_col[:, :], 0.0)
            v.memset(zero_t[:, :], 0.0)
            v.memset(expb_sb[:, :], EXPB)

        specs = [(zg, sg, nzg_dram, 32, HFF), (zu, su, nzu_dram, 32, HFF),
                 (zd, sd, nzd_dram, GFF, D)]
        with (
            nc.sbuf_tensor("nz_z", [GFF, HFF], BF16) as z_sb,
            nc.sbuf_tensor("nz_s", [GFF, HFF], F32) as s_sb,
            nc.sbuf_tensor("nz_o", [GFF, HFF], BF16) as o_sb,
        ):
            @blk.gpsimd
            def _(g):
                for j, (z_d, s_d, nz_d, P, F) in enumerate(specs):
                    if j >= 1:
                        g.wait_ge(SM.s_nz, j)
                    g.dma_start(out=z_sb[0:P, 0:F], in_=z_d[:, :]).then_inc(SM.ld_z, 16)

            @blk.sync
            def _(sy):
                for j, (z_d, s_d, nz_d, P, F) in enumerate(specs):
                    if j >= 1:
                        sy.wait_ge(SM.s_nz, j)
                    sy.dma_start(out=s_sb[0:P, 0:F], in_=s_d[:, :]).then_inc(SM.ld_s, 16)

            @blk.vector
            def _(v):
                for j, (z_d, s_d, nz_d, P, F) in enumerate(specs):
                    v.wait_ge(SM.ld_z, 16 * (j + 1))
                    v.wait_ge(SM.ld_s, 16 * (j + 1))
                    if j >= 1:
                        v.wait_ge(SM.st_nz, 16 * j)
                    v.scalar_tensor_tensor(o_sb[0:P, 0:F], z_sb[0:P, 0:F], -1.0,
                                           s_sb[0:P, 0:F], ALU.mult, ALU.mult
                                           ).then_inc(SM.s_nz, 1)

            @blk.scalar
            def _(s):
                for j, (z_d, s_d, nz_d, P, F) in enumerate(specs):
                    s.wait_ge(SM.s_nz, j + 1)
                    s.dma_start(out=nz_d[:, :], in_=o_sb[0:P, 0:F]).then_inc(SM.st_nz, 16)
                s.wait_ge(SM.st_nz, 48)
        barrier()

    ln_ctr = {"ld0": 0, "ld1": 0, "sq": 0, "v1": 0, "sqr": 0, "n": 0, "t": 0, "e": 0}

    def emit_ln(src_dram, xnT, tag):
        c = dict(ln_ctr)
        with (
            nc.sbuf_tensor(f"ln_xt_{tag}", [128, 2 * D], F32) as xt,
            nc.sbuf_tensor(f"ln_dump_{tag}", [128, D], BF16) as dump,
            nc.sbuf_tensor(f"ln_xn_{tag}", [128, 2 * D], BF16) as xn,
            nc.sbuf_tensor(f"ln_st_{tag}", [128, 8], F32) as st,
            nc.psum_tensor(f"ln_ps_{tag}", [128, 2048], BF16) as ptr,
        ):
            @blk.sync
            def _(sy):
                for tc in range(8):
                    if tc >= 2:
                        sy.wait_ge(SM.ln_n, c["n"] + tc - 1)
                        sy.wait_ge(SM.ln_sq, c["sq"] + tc - 1)
                    sy.dma_start(out=xt[:, (tc % 2) * D:(tc % 2 + 1) * D],
                                 in_=src_dram[tc * 128:(tc + 1) * 128, :]
                                 ).then_inc(SM.ln_ld[tc % 2], 16)

            @blk.scalar
            def _(s):
                for tc in range(8):
                    xv = xt[:, (tc % 2) * D:(tc % 2 + 1) * D]
                    s.wait_ge(SM.ln_ld[tc % 2], c["ld%d" % (tc % 2)] + 16 * (tc // 2 + 1))
                    s.activation(dump[:, :], xv, AF.Square,
                                 accum_out=st[:, 1:2]).then_inc(SM.ln_sq, 1)
                    s.wait_ge(SM.ln_v1, c["v1"] + tc + 1)
                    s.activation(st[:, 7:8], st[:, 6:7], AF.Sqrt).then_inc(SM.ln_sqr, 1)

            @blk.vector
            def _(v):
                for tc in range(9):
                    if tc < 8:
                        xv = xt[:, (tc % 2) * D:(tc % 2 + 1) * D]
                        xnv = xn[:, (tc % 2) * D:(tc % 2 + 1) * D]
                        v.wait_ge(SM.ln_ld[tc % 2], c["ld%d" % (tc % 2)] + 16 * (tc // 2 + 1))
                        v.reduce_sum(st[:, 0:1], xv, axis=AX.X).then_inc(SM.vv, 1)
                        vvc[0] += 1
                        v.wait_ge(SM.ln_sq, c["sq"] + tc + 1)
                        v.wait_ge(SM.vv, vvc[0])
                        v.tensor_scalar_mul(st[:, 2:3], st[:, 0:1], 1.0 / D).then_inc(SM.vv, 1)
                        vvc[0] += 1
                        v.tensor_scalar_mul(st[:, 3:4], st[:, 1:2], 1.0 / D).then_inc(SM.vv, 1)
                        vvc[0] += 1
                        v.wait_ge(SM.vv, vvc[0])
                        v.tensor_tensor(st[:, 5:6], st[:, 2:3], st[:, 2:3], ALU.mult).then_inc(SM.vv, 1)
                        vvc[0] += 1
                        v.wait_ge(SM.vv, vvc[0])
                        v.tensor_tensor(st[:, 4:5], st[:, 3:4], st[:, 5:6], ALU.subtract).then_inc(SM.vv, 1)
                        vvc[0] += 1
                        v.wait_ge(SM.vv, vvc[0])
                        v.tensor_scalar_add(st[:, 6:7], st[:, 4:5], EPS).then_inc(SM.ln_v1, 1)
                        v.wait_ge(SM.ln_sqr, c["sqr"] + tc + 1)
                        v.reciprocal(st[:, 5:6], st[:, 7:8]).then_inc(SM.vv, 1)
                        vvc[0] += 1
                        if tc >= 2:
                            v.wait_ge(SM.ln_t, c["t"] + 4 * (tc - 1))
                        v.wait_ge(SM.vv, vvc[0])
                        v.tensor_scalar(xnv, xv, st[:, 2:3], st[:, 5:6],
                                        ALU.subtract, ALU.mult).then_inc(SM.ln_n, 1)
                    if tc >= 1:
                        for grp in range(4):
                            gidx = (tc - 1) * 4 + grp
                            v.wait_ge(SM.ln_t, c["t"] + gidx + 1)
                            v.tensor_copy(
                                bass.AP(xnT, (grp * 8) * TOK + (tc - 1) * 128,
                                        [[DC * TOK, 128], [TOK, 8], [1, 128]]),
                                ptr[:, (gidx % 2) * 1024:(gidx % 2 + 1) * 1024].rearrange(
                                    "p (j e) -> p j e", j=8)
                            ).then_inc(SM.ln_e, 1)

            @blk.tensor
            def _(t):
                for tc in range(8):
                    off = (tc % 2) * D
                    t.wait_ge(SM.ln_n, c["n"] + tc + 1)
                    for grp in range(4):
                        gidx = tc * 4 + grp
                        if gidx >= 2:
                            t.wait_ge(SM.ln_e, c["e"] + gidx - 1)
                        for jj in range(8):
                            j = grp * 8 + jj
                            ins = t.transpose(
                                ptr[:, (gidx % 2) * 1024 + jj * 128:(gidx % 2) * 1024 + (jj + 1) * 128],
                                xn[:, off + j * 128:off + (j + 1) * 128],
                                ident_bf[:, :])
                            if jj == 7:
                                ins.then_inc(SM.ln_t, 1)

        ln_ctr["ld0"] += 64
        ln_ctr["ld1"] += 64
        ln_ctr["sq"] += 8
        ln_ctr["v1"] += 8
        ln_ctr["sqr"] += 8
        ln_ctr["n"] += 8
        ln_ctr["t"] += 32
        ln_ctr["e"] += 32

    def phase12():
        es12 = ExitStack()
        xn1T = es12.enter_context(nc.sbuf_tensor("xn1T", [128, DC * TOK], BF16))
        emit_ln(x_in, xn1T, "ln1")
        barrier()

        if PHASES == 1:
            @blk.gpsimd
            def _(g):
                for j in range(8):
                    g.dma_start(out=bass.AP(y_out.tensor, j * 128 * D, [[D, 128], [1, D]]),
                                in_=xn1T[:, j * 4096:(j + 1) * 4096]).then_inc(SM.dbg, 16)
                g.wait_ge(SM.dbg, 128)
            es12.close()
            return

        with (
            nc.sbuf_tensor("wpan", [128, 2 * 4096], BF16) as wpan,
            nc.sbuf_tensor("qkst", [128, 2 * 1024], BF16) as qkst,
            nc.sbuf_tensor("vtst", [128, 2 * 1024], BF16) as vtst,
            nc.psum_tensor("pQ", [128, 2048], F32) as pQ,
            nc.psum_tensor("pVt", [128, 2048], BF16) as pVt,
        ):
            def w_src(oc):
                if oc < 32:
                    return bass.AP(wkv.tensor, oc * 128,
                                   [[8256, 128], [8256 * 128, 32], [1, 128]])
                if oc < 64:
                    return bass.AP(wkv.tensor, 4096 + (oc - 32) * 128,
                                   [[8256, 128], [8256 * 128, 32], [1, 128]])
                return bass.AP(wq.tensor, (oc - 64) * 192,
                               [[6144, 128], [6144 * 128, 32], [1, 128]])

            @blk.gpsimd
            def _(g):
                for oc in range(96):
                    if oc >= 2:
                        g.wait_ge(SM.p2mm, oc - 1)
                    g.dma_start(out=wpan[:, (oc % 2) * 4096:(oc % 2 + 1) * 4096]
                                .rearrange("p (cc e) -> p cc e", cc=32),
                                in_=w_src(oc)).then_inc(SM.ld_w[oc % 2], 16)

            @blk.tensor
            def _(t):
                for oc in range(96):
                    t.wait_ge(SM.ld_w[oc % 2], 16 * (oc // 2 + 1))
                    if oc >= 2:
                        t.wait_ge(SM.p2ev, oc - 1)
                    pb = (oc % 2) * 1024
                    for dcc in range(DC):
                        lhs = wpan[:, (oc % 2) * 4096 + dcc * 128:(oc % 2) * 4096 + (dcc + 1) * 128]
                        for tt in range(2):
                            ins = t.matmul(pQ[:, pb + tt * 512:pb + (tt + 1) * 512],
                                           lhs,
                                           xn1T[:, dcc * TOK + tt * 512:dcc * TOK + (tt + 1) * 512],
                                           start=(dcc == 0), stop=(dcc == DC - 1))
                            if dcc == DC - 1 and tt == 1:
                                ins.then_inc(SM.p2mm, 1)
                    if 33 <= oc <= 64:
                        hh = oc - 33
                        t.wait_ge(SM.p2ev, (32 + hh) + 1)
                        if hh >= 2:
                            t.wait_ge(SM.p2ve, hh - 1)
                        for tc2 in range(8):
                            ins = t.transpose(
                                pVt[:, (hh % 2) * 1024 + tc2 * 128:(hh % 2) * 1024 + (tc2 + 1) * 128],
                                qkst[:, ((32 + hh) % 2) * 1024 + tc2 * 128:
                                     ((32 + hh) % 2) * 1024 + (tc2 + 1) * 128],
                                ident_bf[:, :])
                            if tc2 == 7:
                                ins.then_inc(SM.p2tr, 1)

            @blk.scalar
            def _(s):
                for oc in range(96):
                    s.wait_ge(SM.p2mm, oc + 1)
                    if oc >= 2:
                        prev = oc - 2
                        if prev < 32:
                            s.wait_ge(SM.st_k[prev % 2], 16 * (prev // 2 + 1))
                        elif prev < 64:
                            s.wait_ge(SM.p2tr, prev - 32 + 1)
                        else:
                            s.wait_ge(SM.st_qq[prev % 2], 16 * ((prev - 64) // 2 + 1))
                    s.activation(qkst[:, (oc % 2) * 1024:(oc % 2 + 1) * 1024],
                                 pQ[:, (oc % 2) * 1024:(oc % 2 + 1) * 1024],
                                 AF.Copy).then_inc(SM.p2ev, 1)

            @blk.sync
            def _(sy):
                for oc in range(96):
                    src = qkst[:, (oc % 2) * 1024:(oc % 2 + 1) * 1024]
                    if 32 <= oc < 64:
                        hh = oc - 32
                        sy.wait_ge(SM.p2ve, hh + 1)
                        sy.dma_start(out=bass.AP(kvV_in.tensor, hh * TOK * DH,
                                                 [[DH, 128], [128 * DH, 8], [1, DH]]),
                                     in_=vtst[:, (hh % 2) * 1024:(hh % 2 + 1) * 1024]
                                     .rearrange("p (tc e) -> p tc e", tc=8)
                                     ).then_inc(SM.st_v[hh % 2], 16)
                    if oc < 32:
                        sy.wait_ge(SM.p2ev, oc + 1)
                        sy.dma_start(out=bass.AP(kvK_in.tensor, oc * 8 * 16384,
                                                 [[128, 128], [16384, 8], [1, 128]]),
                                     in_=src.rearrange("p (c e) -> p c e", c=8)
                                     ).then_inc(SM.st_k[oc % 2], 16)
                    elif oc >= 64:
                        sy.wait_ge(SM.p2ev, oc + 1)
                        sy.dma_start(out=qT_dram[(oc - 64) * 128:(oc - 63) * 128, :], in_=src
                                     ).then_inc(SM.st_qq[oc % 2], 16)

            @blk.vector
            def _(v):
                for hh in range(32):
                    v.wait_ge(SM.p2tr, hh + 1)
                    if hh >= 2:
                        v.wait_ge(SM.st_v[hh % 2], 16 * (hh // 2))
                    v.tensor_copy(vtst[:, (hh % 2) * 1024:(hh % 2 + 1) * 1024],
                                  pVt[:, (hh % 2) * 1024:(hh % 2 + 1) * 1024]
                                  ).then_inc(SM.p2ve, 1)

            @blk.gpsimd
            def _(g):
                g.wait_ge(SM.st_k[0], 16 * 16)
                g.wait_ge(SM.st_k[1], 16 * 16)
                g.wait_ge(SM.st_v[0], 16 * 16)
                g.wait_ge(SM.st_v[1], 16 * 16)
                g.collective_compute("AllGather", ALU.bypass,
                                     replica_groups=[list(range(NCORES))],
                                     ins=[kvK_in], outs=[kvK_out]).then_inc(SM.cc, 1)
                g.collective_compute("AllGather", ALU.bypass,
                                     replica_groups=[list(range(NCORES))],
                                     ins=[kvV_in], outs=[kvV_out]).then_inc(SM.cc, 1)

            @blk.sync
            def _(sy):
                sy.wait_ge(SM.st_qq[0], 16 * 16)
                sy.wait_ge(SM.st_qq[1], 16 * 16)
                sy.wait_ge(SM.cc, 2)
        es12.close()
        barrier()

        if PHASES == 2:
            @blk.gpsimd
            def _(g):
                g.dma_start(out=bass.AP(y_out.tensor, 0, [[1024, 4096], [1, 1024]]),
                            in_=bass.AP(qT_dram.tensor, 0, [[1024, 4096], [1, 1024]])
                            ).then_inc(SM.dbg, 16)
                g.wait_ge(SM.dbg, 16)

    def phase3(attnT):
        DIAG = set(range(4)) | set(range(92, 96))
        SLOTS = NKC
        FK = 2 * SLOTS * 128
        with (
            nc.sbuf_tensor("K_sb", [128, FK], BF16) as K_sb,
            nc.sbuf_tensor("V_sb", [128, FK], BF16) as V_sb,
            nc.sbuf_tensor("Q_sb", [128, 2 * TOK], BF16) as Q_sb,
            nc.sbuf_tensor("probs", [128, 4 * 512], BF16) as probs,
            nc.sbuf_tensor("invs", [1, 1024], FP16) as invs,
            nc.sbuf_tensor("invs32", [1, 1024], F32) as invs32,
            nc.sbuf_tensor("bcs", [128, 1024], F32) as bcs,
            nc.psum_tensor("s_ps", [128, 2048], F32) as s_ps,
            nc.psum_tensor("a_ps", [128, 1024], F32) as a_ps,
            nc.psum_tensor("z_ps", [33, 512], F32) as z_ps,
            nc.psum_tensor("b_ps", [128, 512], F32) as b_ps,
        ):
            @blk.gpsimd
            def _(g):
                jobs = ([(qwg, qwg_bf, j) for j in range(32)] +
                        [(qwu, qwu_bf, j) for j in range(32)])
                for n, (qw_d, qw_b, j) in enumerate(jobs):
                    g.wait_ge(SM.pa, SLOTS * (n // 2))
                    g.dma_start(out=qw_b[j * 128:(j + 1) * 128, :],
                                in_=qw_d[j * 128:(j + 1) * 128, :]
                                ).then_inc(SM.pcast, 16)

            @blk.sync
            def _(sy):
                for h in range(H):
                    if h >= 2:
                        sy.wait_ge(SM.pa, SLOTS * (h - 1))
                    hk = (h % 2) * SLOTS * 128
                    for kv_out, kv_loc, sbuf_t in ((kvK_out, kvK_in, K_sb),
                                                   (kvV_out, kvV_in, V_sb)):
                        sy.dma_start(
                            out=bass.AP(sbuf_t, hk, [[FK, 128], [128, 4], [1, 128]]),
                            in_=bass.AP(kv_loc.tensor, (h * 8) * 16384,
                                        [[128, 128], [16384, 4], [1, 128]]),
                        ).then_inc(SM.ld_kv[h % 2], 16)
                        for gb in range(7):
                            sy.dma_start(
                                out=bass.AP(sbuf_t, hk + (4 + gb * 4) * 128,
                                            [[FK, 128], [128, 4], [1, 128]]),
                                in_=bass.AP(kv_out.tensor, (gb * H * 8 + h * 8) * 16384,
                                            [[128, 128], [16384, 4], [1, 128]]),
                            ).then_inc(SM.ld_kv[h % 2], 16)
                        for gb in range(15):
                            if gb < 8:
                                soff = (gb * H * 8 + h * 8) * 16384
                            else:
                                soff = ((15 - gb) * H * 8 + h * 8 + 4) * 16384
                            sy.dma_start(
                                out=bass.AP(sbuf_t, hk + (32 + gb * 4) * 128,
                                            [[FK, 128], [128, 4], [1, 128]]),
                                in_=bass.AP(kv_out.tensor, soff,
                                            [[128, 128], [16384, 4], [1, 128]]),
                            ).then_inc(SM.ld_kv[h % 2], 16)
                        sy.dma_start(
                            out=bass.AP(sbuf_t, hk + 92 * 128, [[FK, 128], [128, 4], [1, 128]]),
                            in_=bass.AP(kv_loc.tensor, (h * 8 + 4) * 16384,
                                        [[128, 128], [16384, 4], [1, 128]]),
                        ).then_inc(SM.ld_kv[h % 2], 16)
                    sy.dma_start(out=Q_sb[:, (h % 2) * TOK:(h % 2 + 1) * TOK],
                                 in_=qT_dram[h * 128:(h + 1) * 128, :]
                                 ).then_inc(SM.ld_kv[h % 2], 16)

            @blk.tensor
            def _(t):
                for h in range(H):
                    hk = (h % 2) * SLOTS * 128
                    hq = (h % 2) * TOK
                    t.wait_ge(SM.ld_kv[h % 2], 16 * 49 * (h // 2 + 1))
                    if h >= 1:
                        t.wait_ge(SM.sat, 2 * h)
                        t.wait_ge(SM.siv, 2 * h)
                    for qb in range(2):
                        t.matmul(bass.AP(a_ps, qb * 512, [[1024, 128], [1, 512]]),
                                 zero_t[:, :], masks_sb[:, 0:512],
                                 start=True, stop=True, skip_group_check=True)
                    t.matmul(z_ps[0:33, :], zero_t[:, 0:33], masks_sb[:, 0:512],
                             start=True, stop=True, skip_group_check=True)

                    def emit_s(m):
                        qb = 0 if m < 32 else 1
                        if m >= 4:
                            t.wait_ge(SM.ex, SLOTS * h + m - 3)
                        t.matmul(s_ps[:, (m % 4) * 512:(m % 4 + 1) * 512],
                                 K_sb[:, hk + m * 128:hk + (m + 1) * 128],
                                 Q_sb[:, hq + qb * 512:hq + (qb + 1) * 512],
                                 start=True, stop=True, skip_group_check=True
                                 ).then_inc(SM.se, 1)

                    emit_s(0)
                    emit_s(1)
                    for m in range(SLOTS):
                        qb = 0 if m < 32 else 1
                        if m + 2 < SLOTS:
                            emit_s(m + 2)
                        t.wait_ge(SM.ep, SLOTS * h + m + 1)
                        t.matmul(bass.AP(z_ps, qb * 32 * 512, [[512, 1], [1, 512]]),
                                 ones_col[:, :],
                                 probs[:, (m % 4) * 512:(m % 4 + 1) * 512],
                                 start=False, stop=False, skip_group_check=True)
                        t.matmul(bass.AP(a_ps, qb * 512, [[1024, 128], [1, 512]]),
                                 V_sb[:, hk + m * 128:hk + (m + 1) * 128],
                                 probs[:, (m % 4) * 512:(m % 4 + 1) * 512],
                                 start=False, stop=False, skip_group_check=True
                                 ).then_inc(SM.pa, 1)
                    for qb in range(2):
                        t.wait_ge(SM.siv, 2 * h + qb + 1)
                        if 2 * h + qb >= 1:
                            t.wait_ge(SM.sbe, 2 * h + qb)
                        t.matmul(b_ps[:, :], ones1r_sb[:, :],
                                 invs[0:1, qb * 512:(qb + 1) * 512],
                                 start=True, stop=True, skip_group_check=True
                                 ).then_inc(SM.p3bc, 1)

            @blk.scalar
            def _(s):
                for h in range(H):
                    for m in range(SLOTS):
                        s.wait_ge(SM.se, SLOTS * h + m + 1)
                        if SLOTS * h + m >= 4:
                            s.wait_ge(SM.pa, SLOTS * h + m - 3)
                        s.activation(probs[:, (m % 4) * 512:(m % 4 + 1) * 512],
                                     s_ps[:, (m % 4) * 512:(m % 4 + 1) * 512],
                                     AF.Exp, bias=expb_sb[:, :], scale=SCALE
                                     ).then_inc(SM.ex, 1)

            @blk.vector
            def _(v):
                for h in range(H):
                    for m in range(SLOTS):
                        v.wait_ge(SM.ex, SLOTS * h + m + 1)
                        pr = probs[:, (m % 4) * 512:(m % 4 + 1) * 512]
                        if m in DIAG:
                            w = m if m < 4 else m - 92
                            v.tensor_tensor(pr, pr, masks_sb[:, w * 512:(w + 1) * 512],
                                            ALU.mult).then_inc(SM.ep, 1)
                        else:
                            v.tensor_scalar_mul(pr, pr, tbl_sb[:, m:m + 1]
                                                ).then_inc(SM.ep, 1)
                    v.wait_ge(SM.pa, SLOTS * (h + 1))
                    for qb in range(2):
                        v.reciprocal(invs32[0:1, qb * 512:(qb + 1) * 512],
                                     z_ps[32 * qb:32 * qb + 1, :]).then_inc(SM.vv, 1)
                        vvc[0] += 1
                        v.wait_ge(SM.vv, vvc[0])
                        v.tensor_copy(invs[0:1, qb * 512:(qb + 1) * 512],
                                      invs32[0:1, qb * 512:(qb + 1) * 512]
                                      ).then_inc(SM.siv, 1)
                    for qb in range(2):
                        v.wait_ge(SM.p3bc, 2 * h + qb + 1)
                        v.tensor_copy(bcs[:, qb * 512:(qb + 1) * 512], b_ps[:, :]
                                      ).then_inc(SM.sbe, 1)
                        v.wait_ge(SM.sbe, 2 * h + qb + 1)
                        v.tensor_tensor(attnT[:, h * TOK + qb * 512:h * TOK + (qb + 1) * 512],
                                        a_ps[:, qb * 512:(qb + 1) * 512],
                                        bcs[:, qb * 512:(qb + 1) * 512],
                                        ALU.mult).then_inc(SM.sat, 1)
        barrier()

    def phase4(attnT):
        with (
            nc.sbuf_tensor("wopan", [128, 2 * 4096], BF16) as wopan,
            nc.sbuf_tensor("wsb", [128, 2 * 1024], F32) as wsb,
            nc.sbuf_tensor("xp4", [128, 2 * 512], F32) as xp,
            nc.sbuf_tensor("x1p", [128, 2 * 512], F32) as x1p,
            nc.psum_tensor("pW", [128, 2048], F32) as pW,
            nc.psum_tensor("pT4", [128, 1024], F32) as pT,
        ):
            @blk.gpsimd
            def _(g):
                for dc in range(DC):
                    if dc >= 2:
                        g.wait_ge(SM.wo_mm, dc - 1)
                    g.dma_start(out=wopan[:, (dc % 2) * 4096:(dc % 2 + 1) * 4096]
                                .rearrange("p (cc e) -> p cc e", cc=32),
                                in_=bass.AP(wo.tensor, dc * 128,
                                            [[D, 128], [D * 128, 32], [1, 128]])
                                ).then_inc(SM.ld_wo[dc % 2], 16)

            @blk.sync
            def _(sy):
                for gidx in range(64):
                    dc, gg = gidx // 2, gidx % 2
                    if gidx >= 2:
                        sy.wait_ge(SM.wo_add, gidx - 1)
                    sy.dma_start(out=xp[:, (gidx % 2) * 512:(gidx % 2 + 1) * 512]
                                 .rearrange("p (tc e) -> p tc e", tc=4),
                                 in_=bass.AP(x_in.tensor, gg * 512 * D + dc * 128,
                                             [[D, 128], [128 * D, 4], [1, 128]])
                                 ).then_inc(SM.ld_xp[gidx % 2], 16)

            @blk.tensor
            def _(t):
                for dc in range(DC):
                    t.wait_ge(SM.ld_wo[dc % 2], 16 * (dc // 2 + 1))
                    if dc >= 2:
                        t.wait_ge(SM.wo_ev, dc - 1)
                    pb = (dc % 2) * 1024
                    for hc in range(32):
                        lhs = wopan[:, (dc % 2) * 4096 + hc * 128:(dc % 2) * 4096 + (hc + 1) * 128]
                        for tt in range(2):
                            ins = t.matmul(pW[:, pb + tt * 512:pb + (tt + 1) * 512],
                                           lhs,
                                           attnT[:, hc * TOK + tt * 512:hc * TOK + (tt + 1) * 512],
                                           start=(hc == 0), stop=(hc == 31))
                            if hc == 31 and tt == 1:
                                ins.then_inc(SM.wo_mm, 1)
                    t.wait_ge(SM.wo_ev, dc + 1)
                    for gg in range(2):
                        gidx = dc * 2 + gg
                        if gidx >= 2:
                            t.wait_ge(SM.wo_add, gidx - 1)
                        for tc2 in range(4):
                            ins = t.transpose(
                                pT[:, (gidx % 2) * 512 + tc2 * 128:(gidx % 2) * 512 + (tc2 + 1) * 128],
                                wsb[:, (dc % 2) * 1024 + (gg * 4 + tc2) * 128:
                                    (dc % 2) * 1024 + (gg * 4 + tc2 + 1) * 128],
                                ident_f32[:, :])
                            if tc2 == 3:
                                ins.then_inc(SM.wo_tr, 1)

            @blk.vector
            def _(v):
                for dc in range(DC):
                    v.wait_ge(SM.wo_mm, dc + 1)
                    if dc >= 2:
                        v.wait_ge(SM.wo_tr, 2 * (dc - 1))
                    v.tensor_copy(wsb[:, (dc % 2) * 1024:(dc % 2 + 1) * 1024],
                                  pW[:, (dc % 2) * 1024:(dc % 2 + 1) * 1024]
                                  ).then_inc(SM.wo_ev, 1)
                    for gg in range(2):
                        gidx = dc * 2 + gg
                        v.wait_ge(SM.wo_tr, gidx + 1)
                        v.wait_ge(SM.ld_xp[gidx % 2], 16 * (gidx // 2 + 1))
                        if gidx >= 2:
                            v.wait_ge(SM.st_x1[gidx % 2], 16 * (gidx // 2))
                        v.tensor_tensor(x1p[:, (gidx % 2) * 512:(gidx % 2 + 1) * 512],
                                        pT[:, (gidx % 2) * 512:(gidx % 2 + 1) * 512],
                                        xp[:, (gidx % 2) * 512:(gidx % 2 + 1) * 512],
                                        ALU.add).then_inc(SM.wo_add, 1)

            @blk.scalar
            def _(s):
                for gidx in range(64):
                    dc, gg = gidx // 2, gidx % 2
                    s.wait_ge(SM.wo_add, gidx + 1)
                    s.dma_start(out=bass.AP(x1_dram.tensor, gg * 512 * D + dc * 128,
                                            [[D, 128], [128 * D, 4], [1, 128]]),
                                in_=x1p[:, (gidx % 2) * 512:(gidx % 2 + 1) * 512]
                                .rearrange("p (tc e) -> p tc e", tc=4)
                                ).then_inc(SM.st_x1[gidx % 2], 16)
                s.wait_ge(SM.st_x1[0], 16 * 32)
                s.wait_ge(SM.st_x1[1], 16 * 32)
        barrier()

    def phase5():
        es5 = ExitStack()
        xn2T = es5.enter_context(nc.sbuf_tensor("xn2T", [128, DC * TOK], BF16))
        Xg_sb = es5.enter_context(nc.sbuf_tensor("Xg_sb", [32, TOK], BF16))
        sd_sb = es5.enter_context(nc.sbuf_tensor("sd_sb", [GFF, D], FP16))
        emit_ln(x1_dram, xn2T, "ln2")
        barrier()

        with (
            nc.sbuf_tensor("xgst", [1, 1024], BF16) as xgst,
            nc.psum_tensor("pxg", [1, 1024], F32) as pxg,
        ):
            @blk.tensor
            def _(t):
                for i in range(64):
                    dcc, tt = i // 2, i % 2
                    if i >= 2:
                        t.wait_ge(SM.xg_ev, i - 1)
                    t.matmul(pxg[0:1, (i % 2) * 512:(i % 2 + 1) * 512],
                             ones_col[:, :],
                             xn2T[:, dcc * TOK + tt * 512:dcc * TOK + (tt + 1) * 512],
                             start=True, stop=True, skip_group_check=True
                             ).then_inc(SM.xg_mm, 1)

            @blk.vector
            def _(v):
                for i in range(64):
                    v.wait_ge(SM.xg_mm, i + 1)
                    if i >= 2:
                        v.wait_ge(SM.st_xg[i % 2], 16 * (i // 2))
                    v.tensor_copy(xgst[0:1, (i % 2) * 512:(i % 2 + 1) * 512],
                                  pxg[0:1, (i % 2) * 512:(i % 2 + 1) * 512]
                                  ).then_inc(SM.xg_ev, 1)

            @blk.sync
            def _(sy):
                for i in range(64):
                    dcc, tt = i // 2, i % 2
                    sy.wait_ge(SM.xg_ev, i + 1)
                    sy.dma_start(out=Xg_sb[dcc:dcc + 1, tt * 512:(tt + 1) * 512],
                                 in_=xgst[0:1, (i % 2) * 512:(i % 2 + 1) * 512]
                                 ).then_inc(SM.st_xg[i % 2], 16)
                sy.wait_ge(SM.st_xg[0], 16 * 32)
                sy.wait_ge(SM.st_xg[1], 16 * 32)
        barrier()

        @blk.gpsimd
        def _(g):
            g.dma_start(out=sd_sb[:, :], in_=sd[:, :]).then_inc(SM.ld_sd, 16)

        with (
            nc.sbuf_tensor("gbuf", [128, 2 * 8192], BF16) as gbuf,
            nc.sbuf_tensor("ubuf", [128, 2 * 8192], BF16) as ubuf,
            nc.sbuf_tensor("sgp", [32, 2 * 512], FP16) as sgp,
            nc.sbuf_tensor("nzs", [32, 8 * 128], BF16) as nzs,
            nc.sbuf_tensor("sgact", [128, 1024], BF16) as sgact,
            nc.sbuf_tensor("hstage", [128, 2 * 1024], BF16) as hstage,
            nc.sbuf_tensor("qwdb", [128, 2 * 4096], BF16) as qwdb,
            nc.sbuf_tensor("ohdb", [GFF, 2 * 128], FP16) as ohdb,
            nc.psum_tensor("pG", [128, 1024], F32) as pG,
            nc.psum_tensor("pU", [128, 1024], F32) as pU,
            nc.psum_tensor("pbc", [128, 1024], F32) as pbc,
            nc.psum_tensor("pbc2", [128, 1024], F32) as pbc2,
        ):
            @blk.gpsimd
            def _(g):
                g.wait_ge(SM.pcast, 16 * 64)
                for q in range(NPAN):
                    if q >= 2:
                        g.wait_ge(SM.g_mm, 4 * (q - 1))
                    for qw_d, buf in ((qwg_bf, gbuf), (qwu_bf, ubuf)):
                        g.dma_start(out=buf[:, (q % 2) * 8192:(q % 2 + 1) * 8192]
                                    .rearrange("p (cc e) -> p cc e", cc=32),
                                    in_=bass.AP(qw_d.tensor, q * 256,
                                                [[HFF, 128], [HFF * 128, 32], [1, 256]])
                                    ).then_inc(SM.ld_ff[q % 2], 16)
                    g.dma_start(out=sgp[0:32, (q % 2) * 512:(q % 2) * 512 + 256],
                                in_=sg[:, q * 256:(q + 1) * 256]).then_inc(SM.ld_ff[q % 2], 16)
                    g.dma_start(out=sgp[0:32, (q % 2) * 512 + 256:(q % 2 + 1) * 512],
                                in_=su[:, q * 256:(q + 1) * 256]).then_inc(SM.ld_ff[q % 2], 16)
                    for hh in range(2):
                        hcd = 2 * q + hh
                        if hcd >= 2:
                            g.wait_ge(SM.st_wd[hcd % 2], 16 * (hcd // 2))
                        g.dma_start(out=qwdb[:, (hcd % 2) * 4096:(hcd % 2 + 1) * 4096],
                                    in_=qwd[hcd * 128:(hcd + 1) * 128, :]
                                    ).then_inc(SM.ld_qd[hcd % 2], 16)

            @blk.sync
            def _(sy):
                for q in range(NPAN + 1):
                    if q < NPAN:
                        if q >= 2:
                            sy.wait_ge(SM.g_mm, 4 * (q - 1))
                        for mat, nz_d in enumerate([nzg_dram, nzu_dram]):
                            for hc in range(2):
                                slot = (q % 2) * 4 + mat * 2 + hc
                                sy.dma_start(out=nzs[0:32, slot * 128:(slot + 1) * 128],
                                             in_=nz_d[:, (2 * q + hc) * 128:(2 * q + hc + 1) * 128]
                                             ).then_inc(SM.ld_nz[q % 2], 16)
                        for hh in range(2):
                            hcd = 2 * q + hh
                            if hcd >= 2:
                                sy.wait_ge(SM.d_bc, 8 * (hcd - 1))
                            sy.dma_start(out=ohdb[0:GFF, (hcd % 2) * 128:(hcd % 2 + 1) * 128],
                                         in_=bass.AP(ohD_d.tensor, hcd * 128,
                                                     [[GFF * 128, GFF], [1, 128]])
                                         ).then_inc(SM.ld_oh[hcd % 2], 16)
                    if q >= 1:
                        for hc in range(2):
                            hcg = 2 * (q - 1) + hc
                            sy.wait_ge(SM.s_hm, hcg + 1)
                            sy.dma_start(out=hact_dram[hcg * 128:(hcg + 1) * 128, :],
                                         in_=hstage[:, (hcg % 2) * 1024:(hcg % 2 + 1) * 1024]
                                         ).then_inc(SM.st_ha[hcg % 2], 16)
                        for hh in range(2):
                            hcd = 2 * (q - 1) + hh
                            sy.wait_ge(SM.d_dq, 8 * (hcd + 1))
                            sy.dma_start(out=bass.AP(wd_dram.tensor, hcd * 128 * 128,
                                                     [[128, 128], [HFF * 128, 32], [1, 128]]),
                                         in_=qwdb[:, (hcd % 2) * 4096:(hcd % 2 + 1) * 4096]
                                         .rearrange("p (cc e) -> p cc e", cc=32)
                                         ).then_inc(SM.st_wd[hcd % 2], 16)
                sy.wait_ge(SM.st_ha[0], 16 * 43)
                sy.wait_ge(SM.st_ha[1], 16 * 43)
                sy.wait_ge(SM.st_wd[0], 16 * 43)
                sy.wait_ge(SM.st_wd[1], 16 * 43)

            @blk.tensor
            def _(t):
                nbc = 0
                nbc2 = 0
                for q in range(NPAN + 1):
                    if q < NPAN:
                        t.wait_ge(SM.ld_ff[q % 2], 16 * 4 * (q // 2 + 1))
                        for mat in range(2):
                            for dcc in range(32):
                                nbc += 1
                                if nbc >= 3:
                                    t.wait_ge(SM.g_dq, nbc - 2)
                                t.matmul(pbc[:, (nbc % 2) * 512:(nbc % 2) * 512 + 256],
                                         oh32_sb[0:32, dcc * 128:(dcc + 1) * 128],
                                         sgp[0:32, (q % 2) * 512 + mat * 256:
                                             (q % 2) * 512 + (mat + 1) * 256],
                                         start=True, stop=True, skip_group_check=True
                                         ).then_inc(SM.g_bc, 1)
                    if q >= 1:
                        qq = q - 1
                        t.wait_ge(SM.g_dq, 64 * (qq + 1))
                        t.wait_ge(SM.ld_nz[qq % 2], 16 * 4 * (qq // 2 + 1))
                        for hc in range(2):
                            for mat, (buf, ps) in enumerate([(gbuf, pG), (ubuf, pU)]):
                                if 4 * qq + 2 * hc + mat >= 2:
                                    if mat == 0:
                                        t.wait_ge(SM.s_silu, 2 * qq + hc)
                                    else:
                                        t.wait_ge(SM.s_hm, 2 * qq + hc)
                                for tt in range(2):
                                    for dcc in range(32):
                                        t.matmul(ps[:, tt * 512:(tt + 1) * 512],
                                                 buf[:, (qq % 2) * 8192 + dcc * 256 + hc * 128:
                                                     (qq % 2) * 8192 + dcc * 256 + hc * 128 + 128],
                                                 xn2T[:, dcc * TOK + tt * 512:dcc * TOK + (tt + 1) * 512],
                                                 start=(dcc == 0), stop=False,
                                                 skip_group_check=True)
                                    slot = (qq % 2) * 4 + mat * 2 + hc
                                    ins = t.matmul(ps[:, tt * 512:(tt + 1) * 512],
                                                   nzs[0:32, slot * 128:(slot + 1) * 128],
                                                   Xg_sb[0:32, tt * 512:(tt + 1) * 512],
                                                   start=False, stop=True,
                                                   skip_group_check=True)
                                    if tt == 1:
                                        ins.then_inc(SM.g_mm, 1)
                    if q < NPAN:
                        t.wait_ge(SM.ld_sd, 16)
                        for hh in range(2):
                            hcd = 2 * q + hh
                            t.wait_ge(SM.ld_oh[hcd % 2], 16 * (hcd // 2 + 1))
                            for dt in range(8):
                                nbc2 += 1
                                if nbc2 >= 3:
                                    t.wait_ge(SM.d_dq, nbc2 - 2)
                                t.matmul(pbc2[:, (nbc2 % 2) * 512:(nbc2 % 2 + 1) * 512],
                                         ohdb[0:GFF, (hcd % 2) * 128:(hcd % 2 + 1) * 128],
                                         sd_sb[0:GFF, dt * 512:(dt + 1) * 512],
                                         start=True, stop=True, skip_group_check=True
                                         ).then_inc(SM.d_bc, 1)

            @blk.scalar
            def _(s):
                for qq in range(NPAN):
                    for hc in range(2):
                        hcg = 2 * qq + hc
                        s.wait_ge(SM.g_mm, 4 * qq + 2 * hc + 1)
                        if hcg >= 1:
                            s.wait_ge(SM.s_hm, hcg)
                        s.activation(sgact[:, :], pG[:, :], AF.Silu).then_inc(SM.s_silu, 1)

            @blk.vector
            def _(v):
                ndq = 0
                ndq2 = 0
                for q in range(NPAN + 1):
                    if q < NPAN:
                        v.wait_ge(SM.ld_ff[q % 2], 16 * 4 * (q // 2 + 1))
                        for mat, buf in enumerate([gbuf, ubuf]):
                            for dcc in range(32):
                                ndq += 1
                                v.wait_ge(SM.g_bc, ndq)
                                v.tensor_tensor(
                                    buf[:, (q % 2) * 8192 + dcc * 256:(q % 2) * 8192 + (dcc + 1) * 256],
                                    buf[:, (q % 2) * 8192 + dcc * 256:(q % 2) * 8192 + (dcc + 1) * 256],
                                    pbc[:, (ndq % 2) * 512:(ndq % 2) * 512 + 256],
                                    ALU.mult).then_inc(SM.g_dq, 1)
                    if q >= 1:
                        qq = q - 1
                        for hc in range(2):
                            hcg = 2 * qq + hc
                            v.wait_ge(SM.s_silu, hcg + 1)
                            v.wait_ge(SM.g_mm, 4 * qq + 2 * hc + 2)
                            if hcg >= 2:
                                v.wait_ge(SM.st_ha[hcg % 2], 16 * (hcg // 2))
                            v.tensor_tensor(hstage[:, (hcg % 2) * 1024:(hcg % 2 + 1) * 1024],
                                            pU[:, :], sgact[:, :],
                                            ALU.mult).then_inc(SM.s_hm, 1)
                    if q < NPAN:
                        for hh in range(2):
                            hcd = 2 * q + hh
                            v.wait_ge(SM.ld_qd[hcd % 2], 16 * (hcd // 2 + 1))
                            for dt in range(8):
                                ndq2 += 1
                                v.wait_ge(SM.d_bc, ndq2)
                                v.tensor_tensor(
                                    qwdb[:, (hcd % 2) * 4096 + dt * 512:(hcd % 2) * 4096 + (dt + 1) * 512],
                                    qwdb[:, (hcd % 2) * 4096 + dt * 512:(hcd % 2) * 4096 + (dt + 1) * 512],
                                    pbc2[:, (ndq2 % 2) * 512:(ndq2 % 2 + 1) * 512],
                                    ALU.mult).then_inc(SM.d_dq, 1)
        es5.close()
        barrier()

        if PHASES == 5:
            @blk.gpsimd
            def _(g):
                g.dma_start(out=bass.AP(y_out.tensor, 0, [[1024, 4096], [1, 1024]]),
                            in_=bass.AP(hact_dram.tensor, 0, [[1024, 4096], [1, 1024]])
                            ).then_inc(SM.dbg, 16)
                g.wait_ge(SM.dbg, 16)

    def phase6():
        with (
            nc.sbuf_tensor("hact_sb", [128, GFF * 512], BF16) as hact_sb,
            nc.sbuf_tensor("wdc", [128, 2 * GFF * 128], BF16) as wdc,
            nc.sbuf_tensor("nzd6", [GFF, 2 * 128], BF16) as nzd6,
            nc.sbuf_tensor("Ag_sb", [GFF, 512], BF16) as Ag_sb,
            nc.sbuf_tensor("agst", [1, 1024], BF16) as agst,
            nc.sbuf_tensor("dsb", [128, 2 * 512], F32) as dsb,
            nc.sbuf_tensor("xp6", [128, 2 * 512], F32) as xp6,
            nc.sbuf_tensor("yp6", [128, 2 * 512], F32) as yp6,
            nc.psum_tensor("pD", [128, 1024], F32) as pD,
            nc.psum_tensor("pT6", [128, 1024], F32) as pT6,
            nc.psum_tensor("pag", [1, 1024], F32) as pag,
        ):
            @blk.sync
            def _(sy):
                for t_ in range(2):
                    if t_ == 1:
                        sy.wait_ge(SM.d6_mm, 32)
                    sy.dma_start(out=hact_sb[:].rearrange("p (gg e) -> p gg e", gg=GFF),
                                 in_=bass.AP(hact_dram.tensor, t_ * 512,
                                             [[TOK, 128], [128 * TOK, GFF], [1, 512]])
                                 ).then_inc(SM.ld_ha, 16)
                    for dc in range(DC):
                        i = t_ * 32 + dc
                        if i >= 2:
                            sy.wait_ge(SM.d6_mm, i - 1)
                        sy.dma_start(out=wdc[:, (i % 2) * GFF * 128:(i % 2 + 1) * GFF * 128],
                                     in_=bass.AP(wd_dram.tensor, dc * HFF * 128,
                                                 [[128, 128], [128 * 128, GFF], [1, 128]])
                                     ).then_inc(SM.ld_wd[i % 2], 16)
                        if i >= 2:
                            sy.wait_ge(SM.d6_add, i - 1)
                        sy.dma_start(out=xp6[:, (i % 2) * 512:(i % 2 + 1) * 512]
                                     .rearrange("p (tc e) -> p tc e", tc=4),
                                     in_=bass.AP(x1_dram.tensor, t_ * 512 * D + dc * 128,
                                                 [[D, 128], [128 * D, 4], [1, 128]])
                                     ).then_inc(SM.ld_x1[i % 2], 16)

            @blk.scalar
            def _(s):
                for t_ in range(2):
                    if t_ == 1:
                        s.wait_ge(SM.d6_mm, 32)
                    for gg in range(GFF):
                        i2 = t_ * GFF + gg
                        s.wait_ge(SM.ag_ev, i2 + 1)
                        s.dma_start(out=Ag_sb[gg:gg + 1, :],
                                    in_=agst[0:1, (i2 % 2) * 512:(i2 % 2) * 512 + 512]
                                    ).then_inc(SM.st_ag[i2 % 2], 16)
                    def emit_out(i_):
                        dc_ = i_ - t_ * 32
                        s.wait_ge(SM.d6_add, i_ + 1)
                        s.dma_start(out=bass.AP(y_out.tensor, t_ * 512 * D + dc_ * 128,
                                                [[D, 128], [128 * D, 4], [1, 128]]),
                                    in_=yp6[:, (i_ % 2) * 512:(i_ % 2 + 1) * 512]
                                    .rearrange("p (tc e) -> p tc e", tc=4)
                                    ).then_inc(SM.st_out[i_ % 2], 16)

                    for dc in range(DC):
                        i = t_ * 32 + dc
                        if i >= 2:
                            s.wait_ge(SM.d6_mm, i - 1)
                        s.dma_start(out=nzd6[0:GFF, (i % 2) * 128:(i % 2 + 1) * 128],
                                    in_=nzd_dram[:, dc * 128:(dc + 1) * 128]
                                    ).then_inc(SM.ld_nzd[i % 2], 16)
                        if dc >= 2:
                            emit_out(i - 2)
                    emit_out(t_ * 32 + 30)
                    emit_out(t_ * 32 + 31)
                s.wait_ge(SM.st_out[0], 16 * 32)
                s.wait_ge(SM.st_out[1], 16 * 32)

            @blk.tensor
            def _(t):
                def emit_tr(ip):
                    t.wait_ge(SM.d6_ev, ip + 1)
                    if ip >= 2:
                        t.wait_ge(SM.d6_add, ip - 1)
                    for tc2 in range(4):
                        ins = t.transpose(
                            pT6[:, (ip % 2) * 512 + tc2 * 128:(ip % 2) * 512 + (tc2 + 1) * 128],
                            dsb[:, (ip % 2) * 512 + tc2 * 128:(ip % 2) * 512 + (tc2 + 1) * 128],
                            ident_f32[:, :])
                        if tc2 == 3:
                            ins.then_inc(SM.d6_tr, 1)

                for t_ in range(2):
                    t.wait_ge(SM.ld_ha, 16 * (t_ + 1))
                    for gg in range(GFF):
                        i2 = t_ * GFF + gg
                        if i2 >= 2:
                            t.wait_ge(SM.ag_ev, i2 - 1)
                        t.matmul(pag[0:1, (i2 % 2) * 512:(i2 % 2) * 512 + 512],
                                 ones_col[:, :],
                                 hact_sb[:, gg * 512:(gg + 1) * 512],
                                 start=True, stop=True, skip_group_check=True
                                 ).then_inc(SM.ag_mm, 1)
                    t.wait_ge(SM.st_ag[0], 16 * 43 * (t_ + 1))
                    t.wait_ge(SM.st_ag[1], 16 * 43 * (t_ + 1))
                    for dc in range(DC):
                        i = t_ * 32 + dc
                        t.wait_ge(SM.ld_wd[i % 2], 16 * (i // 2 + 1))
                        t.wait_ge(SM.ld_nzd[i % 2], 16 * (i // 2 + 1))
                        if i >= 2:
                            t.wait_ge(SM.d6_ev, i - 1)
                        for gg in range(GFF):
                            t.matmul(pD[:, (i % 2) * 512:(i % 2 + 1) * 512],
                                     wdc[:, (i % 2) * GFF * 128 + gg * 128:
                                         (i % 2) * GFF * 128 + (gg + 1) * 128],
                                     hact_sb[:, gg * 512:(gg + 1) * 512],
                                     start=(gg == 0), stop=False, skip_group_check=True)
                        t.matmul(pD[:, (i % 2) * 512:(i % 2 + 1) * 512],
                                 nzd6[0:GFF, (i % 2) * 128:(i % 2 + 1) * 128],
                                 Ag_sb[0:GFF, :],
                                 start=False, stop=True, skip_group_check=True
                                 ).then_inc(SM.d6_mm, 1)
                        if dc >= 1:
                            emit_tr(i - 1)
                    emit_tr(t_ * 32 + 31)

            @blk.vector
            def _(v):
                for t_ in range(2):
                    for gg in range(GFF):
                        i2 = t_ * GFF + gg
                        v.wait_ge(SM.ag_mm, i2 + 1)
                        if i2 >= 2:
                            v.wait_ge(SM.st_ag[i2 % 2], 16 * (i2 // 2))
                        v.tensor_copy(agst[0:1, (i2 % 2) * 512:(i2 % 2) * 512 + 512],
                                      pag[0:1, (i2 % 2) * 512:(i2 % 2) * 512 + 512]
                                      ).then_inc(SM.ag_ev, 1)
                    for dc in range(DC):
                        i = t_ * 32 + dc
                        v.wait_ge(SM.d6_mm, i + 1)
                        if i >= 2:
                            v.wait_ge(SM.d6_tr, i - 1)
                        v.tensor_copy(dsb[:, (i % 2) * 512:(i % 2 + 1) * 512],
                                      pD[:, (i % 2) * 512:(i % 2 + 1) * 512]
                                      ).then_inc(SM.d6_ev, 1)
                        v.wait_ge(SM.d6_tr, i + 1)
                        v.wait_ge(SM.ld_x1[i % 2], 16 * (i // 2 + 1))
                        if i >= 2:
                            v.wait_ge(SM.st_out[i % 2], 16 * (i // 2))
                        v.tensor_tensor(yp6[:, (i % 2) * 512:(i % 2 + 1) * 512],
                                        pT6[:, (i % 2) * 512:(i % 2 + 1) * 512],
                                        xp6[:, (i % 2) * 512:(i % 2 + 1) * 512],
                                        ALU.add).then_inc(SM.d6_add, 1)

    phase0()
    phase12()
    if PHASES >= 3:
        es34 = ExitStack()
        attnT = es34.enter_context(nc.sbuf_tensor("attnT", [128, H * TOK], BF16))
        phase3(attnT)
        if PHASES == 3:
            @blk.gpsimd
            def _(g):
                for j in range(8):
                    g.dma_start(out=bass.AP(y_out.tensor, j * 128 * D, [[D, 128], [1, D]]),
                                in_=attnT[:, j * 4096:(j + 1) * 4096]).then_inc(SM.dbg, 16)
                g.wait_ge(SM.dbg, 128)
    if PHASES >= 4:
        phase4(attnT)
        if PHASES == 4:
            @blk.gpsimd
            def _(g):
                g.dma_start(out=y_out[:, :], in_=x1_dram[:, :]).then_inc(SM.dbg, 16)
                g.wait_ge(SM.dbg, 16)
    if PHASES >= 3:
        es34.close()
    if PHASES >= 5:
        phase5()
    if PHASES >= 6:
        phase6()
        barrier()

    es.close()
    return nc


def kernel(**inputs):
    x = np.ascontiguousarray(np.asarray(inputs["L0_attn_hidden_states"])[0], dtype=np.float32)
    common = {
        "wq": np.ascontiguousarray(inputs["L0_attn_wq"], dtype=np.float32),
        "wkv": np.ascontiguousarray(inputs["L0_attn_wkv"], dtype=np.float32),
        "wo": np.ascontiguousarray(inputs["L0_attn_wo"], dtype=np.float32),
        "qwg": np.ascontiguousarray(inputs["L0_ffn_qw_gate"], dtype=np.int32),
        "qwu": np.ascontiguousarray(inputs["L0_ffn_qw_up"], dtype=np.int32),
        "qwd": np.ascontiguousarray(inputs["L0_ffn_qw_down"], dtype=np.int32),
        "zg": np.ascontiguousarray(inputs["L0_ffn_z_gate"], dtype=np.int32),
        "zu": np.ascontiguousarray(inputs["L0_ffn_z_up"], dtype=np.int32),
        "zd": np.ascontiguousarray(inputs["L0_ffn_z_down"], dtype=np.int32),
        "sg": np.ascontiguousarray(inputs["L0_ffn_s_gate"], dtype=np.float32),
        "su": np.ascontiguousarray(inputs["L0_ffn_s_up"], dtype=np.float32),
        "sd": np.ascontiguousarray(inputs["L0_ffn_s_down"], dtype=np.float32),
    }
    nc = build_nc()
    in_maps = []
    for c in range(NCORES):
        b0, b1 = c, 15 - c
        xs = np.concatenate([x[b0 * BS:(b0 + 1) * BS], x[b1 * BS:(b1 + 1) * BS]], axis=0)
        m = dict(common)
        m["x"] = np.ascontiguousarray(xs)
        m["tbl"] = make_tables(c)
        in_maps.append(m)
    res = run_bass_kernel_spmd(nc, in_maps, core_ids=list(range(NCORES)))
    out = np.empty((1, S, D), dtype=np.float32)
    for c in range(NCORES):
        b0, b1 = c, 15 - c
        yc = np.asarray(res.results[c]["y"])
        out[0, b0 * BS:(b0 + 1) * BS] = yc[:BS]
        out[0, b1 * BS:(b1 + 1) * BS] = yc[BS:]
    return out


# version 07
# speedup vs baseline: 1.0502x; 1.0425x over previous
import math
from contextlib import ExitStack

import numpy as np
import ml_dtypes

import concourse.bass as bass
import concourse.mybir as mybir
from concourse.bass_utils import run_bass_kernel_spmd

F32 = mybir.dt.float32
BF16 = mybir.dt.bfloat16
FP16 = mybir.dt.float16
I32 = mybir.dt.int32
AF = mybir.ActivationFunctionType
ALU = mybir.AluOpType
AX = mybir.AxisListType

D = 4096
S = 8192
H = 32
DH = 128
HFF = 11008
BS = 512
NCORES = 8
TOK = 1024
DC = 32
GFF = 86
NKC = 96
NPAN = 43
SCALE = 1.0 / math.sqrt(192.0)
EXPB = -2.0
EPS = 1e-5

PHASES = 6


def make_tables(core):
    b0, b1 = core, 15 - core
    pb = np.zeros((128, NKC), dtype=np.float32)
    for m in range(NKC):
        if m < 4 or m >= 92:
            real = True
        elif m < 32:
            real = (m - 4) < 4 * b0
        else:
            real = (m - 32) < 4 * b1
        pb[:, m] = 1.0 if real else 0.0
    return pb


def make_consts():
    ident_bf = np.eye(128, dtype=ml_dtypes.bfloat16)
    ident_f32 = np.eye(128, dtype=np.float32)
    masks = np.zeros((128, 4 * 512), dtype=ml_dtypes.bfloat16)
    kr = np.arange(128)[:, None]
    q = np.arange(512)[None, :]
    for w in range(4):
        masks[:, w * 512:(w + 1) * 512] = (q >= w * 128 + kr).astype(ml_dtypes.bfloat16)
    oh32 = np.zeros((32, 32 * 128), dtype=np.float16)
    for g in range(32):
        oh32[g, g * 128:(g + 1) * 128] = 1.0
    ohD = np.zeros((GFF, GFF * 128), dtype=np.float16)
    for g in range(GFF):
        ohD[g, g * 128:(g + 1) * 128] = 1.0
    ones1r = np.ones((1, 128), dtype=np.float16)
    return ident_bf, ident_f32, masks, oh32, ohD, ones1r


def build_nc():
    nc = bass.Bass("TRN2", target_bir_lowering=False)

    x_in = nc.dram_tensor("x", [TOK, D], F32, kind="ExternalInput").ap()
    wq = nc.dram_tensor("wq", [D, 6144], F32, kind="ExternalInput").ap()
    wkv = nc.dram_tensor("wkv", [D, 8256], F32, kind="ExternalInput").ap()
    wo = nc.dram_tensor("wo", [D, D], F32, kind="ExternalInput").ap()
    qwg = nc.dram_tensor("qwg", [D, HFF], I32, kind="ExternalInput").ap()
    qwu = nc.dram_tensor("qwu", [D, HFF], I32, kind="ExternalInput").ap()
    qwd = nc.dram_tensor("qwd", [HFF, D], I32, kind="ExternalInput").ap()
    zg = nc.dram_tensor("zg", [32, HFF], I32, kind="ExternalInput").ap()
    zu = nc.dram_tensor("zu", [32, HFF], I32, kind="ExternalInput").ap()
    zd = nc.dram_tensor("zd", [GFF, D], I32, kind="ExternalInput").ap()
    sg = nc.dram_tensor("sg", [32, HFF], F32, kind="ExternalInput").ap()
    su = nc.dram_tensor("su", [32, HFF], F32, kind="ExternalInput").ap()
    sd = nc.dram_tensor("sd", [GFF, D], F32, kind="ExternalInput").ap()
    tbl_in = nc.dram_tensor("tbl", [128, NKC], F32, kind="ExternalInput").ap()
    y_out = nc.dram_tensor("y", [TOK, D], F32, kind="ExternalOutput").ap()

    kvK_in = nc.dram_tensor("kvK_in", [H * 8 * 128, 128], BF16).ap()
    kvV_in = nc.dram_tensor("kvV_in", [H * TOK, DH], BF16).ap()
    kvK_out = nc.dram_tensor("kvK_out", [NCORES * H * 8 * 128, 128], BF16, addr_space="Shared").ap()
    kvV_out = nc.dram_tensor("kvV_out", [NCORES * H * TOK, DH], BF16, addr_space="Shared").ap()
    qT_dram = nc.dram_tensor("qT_dram", [H * DH, TOK], BF16).ap()
    x1_dram = nc.dram_tensor("x1_dram", [TOK, D], F32).ap()
    hact_dram = nc.dram_tensor("hact_dram", [HFF, TOK], BF16).ap()
    wd_dram = nc.dram_tensor("wd_dram", [DC, HFF * 128], BF16).ap()
    nzg_dram = nc.dram_tensor("nzg_dram", [32, HFF], BF16).ap()
    nzu_dram = nc.dram_tensor("nzu_dram", [32, HFF], BF16).ap()
    nzd_dram = nc.dram_tensor("nzd_dram", [GFF, D], BF16).ap()
    qwg_bf = nc.dram_tensor("qwg_bf", [D, HFF], BF16).ap()
    qwu_bf = nc.dram_tensor("qwu_bf", [D, HFF], BF16).ap()

    ident_bf_np, ident_f32_np, masks_np, oh32_np, ohD_np, ones1r_np = make_consts()
    ident_bf_d = nc.inline_tensor(ident_bf_np, "c_ident_bf").ap()
    ident_f32_d = nc.inline_tensor(ident_f32_np, "c_ident_f32").ap()
    masks_d = nc.inline_tensor(masks_np, "c_masks").ap()
    oh32_d = nc.inline_tensor(oh32_np, "c_oh32").ap()
    ohD_d = nc.inline_tensor(ohD_np, "c_ohD").ap()
    ones1r_d = nc.inline_tensor(ones1r_np, "c_ones1r").ap()

    es = ExitStack()

    ident_bf = nc.alloc_sbuf_tensor("ident_bf_sb", [128, 128], BF16)
    ident_f32 = nc.alloc_sbuf_tensor("ident_f32_sb", [128, 128], F32)
    masks_sb = nc.alloc_sbuf_tensor("masks_sb", [128, 2048], BF16)
    oh32_sb = nc.alloc_sbuf_tensor("oh32_sb", [32, 4096], FP16)
    ones1r_sb = nc.alloc_sbuf_tensor("ones1r_sb", [1, 128], FP16)
    tbl_sb = nc.alloc_sbuf_tensor("tbl_sb", [128, NKC], F32)
    ones_col = nc.alloc_sbuf_tensor("ones_col", [128, 1], BF16)
    zero_col = nc.alloc_sbuf_tensor("zero_col", [128, 1], BF16)
    zero_t = nc.alloc_sbuf_tensor("zero_t", [128, 128], BF16)
    expb_sb = nc.alloc_sbuf_tensor("expb_sb", [128, 1], F32)

    single = [
        "PH", "ld_c", "ld_z", "ld_s", "s_nz", "st_nz",
        "ln_sq", "ln_v1", "ln_sqr", "ln_n", "ln_t", "ln_e",
        "p2mm", "p2ev", "p2tr", "p2ve", "cc",
        "se", "ex", "ep", "pa", "siv", "p3bc", "sbe", "sat",
        "wo_mm", "wo_ev", "wo_tr", "wo_add",
        "xg_mm", "xg_ev",
        "g_bc", "g_dq", "g_mm", "s_silu", "s_hm",
        "d_bc", "d_dq", "ld_sd",
        "ag_mm", "ag_ev", "ld_ha",
        "d6_mm", "d6_ev", "d6_tr", "d6_add",
        "vv", "pcast", "dbg",
    ]
    paired = [
        "ln_ld", "ld_w", "st_k", "st_qq", "st_v", "ld_kv",
        "ld_wo", "ld_xp", "st_x1", "st_xg",
        "ld_ff", "ld_nz", "ld_qd", "ld_oh", "st_ha", "st_wd",
        "ld_wd", "ld_nzd", "ld_x1", "st_ag", "st_out",
    ]
    sems = {}
    for n in single:
        sems[n] = es.enter_context(nc.semaphore(n))
    for n in paired:
        sems[n] = [es.enter_context(nc.semaphore(n + "0")),
                   es.enter_context(nc.semaphore(n + "1"))]

    class SM:
        pass
    for n, h_ in sems.items():
        setattr(SM, n, h_)

    blk = es.enter_context(nc.Block())

    ph_target = [0]
    vvc = [0]

    def barrier():
        ph_target[0] += 5
        tgt = ph_target[0]
        for eng in ("sync", "scalar", "vector", "tensor", "gpsimd"):
            @getattr(blk, eng)
            def _(e):
                e.drain()
                e.nop().then_inc(SM.PH, 1)
                e.wait_ge(SM.PH, tgt)

    def phase0():
        @blk.sync
        def _(sy):
            sy.dma_start(out=ident_bf[:, :], in_=ident_bf_d[:, :]).then_inc(SM.ld_c, 16)
            sy.dma_start(out=ident_f32[:, :], in_=ident_f32_d[:, :]).then_inc(SM.ld_c, 16)
            sy.dma_start(out=masks_sb[:, :], in_=masks_d[:, :]).then_inc(SM.ld_c, 16)
            sy.dma_start(out=oh32_sb[:, :], in_=oh32_d[:, :]).then_inc(SM.ld_c, 16)
            sy.dma_start(out=ones1r_sb[:, :], in_=ones1r_d[:, :]).then_inc(SM.ld_c, 16)
            sy.dma_start(out=tbl_sb[:, :], in_=tbl_in[:, :]).then_inc(SM.ld_c, 16)
            sy.wait_ge(SM.ld_c, 96)

        @blk.vector
        def _(v):
            v.memset(ones_col[:, :], 1.0)
            v.memset(zero_col[:, :], 0.0)
            v.memset(zero_t[:, :], 0.0)
            v.memset(expb_sb[:, :], EXPB)

        specs = [(zg, sg, nzg_dram, 32, HFF), (zu, su, nzu_dram, 32, HFF),
                 (zd, sd, nzd_dram, GFF, D)]
        with (
            nc.sbuf_tensor("nz_z", [GFF, HFF], BF16) as z_sb,
            nc.sbuf_tensor("nz_s", [GFF, HFF], F32) as s_sb,
            nc.sbuf_tensor("nz_o", [GFF, HFF], BF16) as o_sb,
        ):
            @blk.gpsimd
            def _(g):
                for j, (z_d, s_d, nz_d, P, F) in enumerate(specs):
                    if j >= 1:
                        g.wait_ge(SM.s_nz, j)
                    g.dma_start(out=z_sb[0:P, 0:F], in_=z_d[:, :]).then_inc(SM.ld_z, 16)

            @blk.sync
            def _(sy):
                for j, (z_d, s_d, nz_d, P, F) in enumerate(specs):
                    if j >= 1:
                        sy.wait_ge(SM.s_nz, j)
                    sy.dma_start(out=s_sb[0:P, 0:F], in_=s_d[:, :]).then_inc(SM.ld_s, 16)

            @blk.vector
            def _(v):
                for j, (z_d, s_d, nz_d, P, F) in enumerate(specs):
                    v.wait_ge(SM.ld_z, 16 * (j + 1))
                    v.wait_ge(SM.ld_s, 16 * (j + 1))
                    if j >= 1:
                        v.wait_ge(SM.st_nz, 16 * j)
                    v.scalar_tensor_tensor(o_sb[0:P, 0:F], z_sb[0:P, 0:F], -1.0,
                                           s_sb[0:P, 0:F], ALU.mult, ALU.mult
                                           ).then_inc(SM.s_nz, 1)

            @blk.scalar
            def _(s):
                for j, (z_d, s_d, nz_d, P, F) in enumerate(specs):
                    s.wait_ge(SM.s_nz, j + 1)
                    s.dma_start(out=nz_d[:, :], in_=o_sb[0:P, 0:F]).then_inc(SM.st_nz, 16)
                s.wait_ge(SM.st_nz, 48)
        barrier()

    ln_ctr = {"ld0": 0, "ld1": 0, "sq": 0, "v1": 0, "sqr": 0, "n": 0, "t": 0, "e": 0}

    def emit_ln(src_dram, xnT, tag):
        c = dict(ln_ctr)
        with (
            nc.sbuf_tensor(f"ln_xt_{tag}", [128, 2 * D], F32) as xt,
            nc.sbuf_tensor(f"ln_dump_{tag}", [128, D], BF16) as dump,
            nc.sbuf_tensor(f"ln_xn_{tag}", [128, 2 * D], BF16) as xn,
            nc.sbuf_tensor(f"ln_st_{tag}", [128, 8], F32) as st,
            nc.psum_tensor(f"ln_ps_{tag}", [128, 2048], BF16) as ptr,
        ):
            @blk.sync
            def _(sy):
                for tc in range(8):
                    if tc >= 2:
                        sy.wait_ge(SM.ln_n, c["n"] + tc - 1)
                        sy.wait_ge(SM.ln_sq, c["sq"] + tc - 1)
                    sy.dma_start(out=xt[:, (tc % 2) * D:(tc % 2 + 1) * D],
                                 in_=src_dram[tc * 128:(tc + 1) * 128, :]
                                 ).then_inc(SM.ln_ld[tc % 2], 16)

            @blk.scalar
            def _(s):
                for tc in range(8):
                    xv = xt[:, (tc % 2) * D:(tc % 2 + 1) * D]
                    s.wait_ge(SM.ln_ld[tc % 2], c["ld%d" % (tc % 2)] + 16 * (tc // 2 + 1))
                    s.activation(dump[:, :], xv, AF.Square,
                                 accum_out=st[:, 1:2]).then_inc(SM.ln_sq, 1)
                    s.wait_ge(SM.ln_v1, c["v1"] + tc + 1)
                    s.activation(st[:, 7:8], st[:, 6:7], AF.Sqrt).then_inc(SM.ln_sqr, 1)

            @blk.vector
            def _(v):
                for tc in range(9):
                    if tc < 8:
                        xv = xt[:, (tc % 2) * D:(tc % 2 + 1) * D]
                        xnv = xn[:, (tc % 2) * D:(tc % 2 + 1) * D]
                        v.wait_ge(SM.ln_ld[tc % 2], c["ld%d" % (tc % 2)] + 16 * (tc // 2 + 1))
                        v.reduce_sum(st[:, 0:1], xv, axis=AX.X).then_inc(SM.vv, 1)
                        vvc[0] += 1
                        v.wait_ge(SM.ln_sq, c["sq"] + tc + 1)
                        v.wait_ge(SM.vv, vvc[0])
                        v.tensor_scalar_mul(st[:, 2:3], st[:, 0:1], 1.0 / D).then_inc(SM.vv, 1)
                        vvc[0] += 1
                        v.tensor_scalar_mul(st[:, 3:4], st[:, 1:2], 1.0 / D).then_inc(SM.vv, 1)
                        vvc[0] += 1
                        v.wait_ge(SM.vv, vvc[0])
                        v.tensor_tensor(st[:, 5:6], st[:, 2:3], st[:, 2:3], ALU.mult).then_inc(SM.vv, 1)
                        vvc[0] += 1
                        v.wait_ge(SM.vv, vvc[0])
                        v.tensor_tensor(st[:, 4:5], st[:, 3:4], st[:, 5:6], ALU.subtract).then_inc(SM.vv, 1)
                        vvc[0] += 1
                        v.wait_ge(SM.vv, vvc[0])
                        v.tensor_scalar_add(st[:, 6:7], st[:, 4:5], EPS).then_inc(SM.ln_v1, 1)
                        v.wait_ge(SM.ln_sqr, c["sqr"] + tc + 1)
                        v.reciprocal(st[:, 5:6], st[:, 7:8]).then_inc(SM.vv, 1)
                        vvc[0] += 1
                        if tc >= 2:
                            v.wait_ge(SM.ln_t, c["t"] + 4 * (tc - 1))
                        v.wait_ge(SM.vv, vvc[0])
                        v.tensor_scalar(xnv, xv, st[:, 2:3], st[:, 5:6],
                                        ALU.subtract, ALU.mult).then_inc(SM.ln_n, 1)
                    if tc >= 1:
                        for grp in range(4):
                            gidx = (tc - 1) * 4 + grp
                            v.wait_ge(SM.ln_t, c["t"] + gidx + 1)
                            v.tensor_copy(
                                bass.AP(xnT, (grp * 8) * TOK + (tc - 1) * 128,
                                        [[DC * TOK, 128], [TOK, 8], [1, 128]]),
                                ptr[:, (gidx % 2) * 1024:(gidx % 2 + 1) * 1024].rearrange(
                                    "p (j e) -> p j e", j=8)
                            ).then_inc(SM.ln_e, 1)

            @blk.tensor
            def _(t):
                for tc in range(8):
                    off = (tc % 2) * D
                    t.wait_ge(SM.ln_n, c["n"] + tc + 1)
                    for grp in range(4):
                        gidx = tc * 4 + grp
                        if gidx >= 2:
                            t.wait_ge(SM.ln_e, c["e"] + gidx - 1)
                        for jj in range(8):
                            j = grp * 8 + jj
                            ins = t.transpose(
                                ptr[:, (gidx % 2) * 1024 + jj * 128:(gidx % 2) * 1024 + (jj + 1) * 128],
                                xn[:, off + j * 128:off + (j + 1) * 128],
                                ident_bf[:, :])
                            if jj == 7:
                                ins.then_inc(SM.ln_t, 1)

        ln_ctr["ld0"] += 64
        ln_ctr["ld1"] += 64
        ln_ctr["sq"] += 8
        ln_ctr["v1"] += 8
        ln_ctr["sqr"] += 8
        ln_ctr["n"] += 8
        ln_ctr["t"] += 32
        ln_ctr["e"] += 32

    def phase12():
        es12 = ExitStack()
        xn1T = es12.enter_context(nc.sbuf_tensor("xn1T", [128, DC * TOK], BF16))
        emit_ln(x_in, xn1T, "ln1")
        barrier()

        if PHASES == 1:
            @blk.gpsimd
            def _(g):
                for j in range(8):
                    g.dma_start(out=bass.AP(y_out.tensor, j * 128 * D, [[D, 128], [1, D]]),
                                in_=xn1T[:, j * 4096:(j + 1) * 4096]).then_inc(SM.dbg, 16)
                g.wait_ge(SM.dbg, 128)
            es12.close()
            return

        with (
            nc.sbuf_tensor("wpan", [128, 2 * 4096], BF16) as wpan,
            nc.sbuf_tensor("qkst", [128, 2 * 1024], BF16) as qkst,
            nc.sbuf_tensor("vtst", [128, 2 * 1024], BF16) as vtst,
            nc.psum_tensor("pQ", [128, 2048], F32) as pQ,
            nc.psum_tensor("pVt", [128, 2048], BF16) as pVt,
        ):
            def w_src(oc):
                if oc < 32:
                    return bass.AP(wkv.tensor, oc * 128,
                                   [[8256, 128], [8256 * 128, 32], [1, 128]])
                if oc < 64:
                    return bass.AP(wkv.tensor, 4096 + (oc - 32) * 128,
                                   [[8256, 128], [8256 * 128, 32], [1, 128]])
                return bass.AP(wq.tensor, (oc - 64) * 192,
                               [[6144, 128], [6144 * 128, 32], [1, 128]])

            @blk.gpsimd
            def _(g):
                for oc in range(96):
                    if oc >= 2:
                        g.wait_ge(SM.p2mm, oc - 1)
                    g.dma_start(out=wpan[:, (oc % 2) * 4096:(oc % 2 + 1) * 4096]
                                .rearrange("p (cc e) -> p cc e", cc=32),
                                in_=w_src(oc)).then_inc(SM.ld_w[oc % 2], 16)

            @blk.tensor
            def _(t):
                for oc in range(96):
                    t.wait_ge(SM.ld_w[oc % 2], 16 * (oc // 2 + 1))
                    if oc >= 2:
                        t.wait_ge(SM.p2ev, oc - 1)
                    pb = (oc % 2) * 1024
                    for dcc in range(DC):
                        lhs = wpan[:, (oc % 2) * 4096 + dcc * 128:(oc % 2) * 4096 + (dcc + 1) * 128]
                        for tt in range(2):
                            ins = t.matmul(pQ[:, pb + tt * 512:pb + (tt + 1) * 512],
                                           lhs,
                                           xn1T[:, dcc * TOK + tt * 512:dcc * TOK + (tt + 1) * 512],
                                           start=(dcc == 0), stop=(dcc == DC - 1))
                            if dcc == DC - 1 and tt == 1:
                                ins.then_inc(SM.p2mm, 1)
                    if 33 <= oc <= 64:
                        hh = oc - 33
                        t.wait_ge(SM.p2ev, (32 + hh) + 1)
                        if hh >= 2:
                            t.wait_ge(SM.p2ve, hh - 1)
                        for tc2 in range(8):
                            ins = t.transpose(
                                pVt[:, (hh % 2) * 1024 + tc2 * 128:(hh % 2) * 1024 + (tc2 + 1) * 128],
                                qkst[:, ((32 + hh) % 2) * 1024 + tc2 * 128:
                                     ((32 + hh) % 2) * 1024 + (tc2 + 1) * 128],
                                ident_bf[:, :])
                            if tc2 == 7:
                                ins.then_inc(SM.p2tr, 1)

            @blk.scalar
            def _(s):
                for oc in range(96):
                    s.wait_ge(SM.p2mm, oc + 1)
                    if oc >= 2:
                        prev = oc - 2
                        if prev < 32:
                            s.wait_ge(SM.st_k[prev % 2], 16 * (prev // 2 + 1))
                        elif prev < 64:
                            s.wait_ge(SM.p2tr, prev - 32 + 1)
                        else:
                            s.wait_ge(SM.st_qq[prev % 2], 16 * ((prev - 64) // 2 + 1))
                    s.activation(qkst[:, (oc % 2) * 1024:(oc % 2 + 1) * 1024],
                                 pQ[:, (oc % 2) * 1024:(oc % 2 + 1) * 1024],
                                 AF.Copy).then_inc(SM.p2ev, 1)

            @blk.sync
            def _(sy):
                for oc in range(96):
                    src = qkst[:, (oc % 2) * 1024:(oc % 2 + 1) * 1024]
                    if 32 <= oc < 64:
                        hh = oc - 32
                        sy.wait_ge(SM.p2ve, hh + 1)
                        sy.dma_start(out=bass.AP(kvV_in.tensor, hh * TOK * DH,
                                                 [[DH, 128], [128 * DH, 8], [1, DH]]),
                                     in_=vtst[:, (hh % 2) * 1024:(hh % 2 + 1) * 1024]
                                     .rearrange("p (tc e) -> p tc e", tc=8)
                                     ).then_inc(SM.st_v[hh % 2], 16)
                    if oc < 32:
                        sy.wait_ge(SM.p2ev, oc + 1)
                        sy.dma_start(out=bass.AP(kvK_in.tensor, oc * 8 * 16384,
                                                 [[128, 128], [16384, 8], [1, 128]]),
                                     in_=src.rearrange("p (c e) -> p c e", c=8)
                                     ).then_inc(SM.st_k[oc % 2], 16)
                    elif oc >= 64:
                        sy.wait_ge(SM.p2ev, oc + 1)
                        sy.dma_start(out=qT_dram[(oc - 64) * 128:(oc - 63) * 128, :], in_=src
                                     ).then_inc(SM.st_qq[oc % 2], 16)

            @blk.vector
            def _(v):
                for hh in range(32):
                    v.wait_ge(SM.p2tr, hh + 1)
                    if hh >= 2:
                        v.wait_ge(SM.st_v[hh % 2], 16 * (hh // 2))
                    v.tensor_copy(vtst[:, (hh % 2) * 1024:(hh % 2 + 1) * 1024],
                                  pVt[:, (hh % 2) * 1024:(hh % 2 + 1) * 1024]
                                  ).then_inc(SM.p2ve, 1)

            @blk.gpsimd
            def _(g):
                g.wait_ge(SM.st_k[0], 16 * 16)
                g.wait_ge(SM.st_k[1], 16 * 16)
                g.wait_ge(SM.st_v[0], 16 * 16)
                g.wait_ge(SM.st_v[1], 16 * 16)
                g.collective_compute("AllGather", ALU.bypass,
                                     replica_groups=[list(range(NCORES))],
                                     ins=[kvK_in], outs=[kvK_out]).then_inc(SM.cc, 1)
                g.collective_compute("AllGather", ALU.bypass,
                                     replica_groups=[list(range(NCORES))],
                                     ins=[kvV_in], outs=[kvV_out]).then_inc(SM.cc, 1)

            @blk.sync
            def _(sy):
                sy.wait_ge(SM.st_qq[0], 16 * 16)
                sy.wait_ge(SM.st_qq[1], 16 * 16)
                sy.wait_ge(SM.cc, 2)
        es12.close()
        barrier()

        if PHASES == 2:
            @blk.gpsimd
            def _(g):
                g.dma_start(out=bass.AP(y_out.tensor, 0, [[1024, 4096], [1, 1024]]),
                            in_=bass.AP(qT_dram.tensor, 0, [[1024, 4096], [1, 1024]])
                            ).then_inc(SM.dbg, 16)
                g.wait_ge(SM.dbg, 16)

    def phase3(attnT):
        DIAG = set(range(4)) | set(range(92, 96))
        SLOTS = NKC
        FK = 2 * SLOTS * 128
        with (
            nc.sbuf_tensor("K_sb", [128, FK], BF16) as K_sb,
            nc.sbuf_tensor("V_sb", [128, FK], BF16) as V_sb,
            nc.sbuf_tensor("Q_sb", [128, 2 * TOK], BF16) as Q_sb,
            nc.sbuf_tensor("probs", [128, 4 * 512], BF16) as probs,
            nc.sbuf_tensor("invs", [1, 1024], FP16) as invs,
            nc.sbuf_tensor("invs32", [1, 1024], F32) as invs32,
            nc.sbuf_tensor("bcs", [128, 1024], F32) as bcs,
            nc.psum_tensor("s_ps", [128, 2048], F32) as s_ps,
            nc.psum_tensor("a_ps", [128, 1024], F32) as a_ps,
            nc.psum_tensor("z_ps", [33, 512], F32) as z_ps,
            nc.psum_tensor("b_ps", [128, 512], F32) as b_ps,
        ):
            @blk.gpsimd
            def _(g):
                jobs = ([(qwg, qwg_bf, j) for j in range(32)] +
                        [(qwu, qwu_bf, j) for j in range(32)])
                for n, (qw_d, qw_b, j) in enumerate(jobs):
                    g.wait_ge(SM.pa, SLOTS * (n // 2))
                    g.dma_start(out=qw_b[j * 128:(j + 1) * 128, :],
                                in_=qw_d[j * 128:(j + 1) * 128, :]
                                ).then_inc(SM.pcast, 16)

            @blk.sync
            def _(sy):
                for h in range(H):
                    if h >= 2:
                        sy.wait_ge(SM.pa, SLOTS * (h - 1))
                    hk = (h % 2) * SLOTS * 128
                    for kv_out, kv_loc, sbuf_t in ((kvK_out, kvK_in, K_sb),
                                                   (kvV_out, kvV_in, V_sb)):
                        sy.dma_start(
                            out=bass.AP(sbuf_t, hk, [[FK, 128], [128, 4], [1, 128]]),
                            in_=bass.AP(kv_loc.tensor, (h * 8) * 16384,
                                        [[128, 128], [16384, 4], [1, 128]]),
                        ).then_inc(SM.ld_kv[h % 2], 16)
                        for gb in range(7):
                            sy.dma_start(
                                out=bass.AP(sbuf_t, hk + (4 + gb * 4) * 128,
                                            [[FK, 128], [128, 4], [1, 128]]),
                                in_=bass.AP(kv_out.tensor, (gb * H * 8 + h * 8) * 16384,
                                            [[128, 128], [16384, 4], [1, 128]]),
                            ).then_inc(SM.ld_kv[h % 2], 16)
                        for gb in range(15):
                            if gb < 8:
                                soff = (gb * H * 8 + h * 8) * 16384
                            else:
                                soff = ((15 - gb) * H * 8 + h * 8 + 4) * 16384
                            sy.dma_start(
                                out=bass.AP(sbuf_t, hk + (32 + gb * 4) * 128,
                                            [[FK, 128], [128, 4], [1, 128]]),
                                in_=bass.AP(kv_out.tensor, soff,
                                            [[128, 128], [16384, 4], [1, 128]]),
                            ).then_inc(SM.ld_kv[h % 2], 16)
                        sy.dma_start(
                            out=bass.AP(sbuf_t, hk + 92 * 128, [[FK, 128], [128, 4], [1, 128]]),
                            in_=bass.AP(kv_loc.tensor, (h * 8 + 4) * 16384,
                                        [[128, 128], [16384, 4], [1, 128]]),
                        ).then_inc(SM.ld_kv[h % 2], 16)
                    sy.dma_start(out=Q_sb[:, (h % 2) * TOK:(h % 2 + 1) * TOK],
                                 in_=qT_dram[h * 128:(h + 1) * 128, :]
                                 ).then_inc(SM.ld_kv[h % 2], 16)

            @blk.tensor
            def _(t):
                for h in range(H):
                    hk = (h % 2) * SLOTS * 128
                    hq = (h % 2) * TOK
                    t.wait_ge(SM.ld_kv[h % 2], 16 * 49 * (h // 2 + 1))
                    if h >= 1:
                        t.wait_ge(SM.sat, 2 * h)
                        t.wait_ge(SM.siv, 2 * h)
                    for qb in range(2):
                        t.matmul(bass.AP(a_ps, qb * 512, [[1024, 128], [1, 512]]),
                                 zero_t[:, :], masks_sb[:, 0:512],
                                 start=True, stop=True, skip_group_check=True)
                    t.matmul(z_ps[0:33, :], zero_t[:, 0:33], masks_sb[:, 0:512],
                             start=True, stop=True, skip_group_check=True)

                    def emit_s(m):
                        qb = 0 if m < 32 else 1
                        if m >= 4:
                            t.wait_ge(SM.ex, SLOTS * h + m - 3)
                        t.matmul(s_ps[:, (m % 4) * 512:(m % 4 + 1) * 512],
                                 K_sb[:, hk + m * 128:hk + (m + 1) * 128],
                                 Q_sb[:, hq + qb * 512:hq + (qb + 1) * 512],
                                 start=True, stop=True, skip_group_check=True
                                 ).then_inc(SM.se, 1)

                    emit_s(0)
                    emit_s(1)
                    for m in range(SLOTS):
                        qb = 0 if m < 32 else 1
                        if m + 2 < SLOTS:
                            emit_s(m + 2)
                        t.wait_ge(SM.ep, SLOTS * h + m + 1)
                        t.matmul(bass.AP(z_ps, qb * 32 * 512, [[512, 1], [1, 512]]),
                                 ones_col[:, :],
                                 probs[:, (m % 4) * 512:(m % 4 + 1) * 512],
                                 start=False, stop=False, skip_group_check=True)
                        t.matmul(bass.AP(a_ps, qb * 512, [[1024, 128], [1, 512]]),
                                 V_sb[:, hk + m * 128:hk + (m + 1) * 128],
                                 probs[:, (m % 4) * 512:(m % 4 + 1) * 512],
                                 start=False, stop=False, skip_group_check=True
                                 ).then_inc(SM.pa, 1)
                    for qb in range(2):
                        t.wait_ge(SM.siv, 2 * h + qb + 1)
                        if 2 * h + qb >= 1:
                            t.wait_ge(SM.sbe, 2 * h + qb)
                        t.matmul(b_ps[:, :], ones1r_sb[:, :],
                                 invs[0:1, qb * 512:(qb + 1) * 512],
                                 start=True, stop=True, skip_group_check=True
                                 ).then_inc(SM.p3bc, 1)

            @blk.scalar
            def _(s):
                for h in range(H):
                    for m in range(SLOTS):
                        s.wait_ge(SM.se, SLOTS * h + m + 1)
                        if SLOTS * h + m >= 4:
                            s.wait_ge(SM.pa, SLOTS * h + m - 3)
                        s.activation(probs[:, (m % 4) * 512:(m % 4 + 1) * 512],
                                     s_ps[:, (m % 4) * 512:(m % 4 + 1) * 512],
                                     AF.Exp, bias=expb_sb[:, :], scale=SCALE
                                     ).then_inc(SM.ex, 1)

            @blk.vector
            def _(v):
                for h in range(H):
                    for m in range(SLOTS):
                        v.wait_ge(SM.ex, SLOTS * h + m + 1)
                        pr = probs[:, (m % 4) * 512:(m % 4 + 1) * 512]
                        if m in DIAG:
                            w = m if m < 4 else m - 92
                            v.tensor_tensor(pr, pr, masks_sb[:, w * 512:(w + 1) * 512],
                                            ALU.mult).then_inc(SM.ep, 1)
                        else:
                            v.tensor_scalar_mul(pr, pr, tbl_sb[:, m:m + 1]
                                                ).then_inc(SM.ep, 1)
                    v.wait_ge(SM.pa, SLOTS * (h + 1))
                    for qb in range(2):
                        v.reciprocal(invs32[0:1, qb * 512:(qb + 1) * 512],
                                     z_ps[32 * qb:32 * qb + 1, :]).then_inc(SM.vv, 1)
                        vvc[0] += 1
                        v.wait_ge(SM.vv, vvc[0])
                        v.tensor_copy(invs[0:1, qb * 512:(qb + 1) * 512],
                                      invs32[0:1, qb * 512:(qb + 1) * 512]
                                      ).then_inc(SM.siv, 1)
                    for qb in range(2):
                        v.wait_ge(SM.p3bc, 2 * h + qb + 1)
                        v.tensor_copy(bcs[:, qb * 512:(qb + 1) * 512], b_ps[:, :]
                                      ).then_inc(SM.sbe, 1)
                        v.wait_ge(SM.sbe, 2 * h + qb + 1)
                        v.tensor_tensor(attnT[:, h * TOK + qb * 512:h * TOK + (qb + 1) * 512],
                                        a_ps[:, qb * 512:(qb + 1) * 512],
                                        bcs[:, qb * 512:(qb + 1) * 512],
                                        ALU.mult).then_inc(SM.sat, 1)
        barrier()

    def phase4(attnT):
        with (
            nc.sbuf_tensor("wopan", [128, 2 * 4096], BF16) as wopan,
            nc.sbuf_tensor("wsb", [128, 2 * 1024], F32) as wsb,
            nc.sbuf_tensor("xp4", [128, 2 * 512], F32) as xp,
            nc.sbuf_tensor("x1p", [128, 2 * 512], F32) as x1p,
            nc.psum_tensor("pW", [128, 2048], F32) as pW,
            nc.psum_tensor("pT4", [128, 1024], F32) as pT,
        ):
            @blk.gpsimd
            def _(g):
                for dc in range(DC):
                    if dc >= 2:
                        g.wait_ge(SM.wo_mm, dc - 1)
                    g.dma_start(out=wopan[:, (dc % 2) * 4096:(dc % 2 + 1) * 4096]
                                .rearrange("p (cc e) -> p cc e", cc=32),
                                in_=bass.AP(wo.tensor, dc * 128,
                                            [[D, 128], [D * 128, 32], [1, 128]])
                                ).then_inc(SM.ld_wo[dc % 2], 16)

            @blk.sync
            def _(sy):
                for gidx in range(64):
                    dc, gg = gidx // 2, gidx % 2
                    if gidx >= 2:
                        sy.wait_ge(SM.wo_add, gidx - 1)
                    sy.dma_start(out=xp[:, (gidx % 2) * 512:(gidx % 2 + 1) * 512]
                                 .rearrange("p (tc e) -> p tc e", tc=4),
                                 in_=bass.AP(x_in.tensor, gg * 512 * D + dc * 128,
                                             [[D, 128], [128 * D, 4], [1, 128]])
                                 ).then_inc(SM.ld_xp[gidx % 2], 16)

            @blk.tensor
            def _(t):
                for dc in range(DC):
                    t.wait_ge(SM.ld_wo[dc % 2], 16 * (dc // 2 + 1))
                    if dc >= 2:
                        t.wait_ge(SM.wo_ev, dc - 1)
                    pb = (dc % 2) * 1024
                    for hc in range(32):
                        lhs = wopan[:, (dc % 2) * 4096 + hc * 128:(dc % 2) * 4096 + (hc + 1) * 128]
                        for tt in range(2):
                            ins = t.matmul(pW[:, pb + tt * 512:pb + (tt + 1) * 512],
                                           lhs,
                                           attnT[:, hc * TOK + tt * 512:hc * TOK + (tt + 1) * 512],
                                           start=(hc == 0), stop=(hc == 31))
                            if hc == 31 and tt == 1:
                                ins.then_inc(SM.wo_mm, 1)
                    t.wait_ge(SM.wo_ev, dc + 1)
                    for gg in range(2):
                        gidx = dc * 2 + gg
                        if gidx >= 2:
                            t.wait_ge(SM.wo_add, gidx - 1)
                        for tc2 in range(4):
                            ins = t.transpose(
                                pT[:, (gidx % 2) * 512 + tc2 * 128:(gidx % 2) * 512 + (tc2 + 1) * 128],
                                wsb[:, (dc % 2) * 1024 + (gg * 4 + tc2) * 128:
                                    (dc % 2) * 1024 + (gg * 4 + tc2 + 1) * 128],
                                ident_f32[:, :])
                            if tc2 == 3:
                                ins.then_inc(SM.wo_tr, 1)

            @blk.vector
            def _(v):
                for dc in range(DC):
                    v.wait_ge(SM.wo_mm, dc + 1)
                    if dc >= 2:
                        v.wait_ge(SM.wo_tr, 2 * (dc - 1))
                    v.tensor_copy(wsb[:, (dc % 2) * 1024:(dc % 2 + 1) * 1024],
                                  pW[:, (dc % 2) * 1024:(dc % 2 + 1) * 1024]
                                  ).then_inc(SM.wo_ev, 1)
                    for gg in range(2):
                        gidx = dc * 2 + gg
                        v.wait_ge(SM.wo_tr, gidx + 1)
                        v.wait_ge(SM.ld_xp[gidx % 2], 16 * (gidx // 2 + 1))
                        if gidx >= 2:
                            v.wait_ge(SM.st_x1[gidx % 2], 16 * (gidx // 2))
                        v.tensor_tensor(x1p[:, (gidx % 2) * 512:(gidx % 2 + 1) * 512],
                                        pT[:, (gidx % 2) * 512:(gidx % 2 + 1) * 512],
                                        xp[:, (gidx % 2) * 512:(gidx % 2 + 1) * 512],
                                        ALU.add).then_inc(SM.wo_add, 1)

            @blk.scalar
            def _(s):
                for gidx in range(64):
                    dc, gg = gidx // 2, gidx % 2
                    s.wait_ge(SM.wo_add, gidx + 1)
                    s.dma_start(out=bass.AP(x1_dram.tensor, gg * 512 * D + dc * 128,
                                            [[D, 128], [128 * D, 4], [1, 128]]),
                                in_=x1p[:, (gidx % 2) * 512:(gidx % 2 + 1) * 512]
                                .rearrange("p (tc e) -> p tc e", tc=4)
                                ).then_inc(SM.st_x1[gidx % 2], 16)
                s.wait_ge(SM.st_x1[0], 16 * 32)
                s.wait_ge(SM.st_x1[1], 16 * 32)
        barrier()

    def phase5():
        es5 = ExitStack()
        xn2T = es5.enter_context(nc.sbuf_tensor("xn2T", [128, DC * TOK], BF16))
        Xg_sb = es5.enter_context(nc.sbuf_tensor("Xg_sb", [32, TOK], BF16))
        sd_sb = es5.enter_context(nc.sbuf_tensor("sd_sb", [GFF, D], FP16))
        emit_ln(x1_dram, xn2T, "ln2")
        barrier()

        with (
            nc.sbuf_tensor("xgst", [1, 1024], BF16) as xgst,
            nc.psum_tensor("pxg", [1, 1024], F32) as pxg,
        ):
            @blk.tensor
            def _(t):
                for i in range(64):
                    dcc, tt = i // 2, i % 2
                    if i >= 2:
                        t.wait_ge(SM.xg_ev, i - 1)
                    t.matmul(pxg[0:1, (i % 2) * 512:(i % 2 + 1) * 512],
                             ones_col[:, :],
                             xn2T[:, dcc * TOK + tt * 512:dcc * TOK + (tt + 1) * 512],
                             start=True, stop=True, skip_group_check=True
                             ).then_inc(SM.xg_mm, 1)

            @blk.vector
            def _(v):
                for i in range(64):
                    v.wait_ge(SM.xg_mm, i + 1)
                    if i >= 2:
                        v.wait_ge(SM.st_xg[i % 2], 16 * (i // 2))
                    v.tensor_copy(xgst[0:1, (i % 2) * 512:(i % 2 + 1) * 512],
                                  pxg[0:1, (i % 2) * 512:(i % 2 + 1) * 512]
                                  ).then_inc(SM.xg_ev, 1)

            @blk.sync
            def _(sy):
                for i in range(64):
                    dcc, tt = i // 2, i % 2
                    sy.wait_ge(SM.xg_ev, i + 1)
                    sy.dma_start(out=Xg_sb[dcc:dcc + 1, tt * 512:(tt + 1) * 512],
                                 in_=xgst[0:1, (i % 2) * 512:(i % 2 + 1) * 512]
                                 ).then_inc(SM.st_xg[i % 2], 16)
                sy.wait_ge(SM.st_xg[0], 16 * 32)
                sy.wait_ge(SM.st_xg[1], 16 * 32)
        barrier()

        @blk.gpsimd
        def _(g):
            g.dma_start(out=sd_sb[:, :], in_=sd[:, :]).then_inc(SM.ld_sd, 16)

        with (
            nc.sbuf_tensor("gbuf", [128, 2 * 8192], BF16) as gbuf,
            nc.sbuf_tensor("ubuf", [128, 2 * 8192], BF16) as ubuf,
            nc.sbuf_tensor("sgp", [32, 2 * 512], FP16) as sgp,
            nc.sbuf_tensor("nzs", [32, 8 * 128], BF16) as nzs,
            nc.sbuf_tensor("sgact", [128, 1024], BF16) as sgact,
            nc.sbuf_tensor("hstage", [128, 2 * 1024], BF16) as hstage,
            nc.sbuf_tensor("qwdb", [128, 2 * 4096], BF16) as qwdb,
            nc.sbuf_tensor("ohdb", [GFF, 2 * 128], FP16) as ohdb,
            nc.psum_tensor("pG", [128, 1024], F32) as pG,
            nc.psum_tensor("pU", [128, 1024], F32) as pU,
            nc.psum_tensor("pbc", [128, 1024], F32) as pbc,
            nc.psum_tensor("pbc2", [128, 1024], F32) as pbc2,
        ):
            @blk.gpsimd
            def _(g):
                g.wait_ge(SM.pcast, 16 * 64)
                for q in range(NPAN):
                    if q >= 2:
                        g.wait_ge(SM.g_mm, 4 * (q - 1))
                    for qw_d, buf in ((qwg_bf, gbuf), (qwu_bf, ubuf)):
                        g.dma_start(out=buf[:, (q % 2) * 8192:(q % 2 + 1) * 8192]
                                    .rearrange("p (cc e) -> p cc e", cc=32),
                                    in_=bass.AP(qw_d.tensor, q * 256,
                                                [[HFF, 128], [HFF * 128, 32], [1, 256]])
                                    ).then_inc(SM.ld_ff[q % 2], 16)
                    g.dma_start(out=sgp[0:32, (q % 2) * 512:(q % 2) * 512 + 256],
                                in_=sg[:, q * 256:(q + 1) * 256]).then_inc(SM.ld_ff[q % 2], 16)
                    g.dma_start(out=sgp[0:32, (q % 2) * 512 + 256:(q % 2 + 1) * 512],
                                in_=su[:, q * 256:(q + 1) * 256]).then_inc(SM.ld_ff[q % 2], 16)
                    for hh in range(2):
                        hcd = 2 * q + hh
                        if hcd >= 2:
                            g.wait_ge(SM.st_wd[hcd % 2], 16 * (hcd // 2))
                        g.dma_start(out=qwdb[:, (hcd % 2) * 4096:(hcd % 2 + 1) * 4096],
                                    in_=qwd[hcd * 128:(hcd + 1) * 128, :]
                                    ).then_inc(SM.ld_qd[hcd % 2], 16)

            @blk.sync
            def _(sy):
                for q in range(NPAN + 1):
                    if q < NPAN:
                        if q >= 2:
                            sy.wait_ge(SM.g_mm, 4 * (q - 1))
                        for mat, nz_d in enumerate([nzg_dram, nzu_dram]):
                            for hc in range(2):
                                slot = (q % 2) * 4 + mat * 2 + hc
                                sy.dma_start(out=nzs[0:32, slot * 128:(slot + 1) * 128],
                                             in_=nz_d[:, (2 * q + hc) * 128:(2 * q + hc + 1) * 128]
                                             ).then_inc(SM.ld_nz[q % 2], 16)
                        for hh in range(2):
                            hcd = 2 * q + hh
                            if hcd >= 2:
                                sy.wait_ge(SM.d_bc, 8 * (hcd - 1))
                            sy.dma_start(out=ohdb[0:GFF, (hcd % 2) * 128:(hcd % 2 + 1) * 128],
                                         in_=bass.AP(ohD_d.tensor, hcd * 128,
                                                     [[GFF * 128, GFF], [1, 128]])
                                         ).then_inc(SM.ld_oh[hcd % 2], 16)
                    if q >= 1:
                        for hc in range(2):
                            hcg = 2 * (q - 1) + hc
                            sy.wait_ge(SM.s_hm, hcg + 1)
                            sy.dma_start(out=hact_dram[hcg * 128:(hcg + 1) * 128, :],
                                         in_=hstage[:, (hcg % 2) * 1024:(hcg % 2 + 1) * 1024]
                                         ).then_inc(SM.st_ha[hcg % 2], 16)
                        for hh in range(2):
                            hcd = 2 * (q - 1) + hh
                            sy.wait_ge(SM.d_dq, 8 * (hcd + 1))
                            sy.dma_start(out=bass.AP(wd_dram.tensor, hcd * 128 * 128,
                                                     [[128, 128], [HFF * 128, 32], [1, 128]]),
                                         in_=qwdb[:, (hcd % 2) * 4096:(hcd % 2 + 1) * 4096]
                                         .rearrange("p (cc e) -> p cc e", cc=32)
                                         ).then_inc(SM.st_wd[hcd % 2], 16)
                sy.wait_ge(SM.st_ha[0], 16 * 43)
                sy.wait_ge(SM.st_ha[1], 16 * 43)
                sy.wait_ge(SM.st_wd[0], 16 * 43)
                sy.wait_ge(SM.st_wd[1], 16 * 43)

            @blk.tensor
            def _(t):
                nbc = 0
                nbc2 = 0
                for q in range(NPAN + 1):
                    if q < NPAN:
                        t.wait_ge(SM.ld_ff[q % 2], 16 * 4 * (q // 2 + 1))
                        for dcc in range(32):
                            nbc += 1
                            if nbc >= 3:
                                t.wait_ge(SM.g_dq, 2 * (nbc - 2))
                            t.matmul(pbc[:, (nbc % 2) * 512:(nbc % 2 + 1) * 512],
                                     oh32_sb[0:32, dcc * 128:(dcc + 1) * 128],
                                     sgp[0:32, (q % 2) * 512:(q % 2 + 1) * 512],
                                     start=True, stop=True, skip_group_check=True
                                     ).then_inc(SM.g_bc, 1)
                    if q >= 1:
                        qq = q - 1
                        t.wait_ge(SM.g_dq, 64 * (qq + 1))
                        t.wait_ge(SM.ld_nz[qq % 2], 16 * 4 * (qq // 2 + 1))
                        for hc in range(2):
                            for mat, (buf, ps) in enumerate([(gbuf, pG), (ubuf, pU)]):
                                if 4 * qq + 2 * hc + mat >= 2:
                                    if mat == 0:
                                        t.wait_ge(SM.s_silu, 2 * qq + hc)
                                    else:
                                        t.wait_ge(SM.s_hm, 2 * qq + hc)
                                for tt in range(2):
                                    for dcc in range(32):
                                        t.matmul(ps[:, tt * 512:(tt + 1) * 512],
                                                 buf[:, (qq % 2) * 8192 + dcc * 256 + hc * 128:
                                                     (qq % 2) * 8192 + dcc * 256 + hc * 128 + 128],
                                                 xn2T[:, dcc * TOK + tt * 512:dcc * TOK + (tt + 1) * 512],
                                                 start=(dcc == 0), stop=False,
                                                 skip_group_check=True)
                                    slot = (qq % 2) * 4 + mat * 2 + hc
                                    ins = t.matmul(ps[:, tt * 512:(tt + 1) * 512],
                                                   nzs[0:32, slot * 128:(slot + 1) * 128],
                                                   Xg_sb[0:32, tt * 512:(tt + 1) * 512],
                                                   start=False, stop=True,
                                                   skip_group_check=True)
                                    if tt == 1:
                                        ins.then_inc(SM.g_mm, 1)
                    if q < NPAN:
                        t.wait_ge(SM.ld_sd, 16)
                        for hh in range(2):
                            hcd = 2 * q + hh
                            t.wait_ge(SM.ld_oh[hcd % 2], 16 * (hcd // 2 + 1))
                            for dt in range(8):
                                nbc2 += 1
                                if nbc2 >= 3:
                                    t.wait_ge(SM.d_dq, nbc2 - 2)
                                t.matmul(pbc2[:, (nbc2 % 2) * 512:(nbc2 % 2 + 1) * 512],
                                         ohdb[0:GFF, (hcd % 2) * 128:(hcd % 2 + 1) * 128],
                                         sd_sb[0:GFF, dt * 512:(dt + 1) * 512],
                                         start=True, stop=True, skip_group_check=True
                                         ).then_inc(SM.d_bc, 1)

            @blk.scalar
            def _(s):
                for qq in range(NPAN):
                    for hc in range(2):
                        hcg = 2 * qq + hc
                        s.wait_ge(SM.g_mm, 4 * qq + 2 * hc + 1)
                        if hcg >= 1:
                            s.wait_ge(SM.s_hm, hcg)
                        s.activation(sgact[:, :], pG[:, :], AF.Silu).then_inc(SM.s_silu, 1)

            @blk.vector
            def _(v):
                ndq = 0
                ndq2 = 0
                for q in range(NPAN + 1):
                    if q < NPAN:
                        v.wait_ge(SM.ld_ff[q % 2], 16 * 4 * (q // 2 + 1))
                        for dcc in range(32):
                            ndq += 1
                            v.wait_ge(SM.g_bc, ndq)
                            for mat, buf in enumerate([gbuf, ubuf]):
                                v.tensor_tensor(
                                    buf[:, (q % 2) * 8192 + dcc * 256:(q % 2) * 8192 + (dcc + 1) * 256],
                                    buf[:, (q % 2) * 8192 + dcc * 256:(q % 2) * 8192 + (dcc + 1) * 256],
                                    pbc[:, (ndq % 2) * 512 + mat * 256:(ndq % 2) * 512 + (mat + 1) * 256],
                                    ALU.mult).then_inc(SM.g_dq, 1)
                    if q >= 1:
                        qq = q - 1
                        for hc in range(2):
                            hcg = 2 * qq + hc
                            v.wait_ge(SM.s_silu, hcg + 1)
                            v.wait_ge(SM.g_mm, 4 * qq + 2 * hc + 2)
                            if hcg >= 2:
                                v.wait_ge(SM.st_ha[hcg % 2], 16 * (hcg // 2))
                            v.tensor_tensor(hstage[:, (hcg % 2) * 1024:(hcg % 2 + 1) * 1024],
                                            pU[:, :], sgact[:, :],
                                            ALU.mult).then_inc(SM.s_hm, 1)
                    if q < NPAN:
                        for hh in range(2):
                            hcd = 2 * q + hh
                            v.wait_ge(SM.ld_qd[hcd % 2], 16 * (hcd // 2 + 1))
                            for dt in range(8):
                                ndq2 += 1
                                v.wait_ge(SM.d_bc, ndq2)
                                v.tensor_tensor(
                                    qwdb[:, (hcd % 2) * 4096 + dt * 512:(hcd % 2) * 4096 + (dt + 1) * 512],
                                    qwdb[:, (hcd % 2) * 4096 + dt * 512:(hcd % 2) * 4096 + (dt + 1) * 512],
                                    pbc2[:, (ndq2 % 2) * 512:(ndq2 % 2 + 1) * 512],
                                    ALU.mult).then_inc(SM.d_dq, 1)
        es5.close()
        barrier()

        if PHASES == 5:
            @blk.gpsimd
            def _(g):
                g.dma_start(out=bass.AP(y_out.tensor, 0, [[1024, 4096], [1, 1024]]),
                            in_=bass.AP(hact_dram.tensor, 0, [[1024, 4096], [1, 1024]])
                            ).then_inc(SM.dbg, 16)
                g.wait_ge(SM.dbg, 16)

    def phase6():
        with (
            nc.sbuf_tensor("hact_sb", [128, GFF * 512], BF16) as hact_sb,
            nc.sbuf_tensor("wdc", [128, 2 * GFF * 128], BF16) as wdc,
            nc.sbuf_tensor("nzd6", [GFF, 2 * 128], BF16) as nzd6,
            nc.sbuf_tensor("Ag_sb", [GFF, 512], BF16) as Ag_sb,
            nc.sbuf_tensor("agst", [1, 1024], BF16) as agst,
            nc.sbuf_tensor("dsb", [128, 2 * 512], F32) as dsb,
            nc.sbuf_tensor("xp6", [128, 2 * 512], F32) as xp6,
            nc.sbuf_tensor("yp6", [128, 2 * 512], F32) as yp6,
            nc.psum_tensor("pD", [128, 1024], F32) as pD,
            nc.psum_tensor("pT6", [128, 1024], F32) as pT6,
            nc.psum_tensor("pag", [1, 1024], F32) as pag,
        ):
            @blk.sync
            def _(sy):
                for t_ in range(2):
                    if t_ == 1:
                        sy.wait_ge(SM.d6_mm, 32)
                    sy.dma_start(out=hact_sb[:].rearrange("p (gg e) -> p gg e", gg=GFF),
                                 in_=bass.AP(hact_dram.tensor, t_ * 512,
                                             [[TOK, 128], [128 * TOK, GFF], [1, 512]])
                                 ).then_inc(SM.ld_ha, 16)
                    for dc in range(DC):
                        i = t_ * 32 + dc
                        if i >= 2:
                            sy.wait_ge(SM.d6_mm, i - 1)
                        sy.dma_start(out=wdc[:, (i % 2) * GFF * 128:(i % 2 + 1) * GFF * 128],
                                     in_=bass.AP(wd_dram.tensor, dc * HFF * 128,
                                                 [[128, 128], [128 * 128, GFF], [1, 128]])
                                     ).then_inc(SM.ld_wd[i % 2], 16)
                        if i >= 2:
                            sy.wait_ge(SM.d6_add, i - 1)
                        sy.dma_start(out=xp6[:, (i % 2) * 512:(i % 2 + 1) * 512]
                                     .rearrange("p (tc e) -> p tc e", tc=4),
                                     in_=bass.AP(x1_dram.tensor, t_ * 512 * D + dc * 128,
                                                 [[D, 128], [128 * D, 4], [1, 128]])
                                     ).then_inc(SM.ld_x1[i % 2], 16)

            @blk.scalar
            def _(s):
                for t_ in range(2):
                    if t_ == 1:
                        s.wait_ge(SM.d6_mm, 32)
                    for gg in range(GFF):
                        i2 = t_ * GFF + gg
                        s.wait_ge(SM.ag_ev, i2 + 1)
                        s.dma_start(out=Ag_sb[gg:gg + 1, :],
                                    in_=agst[0:1, (i2 % 2) * 512:(i2 % 2) * 512 + 512]
                                    ).then_inc(SM.st_ag[i2 % 2], 16)
                    def emit_out(i_):
                        dc_ = i_ - t_ * 32
                        s.wait_ge(SM.d6_add, i_ + 1)
                        s.dma_start(out=bass.AP(y_out.tensor, t_ * 512 * D + dc_ * 128,
                                                [[D, 128], [128 * D, 4], [1, 128]]),
                                    in_=yp6[:, (i_ % 2) * 512:(i_ % 2 + 1) * 512]
                                    .rearrange("p (tc e) -> p tc e", tc=4)
                                    ).then_inc(SM.st_out[i_ % 2], 16)

                    for dc in range(DC):
                        i = t_ * 32 + dc
                        if i >= 2:
                            s.wait_ge(SM.d6_mm, i - 1)
                        s.dma_start(out=nzd6[0:GFF, (i % 2) * 128:(i % 2 + 1) * 128],
                                    in_=nzd_dram[:, dc * 128:(dc + 1) * 128]
                                    ).then_inc(SM.ld_nzd[i % 2], 16)
                        if dc >= 2:
                            emit_out(i - 2)
                    emit_out(t_ * 32 + 30)
                    emit_out(t_ * 32 + 31)
                s.wait_ge(SM.st_out[0], 16 * 32)
                s.wait_ge(SM.st_out[1], 16 * 32)

            @blk.tensor
            def _(t):
                def emit_tr(ip):
                    t.wait_ge(SM.d6_ev, ip + 1)
                    if ip >= 2:
                        t.wait_ge(SM.d6_add, ip - 1)
                    for tc2 in range(4):
                        ins = t.transpose(
                            pT6[:, (ip % 2) * 512 + tc2 * 128:(ip % 2) * 512 + (tc2 + 1) * 128],
                            dsb[:, (ip % 2) * 512 + tc2 * 128:(ip % 2) * 512 + (tc2 + 1) * 128],
                            ident_f32[:, :])
                        if tc2 == 3:
                            ins.then_inc(SM.d6_tr, 1)

                for t_ in range(2):
                    t.wait_ge(SM.ld_ha, 16 * (t_ + 1))
                    for gg in range(GFF):
                        i2 = t_ * GFF + gg
                        if i2 >= 2:
                            t.wait_ge(SM.ag_ev, i2 - 1)
                        t.matmul(pag[0:1, (i2 % 2) * 512:(i2 % 2) * 512 + 512],
                                 ones_col[:, :],
                                 hact_sb[:, gg * 512:(gg + 1) * 512],
                                 start=True, stop=True, skip_group_check=True
                                 ).then_inc(SM.ag_mm, 1)
                    t.wait_ge(SM.st_ag[0], 16 * 43 * (t_ + 1))
                    t.wait_ge(SM.st_ag[1], 16 * 43 * (t_ + 1))
                    for dc in range(DC):
                        i = t_ * 32 + dc
                        t.wait_ge(SM.ld_wd[i % 2], 16 * (i // 2 + 1))
                        t.wait_ge(SM.ld_nzd[i % 2], 16 * (i // 2 + 1))
                        if i >= 2:
                            t.wait_ge(SM.d6_ev, i - 1)
                        for gg in range(GFF):
                            t.matmul(pD[:, (i % 2) * 512:(i % 2 + 1) * 512],
                                     wdc[:, (i % 2) * GFF * 128 + gg * 128:
                                         (i % 2) * GFF * 128 + (gg + 1) * 128],
                                     hact_sb[:, gg * 512:(gg + 1) * 512],
                                     start=(gg == 0), stop=False, skip_group_check=True)
                        t.matmul(pD[:, (i % 2) * 512:(i % 2 + 1) * 512],
                                 nzd6[0:GFF, (i % 2) * 128:(i % 2 + 1) * 128],
                                 Ag_sb[0:GFF, :],
                                 start=False, stop=True, skip_group_check=True
                                 ).then_inc(SM.d6_mm, 1)
                        if dc >= 1:
                            emit_tr(i - 1)
                    emit_tr(t_ * 32 + 31)

            @blk.vector
            def _(v):
                for t_ in range(2):
                    for gg in range(GFF):
                        i2 = t_ * GFF + gg
                        v.wait_ge(SM.ag_mm, i2 + 1)
                        if i2 >= 2:
                            v.wait_ge(SM.st_ag[i2 % 2], 16 * (i2 // 2))
                        v.tensor_copy(agst[0:1, (i2 % 2) * 512:(i2 % 2) * 512 + 512],
                                      pag[0:1, (i2 % 2) * 512:(i2 % 2) * 512 + 512]
                                      ).then_inc(SM.ag_ev, 1)
                    for dc in range(DC):
                        i = t_ * 32 + dc
                        v.wait_ge(SM.d6_mm, i + 1)
                        if i >= 2:
                            v.wait_ge(SM.d6_tr, i - 1)
                        v.tensor_copy(dsb[:, (i % 2) * 512:(i % 2 + 1) * 512],
                                      pD[:, (i % 2) * 512:(i % 2 + 1) * 512]
                                      ).then_inc(SM.d6_ev, 1)
                        v.wait_ge(SM.d6_tr, i + 1)
                        v.wait_ge(SM.ld_x1[i % 2], 16 * (i // 2 + 1))
                        if i >= 2:
                            v.wait_ge(SM.st_out[i % 2], 16 * (i // 2))
                        v.tensor_tensor(yp6[:, (i % 2) * 512:(i % 2 + 1) * 512],
                                        pT6[:, (i % 2) * 512:(i % 2 + 1) * 512],
                                        xp6[:, (i % 2) * 512:(i % 2 + 1) * 512],
                                        ALU.add).then_inc(SM.d6_add, 1)

    phase0()
    phase12()
    if PHASES >= 3:
        es34 = ExitStack()
        attnT = es34.enter_context(nc.sbuf_tensor("attnT", [128, H * TOK], BF16))
        phase3(attnT)
        if PHASES == 3:
            @blk.gpsimd
            def _(g):
                for j in range(8):
                    g.dma_start(out=bass.AP(y_out.tensor, j * 128 * D, [[D, 128], [1, D]]),
                                in_=attnT[:, j * 4096:(j + 1) * 4096]).then_inc(SM.dbg, 16)
                g.wait_ge(SM.dbg, 128)
    if PHASES >= 4:
        phase4(attnT)
        if PHASES == 4:
            @blk.gpsimd
            def _(g):
                g.dma_start(out=y_out[:, :], in_=x1_dram[:, :]).then_inc(SM.dbg, 16)
                g.wait_ge(SM.dbg, 16)
    if PHASES >= 3:
        es34.close()
    if PHASES >= 5:
        phase5()
    if PHASES >= 6:
        phase6()
        barrier()

    es.close()
    return nc


def kernel(**inputs):
    x = np.ascontiguousarray(np.asarray(inputs["L0_attn_hidden_states"])[0], dtype=np.float32)
    common = {
        "wq": np.ascontiguousarray(inputs["L0_attn_wq"], dtype=np.float32),
        "wkv": np.ascontiguousarray(inputs["L0_attn_wkv"], dtype=np.float32),
        "wo": np.ascontiguousarray(inputs["L0_attn_wo"], dtype=np.float32),
        "qwg": np.ascontiguousarray(inputs["L0_ffn_qw_gate"], dtype=np.int32),
        "qwu": np.ascontiguousarray(inputs["L0_ffn_qw_up"], dtype=np.int32),
        "qwd": np.ascontiguousarray(inputs["L0_ffn_qw_down"], dtype=np.int32),
        "zg": np.ascontiguousarray(inputs["L0_ffn_z_gate"], dtype=np.int32),
        "zu": np.ascontiguousarray(inputs["L0_ffn_z_up"], dtype=np.int32),
        "zd": np.ascontiguousarray(inputs["L0_ffn_z_down"], dtype=np.int32),
        "sg": np.ascontiguousarray(inputs["L0_ffn_s_gate"], dtype=np.float32),
        "su": np.ascontiguousarray(inputs["L0_ffn_s_up"], dtype=np.float32),
        "sd": np.ascontiguousarray(inputs["L0_ffn_s_down"], dtype=np.float32),
    }
    nc = build_nc()
    in_maps = []
    for c in range(NCORES):
        b0, b1 = c, 15 - c
        xs = np.concatenate([x[b0 * BS:(b0 + 1) * BS], x[b1 * BS:(b1 + 1) * BS]], axis=0)
        m = dict(common)
        m["x"] = np.ascontiguousarray(xs)
        m["tbl"] = make_tables(c)
        in_maps.append(m)
    res = run_bass_kernel_spmd(nc, in_maps, core_ids=list(range(NCORES)))
    out = np.empty((1, S, D), dtype=np.float32)
    for c in range(NCORES):
        b0, b1 = c, 15 - c
        yc = np.asarray(res.results[c]["y"])
        out[0, b0 * BS:(b0 + 1) * BS] = yc[:BS]
        out[0, b1 * BS:(b1 + 1) * BS] = yc[BS:]
    return out


# version 08
# speedup vs baseline: 1.0624x; 1.0116x over previous
import math
from contextlib import ExitStack

import numpy as np
import ml_dtypes

import concourse.bass as bass
import concourse.mybir as mybir
from concourse.bass_utils import run_bass_kernel_spmd

F32 = mybir.dt.float32
BF16 = mybir.dt.bfloat16
FP16 = mybir.dt.float16
I32 = mybir.dt.int32
AF = mybir.ActivationFunctionType
ALU = mybir.AluOpType
AX = mybir.AxisListType

D = 4096
S = 8192
H = 32
DH = 128
HFF = 11008
BS = 512
NCORES = 8
TOK = 1024
DC = 32
GFF = 86
NKC = 96
NPAN = 43
SCALE = 1.0 / math.sqrt(192.0)
EXPB = -2.0
EPS = 1e-5

PHASES = 6


def make_tables(core):
    b0, b1 = core, 15 - core
    pb = np.zeros((128, NKC), dtype=np.float32)
    for m in range(NKC):
        if m < 4 or m >= 92:
            real = True
        elif m < 32:
            real = (m - 4) < 4 * b0
        else:
            real = (m - 32) < 4 * b1
        pb[:, m] = 1.0 if real else 0.0
    return pb


def make_consts():
    ident_bf = np.eye(128, dtype=ml_dtypes.bfloat16)
    ident_f32 = np.eye(128, dtype=np.float32)
    masks = np.zeros((128, 4 * 512), dtype=ml_dtypes.bfloat16)
    kr = np.arange(128)[:, None]
    q = np.arange(512)[None, :]
    for w in range(4):
        masks[:, w * 512:(w + 1) * 512] = (q >= w * 128 + kr).astype(ml_dtypes.bfloat16)
    oh32 = np.zeros((32, 32 * 128), dtype=np.float16)
    for g in range(32):
        oh32[g, g * 128:(g + 1) * 128] = 1.0
    ohD = np.zeros((GFF, GFF * 128), dtype=np.float16)
    for g in range(GFF):
        ohD[g, g * 128:(g + 1) * 128] = 1.0
    ones1r = np.ones((1, 128), dtype=np.float16)
    return ident_bf, ident_f32, masks, oh32, ohD, ones1r


def build_nc():
    nc = bass.Bass("TRN2", target_bir_lowering=False)

    x_in = nc.dram_tensor("x", [TOK, D], F32, kind="ExternalInput").ap()
    wq = nc.dram_tensor("wq", [D, 6144], F32, kind="ExternalInput").ap()
    wkv = nc.dram_tensor("wkv", [D, 8256], F32, kind="ExternalInput").ap()
    wo = nc.dram_tensor("wo", [D, D], F32, kind="ExternalInput").ap()
    qwg = nc.dram_tensor("qwg", [D, HFF], I32, kind="ExternalInput").ap()
    qwu = nc.dram_tensor("qwu", [D, HFF], I32, kind="ExternalInput").ap()
    qwd = nc.dram_tensor("qwd", [HFF, D], I32, kind="ExternalInput").ap()
    zg = nc.dram_tensor("zg", [32, HFF], I32, kind="ExternalInput").ap()
    zu = nc.dram_tensor("zu", [32, HFF], I32, kind="ExternalInput").ap()
    zd = nc.dram_tensor("zd", [GFF, D], I32, kind="ExternalInput").ap()
    sg = nc.dram_tensor("sg", [32, HFF], F32, kind="ExternalInput").ap()
    su = nc.dram_tensor("su", [32, HFF], F32, kind="ExternalInput").ap()
    sd = nc.dram_tensor("sd", [GFF, D], F32, kind="ExternalInput").ap()
    tbl_in = nc.dram_tensor("tbl", [128, NKC], F32, kind="ExternalInput").ap()
    y_out = nc.dram_tensor("y", [TOK, D], F32, kind="ExternalOutput").ap()

    kvK_in = nc.dram_tensor("kvK_in", [H * 8 * 128, 128], BF16).ap()
    kvV_in = nc.dram_tensor("kvV_in", [H * TOK, DH], BF16).ap()
    kvK_out = nc.dram_tensor("kvK_out", [NCORES * H * 8 * 128, 128], BF16, addr_space="Shared").ap()
    kvV_out = nc.dram_tensor("kvV_out", [NCORES * H * TOK, DH], BF16, addr_space="Shared").ap()
    qT_dram = nc.dram_tensor("qT_dram", [H * DH, TOK], BF16).ap()
    x1_dram = nc.dram_tensor("x1_dram", [TOK, D], F32).ap()
    hact_dram = nc.dram_tensor("hact_dram", [HFF, TOK], BF16).ap()
    wd_dram = nc.dram_tensor("wd_dram", [DC, HFF * 128], BF16).ap()
    nzg_dram = nc.dram_tensor("nzg_dram", [32, HFF], BF16).ap()
    nzu_dram = nc.dram_tensor("nzu_dram", [32, HFF], BF16).ap()
    nzd_dram = nc.dram_tensor("nzd_dram", [GFF, D], BF16).ap()
    qwg_bf = nc.dram_tensor("qwg_bf", [D, HFF], BF16).ap()
    qwu_bf = nc.dram_tensor("qwu_bf", [D, HFF], BF16).ap()

    ident_bf_np, ident_f32_np, masks_np, oh32_np, ohD_np, ones1r_np = make_consts()
    ident_bf_d = nc.inline_tensor(ident_bf_np, "c_ident_bf").ap()
    ident_f32_d = nc.inline_tensor(ident_f32_np, "c_ident_f32").ap()
    masks_d = nc.inline_tensor(masks_np, "c_masks").ap()
    oh32_d = nc.inline_tensor(oh32_np, "c_oh32").ap()
    ohD_d = nc.inline_tensor(ohD_np, "c_ohD").ap()
    ones1r_d = nc.inline_tensor(ones1r_np, "c_ones1r").ap()

    es = ExitStack()

    ident_bf = nc.alloc_sbuf_tensor("ident_bf_sb", [128, 128], BF16)
    ident_f32 = nc.alloc_sbuf_tensor("ident_f32_sb", [128, 128], F32)
    masks_sb = nc.alloc_sbuf_tensor("masks_sb", [128, 2048], BF16)
    oh32_sb = nc.alloc_sbuf_tensor("oh32_sb", [32, 4096], FP16)
    ones1r_sb = nc.alloc_sbuf_tensor("ones1r_sb", [1, 128], FP16)
    tbl_sb = nc.alloc_sbuf_tensor("tbl_sb", [128, NKC], F32)
    ones_col = nc.alloc_sbuf_tensor("ones_col", [128, 1], BF16)
    zero_col = nc.alloc_sbuf_tensor("zero_col", [128, 1], BF16)
    zero_t = nc.alloc_sbuf_tensor("zero_t", [128, 128], BF16)
    expb_sb = nc.alloc_sbuf_tensor("expb_sb", [128, 1], F32)

    single = [
        "PH", "ld_c", "ld_z", "ld_s", "s_nz", "st_nz",
        "ln_sq", "ln_v1", "ln_sqr", "ln_n", "ln_t", "ln_e",
        "p2mm", "p2ev", "p2tr", "p2ve", "cc",
        "se", "ex", "ep", "pa", "siv", "p3bc", "sbe", "sat",
        "wo_mm", "wo_ev", "wo_tr", "wo_add",
        "xg_mm", "xg_ev",
        "g_bc", "g_dq", "g_mm", "s_silu", "s_hm",
        "d_bc", "d_dq", "ld_sd",
        "ag_mm", "ag_ev", "ld_ha",
        "d6_mm", "d6_ev", "d6_tr", "d6_add",
        "vv", "pcast", "dbg",
    ]
    paired = [
        "ln_ld", "ld_w", "st_k", "st_qq", "st_v", "ld_kv",
        "ld_wo", "ld_xp", "st_x1", "st_xg",
        "ld_ff", "ld_fs", "ld_nz", "ld_qd", "ld_oh", "st_ha", "st_wd",
        "ld_wd", "ld_nzd", "ld_x1", "st_ag", "st_out",
    ]
    sems = {}
    for n in single:
        sems[n] = es.enter_context(nc.semaphore(n))
    for n in paired:
        sems[n] = [es.enter_context(nc.semaphore(n + "0")),
                   es.enter_context(nc.semaphore(n + "1"))]

    class SM:
        pass
    for n, h_ in sems.items():
        setattr(SM, n, h_)

    blk = es.enter_context(nc.Block())

    ph_target = [0]
    vvc = [0]

    def barrier():
        ph_target[0] += 5
        tgt = ph_target[0]
        for eng in ("sync", "scalar", "vector", "tensor", "gpsimd"):
            @getattr(blk, eng)
            def _(e):
                e.drain()
                e.nop().then_inc(SM.PH, 1)
                e.wait_ge(SM.PH, tgt)

    def phase0():
        @blk.sync
        def _(sy):
            sy.dma_start(out=ident_bf[:, :], in_=ident_bf_d[:, :]).then_inc(SM.ld_c, 16)
            sy.dma_start(out=ident_f32[:, :], in_=ident_f32_d[:, :]).then_inc(SM.ld_c, 16)
            sy.dma_start(out=masks_sb[:, :], in_=masks_d[:, :]).then_inc(SM.ld_c, 16)
            sy.dma_start(out=oh32_sb[:, :], in_=oh32_d[:, :]).then_inc(SM.ld_c, 16)
            sy.dma_start(out=ones1r_sb[:, :], in_=ones1r_d[:, :]).then_inc(SM.ld_c, 16)
            sy.dma_start(out=tbl_sb[:, :], in_=tbl_in[:, :]).then_inc(SM.ld_c, 16)
            sy.wait_ge(SM.ld_c, 96)

        @blk.vector
        def _(v):
            v.memset(ones_col[:, :], 1.0)
            v.memset(zero_col[:, :], 0.0)
            v.memset(zero_t[:, :], 0.0)
            v.memset(expb_sb[:, :], EXPB)

        specs = [(zg, sg, nzg_dram, 32, HFF), (zu, su, nzu_dram, 32, HFF),
                 (zd, sd, nzd_dram, GFF, D)]
        with (
            nc.sbuf_tensor("nz_z", [GFF, HFF], BF16) as z_sb,
            nc.sbuf_tensor("nz_s", [GFF, HFF], F32) as s_sb,
            nc.sbuf_tensor("nz_o", [GFF, HFF], BF16) as o_sb,
        ):
            @blk.gpsimd
            def _(g):
                for j, (z_d, s_d, nz_d, P, F) in enumerate(specs):
                    if j >= 1:
                        g.wait_ge(SM.s_nz, j)
                    g.dma_start(out=z_sb[0:P, 0:F], in_=z_d[:, :]).then_inc(SM.ld_z, 16)

            @blk.sync
            def _(sy):
                for j, (z_d, s_d, nz_d, P, F) in enumerate(specs):
                    if j >= 1:
                        sy.wait_ge(SM.s_nz, j)
                    sy.dma_start(out=s_sb[0:P, 0:F], in_=s_d[:, :]).then_inc(SM.ld_s, 16)

            @blk.vector
            def _(v):
                for j, (z_d, s_d, nz_d, P, F) in enumerate(specs):
                    v.wait_ge(SM.ld_z, 16 * (j + 1))
                    v.wait_ge(SM.ld_s, 16 * (j + 1))
                    if j >= 1:
                        v.wait_ge(SM.st_nz, 16 * j)
                    v.scalar_tensor_tensor(o_sb[0:P, 0:F], z_sb[0:P, 0:F], -1.0,
                                           s_sb[0:P, 0:F], ALU.mult, ALU.mult
                                           ).then_inc(SM.s_nz, 1)

            @blk.scalar
            def _(s):
                for j, (z_d, s_d, nz_d, P, F) in enumerate(specs):
                    s.wait_ge(SM.s_nz, j + 1)
                    s.dma_start(out=nz_d[:, :], in_=o_sb[0:P, 0:F]).then_inc(SM.st_nz, 16)
                s.wait_ge(SM.st_nz, 48)
        barrier()

    ln_ctr = {"ld0": 0, "ld1": 0, "sq": 0, "v1": 0, "sqr": 0, "n": 0, "t": 0, "e": 0}

    def emit_ln(src_dram, xnT, tag):
        c = dict(ln_ctr)
        with (
            nc.sbuf_tensor(f"ln_xt_{tag}", [128, 2 * D], F32) as xt,
            nc.sbuf_tensor(f"ln_dump_{tag}", [128, D], BF16) as dump,
            nc.sbuf_tensor(f"ln_xn_{tag}", [128, 2 * D], BF16) as xn,
            nc.sbuf_tensor(f"ln_st_{tag}", [128, 8], F32) as st,
            nc.psum_tensor(f"ln_ps_{tag}", [128, 2048], BF16) as ptr,
        ):
            @blk.sync
            def _(sy):
                for tc in range(8):
                    if tc >= 2:
                        sy.wait_ge(SM.ln_n, c["n"] + tc - 1)
                        sy.wait_ge(SM.ln_sq, c["sq"] + tc - 1)
                    sy.dma_start(out=xt[:, (tc % 2) * D:(tc % 2 + 1) * D],
                                 in_=src_dram[tc * 128:(tc + 1) * 128, :]
                                 ).then_inc(SM.ln_ld[tc % 2], 16)

            @blk.scalar
            def _(s):
                for tc in range(8):
                    xv = xt[:, (tc % 2) * D:(tc % 2 + 1) * D]
                    s.wait_ge(SM.ln_ld[tc % 2], c["ld%d" % (tc % 2)] + 16 * (tc // 2 + 1))
                    s.activation(dump[:, :], xv, AF.Square,
                                 accum_out=st[:, 1:2]).then_inc(SM.ln_sq, 1)
                    s.wait_ge(SM.ln_v1, c["v1"] + tc + 1)
                    s.activation(st[:, 7:8], st[:, 6:7], AF.Sqrt).then_inc(SM.ln_sqr, 1)

            @blk.vector
            def _(v):
                for tc in range(9):
                    if tc < 8:
                        xv = xt[:, (tc % 2) * D:(tc % 2 + 1) * D]
                        xnv = xn[:, (tc % 2) * D:(tc % 2 + 1) * D]
                        v.wait_ge(SM.ln_ld[tc % 2], c["ld%d" % (tc % 2)] + 16 * (tc // 2 + 1))
                        v.reduce_sum(st[:, 0:1], xv, axis=AX.X).then_inc(SM.vv, 1)
                        vvc[0] += 1
                        v.wait_ge(SM.ln_sq, c["sq"] + tc + 1)
                        v.wait_ge(SM.vv, vvc[0])
                        v.tensor_scalar_mul(st[:, 2:3], st[:, 0:1], 1.0 / D).then_inc(SM.vv, 1)
                        vvc[0] += 1
                        v.tensor_scalar_mul(st[:, 3:4], st[:, 1:2], 1.0 / D).then_inc(SM.vv, 1)
                        vvc[0] += 1
                        v.wait_ge(SM.vv, vvc[0])
                        v.tensor_tensor(st[:, 5:6], st[:, 2:3], st[:, 2:3], ALU.mult).then_inc(SM.vv, 1)
                        vvc[0] += 1
                        v.wait_ge(SM.vv, vvc[0])
                        v.tensor_tensor(st[:, 4:5], st[:, 3:4], st[:, 5:6], ALU.subtract).then_inc(SM.vv, 1)
                        vvc[0] += 1
                        v.wait_ge(SM.vv, vvc[0])
                        v.tensor_scalar_add(st[:, 6:7], st[:, 4:5], EPS).then_inc(SM.ln_v1, 1)
                        v.wait_ge(SM.ln_sqr, c["sqr"] + tc + 1)
                        v.reciprocal(st[:, 5:6], st[:, 7:8]).then_inc(SM.vv, 1)
                        vvc[0] += 1
                        if tc >= 2:
                            v.wait_ge(SM.ln_t, c["t"] + 4 * (tc - 1))
                        v.wait_ge(SM.vv, vvc[0])
                        v.tensor_scalar(xnv, xv, st[:, 2:3], st[:, 5:6],
                                        ALU.subtract, ALU.mult).then_inc(SM.ln_n, 1)
                    if tc >= 1:
                        for grp in range(4):
                            gidx = (tc - 1) * 4 + grp
                            v.wait_ge(SM.ln_t, c["t"] + gidx + 1)
                            v.tensor_copy(
                                bass.AP(xnT, (grp * 8) * TOK + (tc - 1) * 128,
                                        [[DC * TOK, 128], [TOK, 8], [1, 128]]),
                                ptr[:, (gidx % 2) * 1024:(gidx % 2 + 1) * 1024].rearrange(
                                    "p (j e) -> p j e", j=8)
                            ).then_inc(SM.ln_e, 1)

            @blk.tensor
            def _(t):
                for tc in range(8):
                    off = (tc % 2) * D
                    t.wait_ge(SM.ln_n, c["n"] + tc + 1)
                    for grp in range(4):
                        gidx = tc * 4 + grp
                        if gidx >= 2:
                            t.wait_ge(SM.ln_e, c["e"] + gidx - 1)
                        for jj in range(8):
                            j = grp * 8 + jj
                            ins = t.transpose(
                                ptr[:, (gidx % 2) * 1024 + jj * 128:(gidx % 2) * 1024 + (jj + 1) * 128],
                                xn[:, off + j * 128:off + (j + 1) * 128],
                                ident_bf[:, :])
                            if jj == 7:
                                ins.then_inc(SM.ln_t, 1)

        ln_ctr["ld0"] += 64
        ln_ctr["ld1"] += 64
        ln_ctr["sq"] += 8
        ln_ctr["v1"] += 8
        ln_ctr["sqr"] += 8
        ln_ctr["n"] += 8
        ln_ctr["t"] += 32
        ln_ctr["e"] += 32

    def phase12():
        es12 = ExitStack()
        xn1T = es12.enter_context(nc.sbuf_tensor("xn1T", [128, DC * TOK], BF16))
        emit_ln(x_in, xn1T, "ln1")
        barrier()

        if PHASES == 1:
            @blk.gpsimd
            def _(g):
                for j in range(8):
                    g.dma_start(out=bass.AP(y_out.tensor, j * 128 * D, [[D, 128], [1, D]]),
                                in_=xn1T[:, j * 4096:(j + 1) * 4096]).then_inc(SM.dbg, 16)
                g.wait_ge(SM.dbg, 128)
            es12.close()
            return

        with (
            nc.sbuf_tensor("wpan", [128, 2 * 4096], BF16) as wpan,
            nc.sbuf_tensor("qkst", [128, 2 * 1024], BF16) as qkst,
            nc.sbuf_tensor("vtst", [128, 2 * 1024], BF16) as vtst,
            nc.psum_tensor("pQ", [128, 2048], F32) as pQ,
            nc.psum_tensor("pVt", [128, 2048], BF16) as pVt,
        ):
            def w_src(oc):
                if oc < 32:
                    return bass.AP(wkv.tensor, oc * 128,
                                   [[8256, 128], [8256 * 128, 32], [1, 128]])
                if oc < 64:
                    return bass.AP(wkv.tensor, 4096 + (oc - 32) * 128,
                                   [[8256, 128], [8256 * 128, 32], [1, 128]])
                return bass.AP(wq.tensor, (oc - 64) * 192,
                               [[6144, 128], [6144 * 128, 32], [1, 128]])

            @blk.gpsimd
            def _(g):
                for oc in range(96):
                    if oc >= 2:
                        g.wait_ge(SM.p2mm, oc - 1)
                    g.dma_start(out=wpan[:, (oc % 2) * 4096:(oc % 2 + 1) * 4096]
                                .rearrange("p (cc e) -> p cc e", cc=32),
                                in_=w_src(oc)).then_inc(SM.ld_w[oc % 2], 16)

            @blk.tensor
            def _(t):
                for oc in range(96):
                    t.wait_ge(SM.ld_w[oc % 2], 16 * (oc // 2 + 1))
                    if oc >= 2:
                        t.wait_ge(SM.p2ev, oc - 1)
                    pb = (oc % 2) * 1024
                    for dcc in range(DC):
                        lhs = wpan[:, (oc % 2) * 4096 + dcc * 128:(oc % 2) * 4096 + (dcc + 1) * 128]
                        for tt in range(2):
                            ins = t.matmul(pQ[:, pb + tt * 512:pb + (tt + 1) * 512],
                                           lhs,
                                           xn1T[:, dcc * TOK + tt * 512:dcc * TOK + (tt + 1) * 512],
                                           start=(dcc == 0), stop=(dcc == DC - 1))
                            if dcc == DC - 1 and tt == 1:
                                ins.then_inc(SM.p2mm, 1)
                    if 33 <= oc <= 64:
                        hh = oc - 33
                        t.wait_ge(SM.p2ev, (32 + hh) + 1)
                        if hh >= 2:
                            t.wait_ge(SM.p2ve, hh - 1)
                        for tc2 in range(8):
                            ins = t.transpose(
                                pVt[:, (hh % 2) * 1024 + tc2 * 128:(hh % 2) * 1024 + (tc2 + 1) * 128],
                                qkst[:, ((32 + hh) % 2) * 1024 + tc2 * 128:
                                     ((32 + hh) % 2) * 1024 + (tc2 + 1) * 128],
                                ident_bf[:, :])
                            if tc2 == 7:
                                ins.then_inc(SM.p2tr, 1)

            @blk.scalar
            def _(s):
                for oc in range(96):
                    s.wait_ge(SM.p2mm, oc + 1)
                    if oc >= 2:
                        prev = oc - 2
                        if prev < 32:
                            s.wait_ge(SM.st_k[prev % 2], 16 * (prev // 2 + 1))
                        elif prev < 64:
                            s.wait_ge(SM.p2tr, prev - 32 + 1)
                        else:
                            s.wait_ge(SM.st_qq[prev % 2], 16 * ((prev - 64) // 2 + 1))
                    s.activation(qkst[:, (oc % 2) * 1024:(oc % 2 + 1) * 1024],
                                 pQ[:, (oc % 2) * 1024:(oc % 2 + 1) * 1024],
                                 AF.Copy).then_inc(SM.p2ev, 1)

            @blk.sync
            def _(sy):
                for oc in range(96):
                    src = qkst[:, (oc % 2) * 1024:(oc % 2 + 1) * 1024]
                    if 32 <= oc < 64:
                        hh = oc - 32
                        sy.wait_ge(SM.p2ve, hh + 1)
                        sy.dma_start(out=bass.AP(kvV_in.tensor, hh * TOK * DH,
                                                 [[DH, 128], [128 * DH, 8], [1, DH]]),
                                     in_=vtst[:, (hh % 2) * 1024:(hh % 2 + 1) * 1024]
                                     .rearrange("p (tc e) -> p tc e", tc=8)
                                     ).then_inc(SM.st_v[hh % 2], 16)
                    if oc < 32:
                        sy.wait_ge(SM.p2ev, oc + 1)
                        sy.dma_start(out=bass.AP(kvK_in.tensor, oc * 8 * 16384,
                                                 [[128, 128], [16384, 8], [1, 128]]),
                                     in_=src.rearrange("p (c e) -> p c e", c=8)
                                     ).then_inc(SM.st_k[oc % 2], 16)
                    elif oc >= 64:
                        sy.wait_ge(SM.p2ev, oc + 1)
                        sy.dma_start(out=qT_dram[(oc - 64) * 128:(oc - 63) * 128, :], in_=src
                                     ).then_inc(SM.st_qq[oc % 2], 16)

            @blk.vector
            def _(v):
                for hh in range(32):
                    v.wait_ge(SM.p2tr, hh + 1)
                    if hh >= 2:
                        v.wait_ge(SM.st_v[hh % 2], 16 * (hh // 2))
                    v.tensor_copy(vtst[:, (hh % 2) * 1024:(hh % 2 + 1) * 1024],
                                  pVt[:, (hh % 2) * 1024:(hh % 2 + 1) * 1024]
                                  ).then_inc(SM.p2ve, 1)

            @blk.gpsimd
            def _(g):
                g.wait_ge(SM.st_k[0], 16 * 16)
                g.wait_ge(SM.st_k[1], 16 * 16)
                g.wait_ge(SM.st_v[0], 16 * 16)
                g.wait_ge(SM.st_v[1], 16 * 16)
                g.collective_compute("AllGather", ALU.bypass,
                                     replica_groups=[list(range(NCORES))],
                                     ins=[kvK_in], outs=[kvK_out]).then_inc(SM.cc, 1)
                g.collective_compute("AllGather", ALU.bypass,
                                     replica_groups=[list(range(NCORES))],
                                     ins=[kvV_in], outs=[kvV_out]).then_inc(SM.cc, 1)

            @blk.sync
            def _(sy):
                sy.wait_ge(SM.st_qq[0], 16 * 16)
                sy.wait_ge(SM.st_qq[1], 16 * 16)
                sy.wait_ge(SM.cc, 2)
        es12.close()
        barrier()

        if PHASES == 2:
            @blk.gpsimd
            def _(g):
                g.dma_start(out=bass.AP(y_out.tensor, 0, [[1024, 4096], [1, 1024]]),
                            in_=bass.AP(qT_dram.tensor, 0, [[1024, 4096], [1, 1024]])
                            ).then_inc(SM.dbg, 16)
                g.wait_ge(SM.dbg, 16)

    def phase3(attnT):
        DIAG = set(range(4)) | set(range(92, 96))
        SLOTS = NKC
        FK = 2 * SLOTS * 128
        with (
            nc.sbuf_tensor("K_sb", [128, FK], BF16) as K_sb,
            nc.sbuf_tensor("V_sb", [128, FK], BF16) as V_sb,
            nc.sbuf_tensor("Q_sb", [128, 2 * TOK], BF16) as Q_sb,
            nc.sbuf_tensor("probs", [128, 4 * 512], BF16) as probs,
            nc.sbuf_tensor("invs", [1, 1024], FP16) as invs,
            nc.sbuf_tensor("invs32", [1, 1024], F32) as invs32,
            nc.sbuf_tensor("bcs", [128, 1024], F32) as bcs,
            nc.psum_tensor("s_ps", [128, 2048], F32) as s_ps,
            nc.psum_tensor("a_ps", [128, 1024], F32) as a_ps,
            nc.psum_tensor("z_ps", [33, 512], F32) as z_ps,
            nc.psum_tensor("b_ps", [128, 512], F32) as b_ps,
        ):
            @blk.gpsimd
            def _(g):
                jobs = ([(qwg, qwg_bf, j) for j in range(32)] +
                        [(qwu, qwu_bf, j) for j in range(32)])
                for n, (qw_d, qw_b, j) in enumerate(jobs):
                    g.wait_ge(SM.pa, SLOTS * (n // 2))
                    g.dma_start(out=qw_b[j * 128:(j + 1) * 128, :],
                                in_=qw_d[j * 128:(j + 1) * 128, :]
                                ).then_inc(SM.pcast, 16)

            @blk.sync
            def _(sy):
                for h in range(H):
                    if h >= 2:
                        sy.wait_ge(SM.pa, SLOTS * (h - 1))
                    hk = (h % 2) * SLOTS * 128
                    for kv_out, kv_loc, sbuf_t in ((kvK_out, kvK_in, K_sb),
                                                   (kvV_out, kvV_in, V_sb)):
                        sy.dma_start(
                            out=bass.AP(sbuf_t, hk, [[FK, 128], [128, 4], [1, 128]]),
                            in_=bass.AP(kv_loc.tensor, (h * 8) * 16384,
                                        [[128, 128], [16384, 4], [1, 128]]),
                        ).then_inc(SM.ld_kv[h % 2], 16)
                        for gb in range(7):
                            sy.dma_start(
                                out=bass.AP(sbuf_t, hk + (4 + gb * 4) * 128,
                                            [[FK, 128], [128, 4], [1, 128]]),
                                in_=bass.AP(kv_out.tensor, (gb * H * 8 + h * 8) * 16384,
                                            [[128, 128], [16384, 4], [1, 128]]),
                            ).then_inc(SM.ld_kv[h % 2], 16)
                        for gb in range(15):
                            if gb < 8:
                                soff = (gb * H * 8 + h * 8) * 16384
                            else:
                                soff = ((15 - gb) * H * 8 + h * 8 + 4) * 16384
                            sy.dma_start(
                                out=bass.AP(sbuf_t, hk + (32 + gb * 4) * 128,
                                            [[FK, 128], [128, 4], [1, 128]]),
                                in_=bass.AP(kv_out.tensor, soff,
                                            [[128, 128], [16384, 4], [1, 128]]),
                            ).then_inc(SM.ld_kv[h % 2], 16)
                        sy.dma_start(
                            out=bass.AP(sbuf_t, hk + 92 * 128, [[FK, 128], [128, 4], [1, 128]]),
                            in_=bass.AP(kv_loc.tensor, (h * 8 + 4) * 16384,
                                        [[128, 128], [16384, 4], [1, 128]]),
                        ).then_inc(SM.ld_kv[h % 2], 16)
                    sy.dma_start(out=Q_sb[:, (h % 2) * TOK:(h % 2 + 1) * TOK],
                                 in_=qT_dram[h * 128:(h + 1) * 128, :]
                                 ).then_inc(SM.ld_kv[h % 2], 16)

            @blk.tensor
            def _(t):
                for h in range(H):
                    hk = (h % 2) * SLOTS * 128
                    hq = (h % 2) * TOK
                    t.wait_ge(SM.ld_kv[h % 2], 16 * 49 * (h // 2 + 1))
                    if h >= 1:
                        t.wait_ge(SM.sat, 2 * h)
                        t.wait_ge(SM.siv, 2 * h)
                    for qb in range(2):
                        t.matmul(bass.AP(a_ps, qb * 512, [[1024, 128], [1, 512]]),
                                 zero_t[:, :], masks_sb[:, 0:512],
                                 start=True, stop=True, skip_group_check=True)
                    t.matmul(z_ps[0:33, :], zero_t[:, 0:33], masks_sb[:, 0:512],
                             start=True, stop=True, skip_group_check=True)

                    def emit_s(m):
                        qb = 0 if m < 32 else 1
                        if m >= 4:
                            t.wait_ge(SM.ex, SLOTS * h + m - 3)
                        t.matmul(s_ps[:, (m % 4) * 512:(m % 4 + 1) * 512],
                                 K_sb[:, hk + m * 128:hk + (m + 1) * 128],
                                 Q_sb[:, hq + qb * 512:hq + (qb + 1) * 512],
                                 start=True, stop=True, skip_group_check=True
                                 ).then_inc(SM.se, 1)

                    emit_s(0)
                    emit_s(1)
                    for m in range(SLOTS):
                        qb = 0 if m < 32 else 1
                        if m + 2 < SLOTS:
                            emit_s(m + 2)
                        t.wait_ge(SM.ep, SLOTS * h + m + 1)
                        t.matmul(bass.AP(z_ps, qb * 32 * 512, [[512, 1], [1, 512]]),
                                 ones_col[:, :],
                                 probs[:, (m % 4) * 512:(m % 4 + 1) * 512],
                                 start=False, stop=False, skip_group_check=True)
                        t.matmul(bass.AP(a_ps, qb * 512, [[1024, 128], [1, 512]]),
                                 V_sb[:, hk + m * 128:hk + (m + 1) * 128],
                                 probs[:, (m % 4) * 512:(m % 4 + 1) * 512],
                                 start=False, stop=False, skip_group_check=True
                                 ).then_inc(SM.pa, 1)
                    for qb in range(2):
                        t.wait_ge(SM.siv, 2 * h + qb + 1)
                        if 2 * h + qb >= 1:
                            t.wait_ge(SM.sbe, 2 * h + qb)
                        t.matmul(b_ps[:, :], ones1r_sb[:, :],
                                 invs[0:1, qb * 512:(qb + 1) * 512],
                                 start=True, stop=True, skip_group_check=True
                                 ).then_inc(SM.p3bc, 1)

            @blk.scalar
            def _(s):
                for h in range(H):
                    for m in range(SLOTS):
                        s.wait_ge(SM.se, SLOTS * h + m + 1)
                        if SLOTS * h + m >= 4:
                            s.wait_ge(SM.pa, SLOTS * h + m - 3)
                        s.activation(probs[:, (m % 4) * 512:(m % 4 + 1) * 512],
                                     s_ps[:, (m % 4) * 512:(m % 4 + 1) * 512],
                                     AF.Exp, bias=expb_sb[:, :], scale=SCALE
                                     ).then_inc(SM.ex, 1)

            @blk.vector
            def _(v):
                for h in range(H):
                    for m in range(SLOTS):
                        v.wait_ge(SM.ex, SLOTS * h + m + 1)
                        pr = probs[:, (m % 4) * 512:(m % 4 + 1) * 512]
                        if m in DIAG:
                            w = m if m < 4 else m - 92
                            v.tensor_tensor(pr, pr, masks_sb[:, w * 512:(w + 1) * 512],
                                            ALU.mult).then_inc(SM.ep, 1)
                        else:
                            v.tensor_scalar_mul(pr, pr, tbl_sb[:, m:m + 1]
                                                ).then_inc(SM.ep, 1)
                    v.wait_ge(SM.pa, SLOTS * (h + 1))
                    for qb in range(2):
                        v.reciprocal(invs32[0:1, qb * 512:(qb + 1) * 512],
                                     z_ps[32 * qb:32 * qb + 1, :]).then_inc(SM.vv, 1)
                        vvc[0] += 1
                        v.wait_ge(SM.vv, vvc[0])
                        v.tensor_copy(invs[0:1, qb * 512:(qb + 1) * 512],
                                      invs32[0:1, qb * 512:(qb + 1) * 512]
                                      ).then_inc(SM.siv, 1)
                    for qb in range(2):
                        v.wait_ge(SM.p3bc, 2 * h + qb + 1)
                        v.tensor_copy(bcs[:, qb * 512:(qb + 1) * 512], b_ps[:, :]
                                      ).then_inc(SM.sbe, 1)
                        v.wait_ge(SM.sbe, 2 * h + qb + 1)
                        v.tensor_tensor(attnT[:, h * TOK + qb * 512:h * TOK + (qb + 1) * 512],
                                        a_ps[:, qb * 512:(qb + 1) * 512],
                                        bcs[:, qb * 512:(qb + 1) * 512],
                                        ALU.mult).then_inc(SM.sat, 1)
        barrier()

    def phase4(attnT):
        with (
            nc.sbuf_tensor("wopan", [128, 2 * 4096], BF16) as wopan,
            nc.sbuf_tensor("wsb", [128, 2 * 1024], F32) as wsb,
            nc.sbuf_tensor("xp4", [128, 2 * 512], F32) as xp,
            nc.sbuf_tensor("x1p", [128, 2 * 512], F32) as x1p,
            nc.psum_tensor("pW", [128, 2048], F32) as pW,
            nc.psum_tensor("pT4", [128, 1024], F32) as pT,
        ):
            @blk.gpsimd
            def _(g):
                for dc in range(DC):
                    if dc >= 2:
                        g.wait_ge(SM.wo_mm, dc - 1)
                    g.dma_start(out=wopan[:, (dc % 2) * 4096:(dc % 2 + 1) * 4096]
                                .rearrange("p (cc e) -> p cc e", cc=32),
                                in_=bass.AP(wo.tensor, dc * 128,
                                            [[D, 128], [D * 128, 32], [1, 128]])
                                ).then_inc(SM.ld_wo[dc % 2], 16)

            @blk.sync
            def _(sy):
                for gidx in range(64):
                    dc, gg = gidx // 2, gidx % 2
                    if gidx >= 2:
                        sy.wait_ge(SM.wo_add, gidx - 1)
                    sy.dma_start(out=xp[:, (gidx % 2) * 512:(gidx % 2 + 1) * 512]
                                 .rearrange("p (tc e) -> p tc e", tc=4),
                                 in_=bass.AP(x_in.tensor, gg * 512 * D + dc * 128,
                                             [[D, 128], [128 * D, 4], [1, 128]])
                                 ).then_inc(SM.ld_xp[gidx % 2], 16)

            @blk.tensor
            def _(t):
                for dc in range(DC):
                    t.wait_ge(SM.ld_wo[dc % 2], 16 * (dc // 2 + 1))
                    if dc >= 2:
                        t.wait_ge(SM.wo_ev, dc - 1)
                    pb = (dc % 2) * 1024
                    for hc in range(32):
                        lhs = wopan[:, (dc % 2) * 4096 + hc * 128:(dc % 2) * 4096 + (hc + 1) * 128]
                        for tt in range(2):
                            ins = t.matmul(pW[:, pb + tt * 512:pb + (tt + 1) * 512],
                                           lhs,
                                           attnT[:, hc * TOK + tt * 512:hc * TOK + (tt + 1) * 512],
                                           start=(hc == 0), stop=(hc == 31))
                            if hc == 31 and tt == 1:
                                ins.then_inc(SM.wo_mm, 1)
                    t.wait_ge(SM.wo_ev, dc + 1)
                    for gg in range(2):
                        gidx = dc * 2 + gg
                        if gidx >= 2:
                            t.wait_ge(SM.wo_add, gidx - 1)
                        for tc2 in range(4):
                            ins = t.transpose(
                                pT[:, (gidx % 2) * 512 + tc2 * 128:(gidx % 2) * 512 + (tc2 + 1) * 128],
                                wsb[:, (dc % 2) * 1024 + (gg * 4 + tc2) * 128:
                                    (dc % 2) * 1024 + (gg * 4 + tc2 + 1) * 128],
                                ident_f32[:, :])
                            if tc2 == 3:
                                ins.then_inc(SM.wo_tr, 1)

            @blk.vector
            def _(v):
                for dc in range(DC):
                    v.wait_ge(SM.wo_mm, dc + 1)
                    if dc >= 2:
                        v.wait_ge(SM.wo_tr, 2 * (dc - 1))
                    v.tensor_copy(wsb[:, (dc % 2) * 1024:(dc % 2 + 1) * 1024],
                                  pW[:, (dc % 2) * 1024:(dc % 2 + 1) * 1024]
                                  ).then_inc(SM.wo_ev, 1)
                    for gg in range(2):
                        gidx = dc * 2 + gg
                        v.wait_ge(SM.wo_tr, gidx + 1)
                        v.wait_ge(SM.ld_xp[gidx % 2], 16 * (gidx // 2 + 1))
                        if gidx >= 2:
                            v.wait_ge(SM.st_x1[gidx % 2], 16 * (gidx // 2))
                        v.tensor_tensor(x1p[:, (gidx % 2) * 512:(gidx % 2 + 1) * 512],
                                        pT[:, (gidx % 2) * 512:(gidx % 2 + 1) * 512],
                                        xp[:, (gidx % 2) * 512:(gidx % 2 + 1) * 512],
                                        ALU.add).then_inc(SM.wo_add, 1)

            @blk.scalar
            def _(s):
                for gidx in range(64):
                    dc, gg = gidx // 2, gidx % 2
                    s.wait_ge(SM.wo_add, gidx + 1)
                    s.dma_start(out=bass.AP(x1_dram.tensor, gg * 512 * D + dc * 128,
                                            [[D, 128], [128 * D, 4], [1, 128]]),
                                in_=x1p[:, (gidx % 2) * 512:(gidx % 2 + 1) * 512]
                                .rearrange("p (tc e) -> p tc e", tc=4)
                                ).then_inc(SM.st_x1[gidx % 2], 16)
                s.wait_ge(SM.st_x1[0], 16 * 32)
                s.wait_ge(SM.st_x1[1], 16 * 32)
        barrier()

    def phase5():
        es5 = ExitStack()
        xn2T = es5.enter_context(nc.sbuf_tensor("xn2T", [128, DC * TOK], BF16))
        Xg_sb = es5.enter_context(nc.sbuf_tensor("Xg_sb", [32, TOK], BF16))
        sd_sb = es5.enter_context(nc.sbuf_tensor("sd_sb", [GFF, D], FP16))
        emit_ln(x1_dram, xn2T, "ln2")
        barrier()

        with (
            nc.sbuf_tensor("xgst", [1, 1024], BF16) as xgst,
            nc.psum_tensor("pxg", [1, 1024], F32) as pxg,
        ):
            @blk.tensor
            def _(t):
                for i in range(64):
                    dcc, tt = i // 2, i % 2
                    if i >= 2:
                        t.wait_ge(SM.xg_ev, i - 1)
                    t.matmul(pxg[0:1, (i % 2) * 512:(i % 2 + 1) * 512],
                             ones_col[:, :],
                             xn2T[:, dcc * TOK + tt * 512:dcc * TOK + (tt + 1) * 512],
                             start=True, stop=True, skip_group_check=True
                             ).then_inc(SM.xg_mm, 1)

            @blk.vector
            def _(v):
                for i in range(64):
                    v.wait_ge(SM.xg_mm, i + 1)
                    if i >= 2:
                        v.wait_ge(SM.st_xg[i % 2], 16 * (i // 2))
                    v.tensor_copy(xgst[0:1, (i % 2) * 512:(i % 2 + 1) * 512],
                                  pxg[0:1, (i % 2) * 512:(i % 2 + 1) * 512]
                                  ).then_inc(SM.xg_ev, 1)

            @blk.sync
            def _(sy):
                for i in range(64):
                    dcc, tt = i // 2, i % 2
                    sy.wait_ge(SM.xg_ev, i + 1)
                    sy.dma_start(out=Xg_sb[dcc:dcc + 1, tt * 512:(tt + 1) * 512],
                                 in_=xgst[0:1, (i % 2) * 512:(i % 2 + 1) * 512]
                                 ).then_inc(SM.st_xg[i % 2], 16)
                sy.wait_ge(SM.st_xg[0], 16 * 32)
                sy.wait_ge(SM.st_xg[1], 16 * 32)
        barrier()

        @blk.gpsimd
        def _(g):
            g.dma_start(out=sd_sb[:, :], in_=sd[:, :]).then_inc(SM.ld_sd, 16)

        with (
            nc.sbuf_tensor("gbuf", [128, 2 * 8192], BF16) as gbuf,
            nc.sbuf_tensor("ubuf", [128, 2 * 8192], BF16) as ubuf,
            nc.sbuf_tensor("sgp", [32, 2 * 512], FP16) as sgp,
            nc.sbuf_tensor("nzs", [32, 8 * 128], BF16) as nzs,
            nc.sbuf_tensor("sgact", [128, 1024], BF16) as sgact,
            nc.sbuf_tensor("hstage", [128, 4 * 1024], BF16) as hstage,
            nc.sbuf_tensor("qwdb", [128, 2 * 4096], BF16) as qwdb,
            nc.sbuf_tensor("ohdb", [GFF, 2 * 128], FP16) as ohdb,
            nc.psum_tensor("pG", [128, 1024], F32) as pG,
            nc.psum_tensor("pU", [128, 1024], F32) as pU,
            nc.psum_tensor("pbc", [128, 1024], F32) as pbc,
            nc.psum_tensor("pbc2", [128, 1024], F32) as pbc2,
        ):
            @blk.gpsimd
            def _(g):
                for q in range(NPAN):
                    if q >= 2:
                        g.wait_ge(SM.g_mm, 4 * (q - 1))
                    g.dma_start(out=sgp[0:32, (q % 2) * 512:(q % 2) * 512 + 256],
                                in_=sg[:, q * 256:(q + 1) * 256]).then_inc(SM.ld_fs[q % 2], 16)
                    g.dma_start(out=sgp[0:32, (q % 2) * 512 + 256:(q % 2 + 1) * 512],
                                in_=su[:, q * 256:(q + 1) * 256]).then_inc(SM.ld_fs[q % 2], 16)
                    for hh in range(2):
                        hcd = 2 * q + hh
                        if hcd >= 2:
                            g.wait_ge(SM.st_wd[hcd % 2], 16 * (hcd // 2))
                        g.dma_start(out=qwdb[:, (hcd % 2) * 4096:(hcd % 2 + 1) * 4096],
                                    in_=qwd[hcd * 128:(hcd + 1) * 128, :]
                                    ).then_inc(SM.ld_qd[hcd % 2], 16)

            @blk.sync
            def _(sy):
                sy.wait_ge(SM.pcast, 16 * 64)
                for q in range(NPAN):
                    if q >= 2:
                        sy.wait_ge(SM.g_mm, 4 * (q - 1))
                    for qw_d, buf in ((qwg_bf, gbuf), (qwu_bf, ubuf)):
                        sy.dma_start(out=buf[:, (q % 2) * 8192:(q % 2 + 1) * 8192]
                                     .rearrange("p (cc e) -> p cc e", cc=32),
                                     in_=bass.AP(qw_d.tensor, q * 256,
                                                 [[HFF, 128], [HFF * 128, 32], [1, 256]])
                                     ).then_inc(SM.ld_ff[q % 2], 16)
                    for mat, nz_d in enumerate([nzg_dram, nzu_dram]):
                        for hc in range(2):
                            slot = (q % 2) * 4 + mat * 2 + hc
                            sy.dma_start(out=nzs[0:32, slot * 128:(slot + 1) * 128],
                                         in_=nz_d[:, (2 * q + hc) * 128:(2 * q + hc + 1) * 128]
                                         ).then_inc(SM.ld_nz[q % 2], 16)
                    for hh in range(2):
                        hcd = 2 * q + hh
                        if hcd >= 2:
                            sy.wait_ge(SM.d_bc, 8 * (hcd - 1))
                        sy.dma_start(out=ohdb[0:GFF, (hcd % 2) * 128:(hcd % 2 + 1) * 128],
                                     in_=bass.AP(ohD_d.tensor, hcd * 128,
                                                 [[GFF * 128, GFF], [1, 128]])
                                     ).then_inc(SM.ld_oh[hcd % 2], 16)

            @blk.tensor
            def _(t):
                nbc = 0
                nbc2 = 0
                for q in range(NPAN + 1):
                    if q < NPAN:
                        t.wait_ge(SM.ld_fs[q % 2], 16 * 2 * (q // 2 + 1))
                        for dcc in range(32):
                            nbc += 1
                            if nbc >= 3:
                                t.wait_ge(SM.g_dq, 2 * (nbc - 2))
                            t.matmul(pbc[:, (nbc % 2) * 512:(nbc % 2 + 1) * 512],
                                     oh32_sb[0:32, dcc * 128:(dcc + 1) * 128],
                                     sgp[0:32, (q % 2) * 512:(q % 2 + 1) * 512],
                                     start=True, stop=True, skip_group_check=True
                                     ).then_inc(SM.g_bc, 1)
                        t.wait_ge(SM.ld_sd, 16)
                        for hh in range(2):
                            hcd = 2 * q + hh
                            t.wait_ge(SM.ld_oh[hcd % 2], 16 * (hcd // 2 + 1))
                            for dt in range(8):
                                nbc2 += 1
                                if nbc2 >= 3:
                                    t.wait_ge(SM.d_dq, nbc2 - 2)
                                t.matmul(pbc2[:, (nbc2 % 2) * 512:(nbc2 % 2 + 1) * 512],
                                         ohdb[0:GFF, (hcd % 2) * 128:(hcd % 2 + 1) * 128],
                                         sd_sb[0:GFF, dt * 512:(dt + 1) * 512],
                                         start=True, stop=True, skip_group_check=True
                                         ).then_inc(SM.d_bc, 1)
                    if q >= 1:
                        qq = q - 1
                        t.wait_ge(SM.g_dq, 64 * (qq + 1))
                        t.wait_ge(SM.ld_nz[qq % 2], 16 * 4 * (qq // 2 + 1))
                        for hc in range(2):
                            for mat, (buf, ps) in enumerate([(gbuf, pG), (ubuf, pU)]):
                                if 4 * qq + 2 * hc + mat >= 2:
                                    if mat == 0:
                                        t.wait_ge(SM.s_silu, 2 * qq + hc)
                                    else:
                                        t.wait_ge(SM.s_hm, 2 * qq + hc)
                                for tt in range(2):
                                    for dcc in range(32):
                                        t.matmul(ps[:, tt * 512:(tt + 1) * 512],
                                                 buf[:, (qq % 2) * 8192 + dcc * 256 + hc * 128:
                                                     (qq % 2) * 8192 + dcc * 256 + hc * 128 + 128],
                                                 xn2T[:, dcc * TOK + tt * 512:dcc * TOK + (tt + 1) * 512],
                                                 start=(dcc == 0), stop=False,
                                                 skip_group_check=True)
                                    slot = (qq % 2) * 4 + mat * 2 + hc
                                    ins = t.matmul(ps[:, tt * 512:(tt + 1) * 512],
                                                   nzs[0:32, slot * 128:(slot + 1) * 128],
                                                   Xg_sb[0:32, tt * 512:(tt + 1) * 512],
                                                   start=False, stop=True,
                                                   skip_group_check=True)
                                    if tt == 1:
                                        ins.then_inc(SM.g_mm, 1)

            @blk.scalar
            def _(s):
                for q in range(NPAN + 2):
                    if 1 <= q <= NPAN:
                        for hh in range(2):
                            hcd = 2 * (q - 1) + hh
                            s.wait_ge(SM.d_dq, 8 * (hcd + 1))
                            s.dma_start(out=bass.AP(wd_dram.tensor, hcd * 128 * 128,
                                                    [[128, 128], [HFF * 128, 32], [1, 128]]),
                                        in_=qwdb[:, (hcd % 2) * 4096:(hcd % 2 + 1) * 4096]
                                        .rearrange("p (cc e) -> p cc e", cc=32)
                                        ).then_inc(SM.st_wd[hcd % 2], 16)
                    if q >= 2:
                        for hc in range(2):
                            hcg = 2 * (q - 2) + hc
                            s.wait_ge(SM.s_hm, hcg + 1)
                            s.dma_start(out=hact_dram[hcg * 128:(hcg + 1) * 128, :],
                                        in_=hstage[:, (hcg % 4) * 1024:(hcg % 4 + 1) * 1024]
                                        ).then_inc(SM.st_ha[hcg % 2], 16)
                    if 1 <= q <= NPAN:
                        qq = q - 1
                        for hc in range(2):
                            hcg = 2 * qq + hc
                            s.wait_ge(SM.g_mm, 4 * qq + 2 * hc + 1)
                            if hcg >= 1:
                                s.wait_ge(SM.s_hm, hcg)
                            s.activation(sgact[:, :], pG[:, :], AF.Silu).then_inc(SM.s_silu, 1)
                s.wait_ge(SM.st_ha[0], 16 * 43)
                s.wait_ge(SM.st_ha[1], 16 * 43)
                s.wait_ge(SM.st_wd[0], 16 * 43)
                s.wait_ge(SM.st_wd[1], 16 * 43)

            @blk.vector
            def _(v):
                ndq = 0
                ndq2 = 0
                for q in range(NPAN + 1):
                    if q < NPAN:
                        v.wait_ge(SM.ld_ff[q % 2], 16 * 2 * (q // 2 + 1))
                        for dcc in range(32):
                            ndq += 1
                            v.wait_ge(SM.g_bc, ndq)
                            for mat, buf in enumerate([gbuf, ubuf]):
                                v.tensor_tensor(
                                    buf[:, (q % 2) * 8192 + dcc * 256:(q % 2) * 8192 + (dcc + 1) * 256],
                                    buf[:, (q % 2) * 8192 + dcc * 256:(q % 2) * 8192 + (dcc + 1) * 256],
                                    pbc[:, (ndq % 2) * 512 + mat * 256:(ndq % 2) * 512 + (mat + 1) * 256],
                                    ALU.mult).then_inc(SM.g_dq, 1)
                    if q < NPAN:
                        for hh in range(2):
                            hcd = 2 * q + hh
                            v.wait_ge(SM.ld_qd[hcd % 2], 16 * (hcd // 2 + 1))
                            for dt in range(8):
                                ndq2 += 1
                                v.wait_ge(SM.d_bc, ndq2)
                                v.tensor_tensor(
                                    qwdb[:, (hcd % 2) * 4096 + dt * 512:(hcd % 2) * 4096 + (dt + 1) * 512],
                                    qwdb[:, (hcd % 2) * 4096 + dt * 512:(hcd % 2) * 4096 + (dt + 1) * 512],
                                    pbc2[:, (ndq2 % 2) * 512:(ndq2 % 2 + 1) * 512],
                                    ALU.mult).then_inc(SM.d_dq, 1)
                    if q >= 1:
                        qq = q - 1
                        for hc in range(2):
                            hcg = 2 * qq + hc
                            v.wait_ge(SM.s_silu, hcg + 1)
                            v.wait_ge(SM.g_mm, 4 * qq + 2 * hc + 2)
                            if hcg >= 2:
                                v.wait_ge(SM.st_ha[hcg % 2], 16 * (hcg // 2))
                            v.tensor_tensor(hstage[:, (hcg % 4) * 1024:(hcg % 4 + 1) * 1024],
                                            pU[:, :], sgact[:, :],
                                            ALU.mult).then_inc(SM.s_hm, 1)
        es5.close()
        barrier()

        if PHASES == 5:
            @blk.gpsimd
            def _(g):
                g.dma_start(out=bass.AP(y_out.tensor, 0, [[1024, 4096], [1, 1024]]),
                            in_=bass.AP(hact_dram.tensor, 0, [[1024, 4096], [1, 1024]])
                            ).then_inc(SM.dbg, 16)
                g.wait_ge(SM.dbg, 16)

    def phase6():
        with (
            nc.sbuf_tensor("hact_sb", [128, GFF * 512], BF16) as hact_sb,
            nc.sbuf_tensor("wdc", [128, 2 * GFF * 128], BF16) as wdc,
            nc.sbuf_tensor("nzd6", [GFF, 2 * 128], BF16) as nzd6,
            nc.sbuf_tensor("Ag_sb", [GFF, 512], BF16) as Ag_sb,
            nc.sbuf_tensor("agst", [1, 1024], BF16) as agst,
            nc.sbuf_tensor("dsb", [128, 2 * 512], F32) as dsb,
            nc.sbuf_tensor("xp6", [128, 2 * 512], F32) as xp6,
            nc.sbuf_tensor("yp6", [128, 2 * 512], F32) as yp6,
            nc.psum_tensor("pD", [128, 1024], F32) as pD,
            nc.psum_tensor("pT6", [128, 1024], F32) as pT6,
            nc.psum_tensor("pag", [1, 1024], F32) as pag,
        ):
            @blk.sync
            def _(sy):
                for t_ in range(2):
                    if t_ == 1:
                        sy.wait_ge(SM.d6_mm, 32)
                    sy.dma_start(out=hact_sb[:].rearrange("p (gg e) -> p gg e", gg=GFF),
                                 in_=bass.AP(hact_dram.tensor, t_ * 512,
                                             [[TOK, 128], [128 * TOK, GFF], [1, 512]])
                                 ).then_inc(SM.ld_ha, 16)
                    for dc in range(DC):
                        i = t_ * 32 + dc
                        if i >= 2:
                            sy.wait_ge(SM.d6_mm, i - 1)
                        sy.dma_start(out=wdc[:, (i % 2) * GFF * 128:(i % 2 + 1) * GFF * 128],
                                     in_=bass.AP(wd_dram.tensor, dc * HFF * 128,
                                                 [[128, 128], [128 * 128, GFF], [1, 128]])
                                     ).then_inc(SM.ld_wd[i % 2], 16)
                        if i >= 2:
                            sy.wait_ge(SM.d6_add, i - 1)
                        sy.dma_start(out=xp6[:, (i % 2) * 512:(i % 2 + 1) * 512]
                                     .rearrange("p (tc e) -> p tc e", tc=4),
                                     in_=bass.AP(x1_dram.tensor, t_ * 512 * D + dc * 128,
                                                 [[D, 128], [128 * D, 4], [1, 128]])
                                     ).then_inc(SM.ld_x1[i % 2], 16)

            @blk.scalar
            def _(s):
                for t_ in range(2):
                    if t_ == 1:
                        s.wait_ge(SM.d6_mm, 32)
                    for gg in range(GFF):
                        i2 = t_ * GFF + gg
                        s.wait_ge(SM.ag_ev, i2 + 1)
                        s.dma_start(out=Ag_sb[gg:gg + 1, :],
                                    in_=agst[0:1, (i2 % 2) * 512:(i2 % 2) * 512 + 512]
                                    ).then_inc(SM.st_ag[i2 % 2], 16)
                    def emit_out(i_):
                        dc_ = i_ - t_ * 32
                        s.wait_ge(SM.d6_add, i_ + 1)
                        s.dma_start(out=bass.AP(y_out.tensor, t_ * 512 * D + dc_ * 128,
                                                [[D, 128], [128 * D, 4], [1, 128]]),
                                    in_=yp6[:, (i_ % 2) * 512:(i_ % 2 + 1) * 512]
                                    .rearrange("p (tc e) -> p tc e", tc=4)
                                    ).then_inc(SM.st_out[i_ % 2], 16)

                    for dc in range(DC):
                        i = t_ * 32 + dc
                        if i >= 2:
                            s.wait_ge(SM.d6_mm, i - 1)
                        s.dma_start(out=nzd6[0:GFF, (i % 2) * 128:(i % 2 + 1) * 128],
                                    in_=nzd_dram[:, dc * 128:(dc + 1) * 128]
                                    ).then_inc(SM.ld_nzd[i % 2], 16)
                        if dc >= 2:
                            emit_out(i - 2)
                    emit_out(t_ * 32 + 30)
                    emit_out(t_ * 32 + 31)
                s.wait_ge(SM.st_out[0], 16 * 32)
                s.wait_ge(SM.st_out[1], 16 * 32)

            @blk.tensor
            def _(t):
                def emit_tr(ip):
                    t.wait_ge(SM.d6_ev, ip + 1)
                    if ip >= 2:
                        t.wait_ge(SM.d6_add, ip - 1)
                    for tc2 in range(4):
                        ins = t.transpose(
                            pT6[:, (ip % 2) * 512 + tc2 * 128:(ip % 2) * 512 + (tc2 + 1) * 128],
                            dsb[:, (ip % 2) * 512 + tc2 * 128:(ip % 2) * 512 + (tc2 + 1) * 128],
                            ident_f32[:, :])
                        if tc2 == 3:
                            ins.then_inc(SM.d6_tr, 1)

                for t_ in range(2):
                    t.wait_ge(SM.ld_ha, 16 * (t_ + 1))
                    for gg in range(GFF):
                        i2 = t_ * GFF + gg
                        if i2 >= 2:
                            t.wait_ge(SM.ag_ev, i2 - 1)
                        t.matmul(pag[0:1, (i2 % 2) * 512:(i2 % 2) * 512 + 512],
                                 ones_col[:, :],
                                 hact_sb[:, gg * 512:(gg + 1) * 512],
                                 start=True, stop=True, skip_group_check=True
                                 ).then_inc(SM.ag_mm, 1)
                    t.wait_ge(SM.st_ag[0], 16 * 43 * (t_ + 1))
                    t.wait_ge(SM.st_ag[1], 16 * 43 * (t_ + 1))
                    for dc in range(DC):
                        i = t_ * 32 + dc
                        t.wait_ge(SM.ld_wd[i % 2], 16 * (i // 2 + 1))
                        t.wait_ge(SM.ld_nzd[i % 2], 16 * (i // 2 + 1))
                        if i >= 2:
                            t.wait_ge(SM.d6_ev, i - 1)
                        for gg in range(GFF):
                            t.matmul(pD[:, (i % 2) * 512:(i % 2 + 1) * 512],
                                     wdc[:, (i % 2) * GFF * 128 + gg * 128:
                                         (i % 2) * GFF * 128 + (gg + 1) * 128],
                                     hact_sb[:, gg * 512:(gg + 1) * 512],
                                     start=(gg == 0), stop=False, skip_group_check=True)
                        t.matmul(pD[:, (i % 2) * 512:(i % 2 + 1) * 512],
                                 nzd6[0:GFF, (i % 2) * 128:(i % 2 + 1) * 128],
                                 Ag_sb[0:GFF, :],
                                 start=False, stop=True, skip_group_check=True
                                 ).then_inc(SM.d6_mm, 1)
                        if dc >= 1:
                            emit_tr(i - 1)
                    emit_tr(t_ * 32 + 31)

            @blk.vector
            def _(v):
                for t_ in range(2):
                    for gg in range(GFF):
                        i2 = t_ * GFF + gg
                        v.wait_ge(SM.ag_mm, i2 + 1)
                        if i2 >= 2:
                            v.wait_ge(SM.st_ag[i2 % 2], 16 * (i2 // 2))
                        v.tensor_copy(agst[0:1, (i2 % 2) * 512:(i2 % 2) * 512 + 512],
                                      pag[0:1, (i2 % 2) * 512:(i2 % 2) * 512 + 512]
                                      ).then_inc(SM.ag_ev, 1)
                    for dc in range(DC):
                        i = t_ * 32 + dc
                        v.wait_ge(SM.d6_mm, i + 1)
                        if i >= 2:
                            v.wait_ge(SM.d6_tr, i - 1)
                        v.tensor_copy(dsb[:, (i % 2) * 512:(i % 2 + 1) * 512],
                                      pD[:, (i % 2) * 512:(i % 2 + 1) * 512]
                                      ).then_inc(SM.d6_ev, 1)
                        v.wait_ge(SM.d6_tr, i + 1)
                        v.wait_ge(SM.ld_x1[i % 2], 16 * (i // 2 + 1))
                        if i >= 2:
                            v.wait_ge(SM.st_out[i % 2], 16 * (i // 2))
                        v.tensor_tensor(yp6[:, (i % 2) * 512:(i % 2 + 1) * 512],
                                        pT6[:, (i % 2) * 512:(i % 2 + 1) * 512],
                                        xp6[:, (i % 2) * 512:(i % 2 + 1) * 512],
                                        ALU.add).then_inc(SM.d6_add, 1)

    phase0()
    phase12()
    if PHASES >= 3:
        es34 = ExitStack()
        attnT = es34.enter_context(nc.sbuf_tensor("attnT", [128, H * TOK], BF16))
        phase3(attnT)
        if PHASES == 3:
            @blk.gpsimd
            def _(g):
                for j in range(8):
                    g.dma_start(out=bass.AP(y_out.tensor, j * 128 * D, [[D, 128], [1, D]]),
                                in_=attnT[:, j * 4096:(j + 1) * 4096]).then_inc(SM.dbg, 16)
                g.wait_ge(SM.dbg, 128)
    if PHASES >= 4:
        phase4(attnT)
        if PHASES == 4:
            @blk.gpsimd
            def _(g):
                g.dma_start(out=y_out[:, :], in_=x1_dram[:, :]).then_inc(SM.dbg, 16)
                g.wait_ge(SM.dbg, 16)
    if PHASES >= 3:
        es34.close()
    if PHASES >= 5:
        phase5()
    if PHASES >= 6:
        phase6()
        barrier()

    es.close()
    return nc


def kernel(**inputs):
    x = np.ascontiguousarray(np.asarray(inputs["L0_attn_hidden_states"])[0], dtype=np.float32)
    common = {
        "wq": np.ascontiguousarray(inputs["L0_attn_wq"], dtype=np.float32),
        "wkv": np.ascontiguousarray(inputs["L0_attn_wkv"], dtype=np.float32),
        "wo": np.ascontiguousarray(inputs["L0_attn_wo"], dtype=np.float32),
        "qwg": np.ascontiguousarray(inputs["L0_ffn_qw_gate"], dtype=np.int32),
        "qwu": np.ascontiguousarray(inputs["L0_ffn_qw_up"], dtype=np.int32),
        "qwd": np.ascontiguousarray(inputs["L0_ffn_qw_down"], dtype=np.int32),
        "zg": np.ascontiguousarray(inputs["L0_ffn_z_gate"], dtype=np.int32),
        "zu": np.ascontiguousarray(inputs["L0_ffn_z_up"], dtype=np.int32),
        "zd": np.ascontiguousarray(inputs["L0_ffn_z_down"], dtype=np.int32),
        "sg": np.ascontiguousarray(inputs["L0_ffn_s_gate"], dtype=np.float32),
        "su": np.ascontiguousarray(inputs["L0_ffn_s_up"], dtype=np.float32),
        "sd": np.ascontiguousarray(inputs["L0_ffn_s_down"], dtype=np.float32),
    }
    nc = build_nc()
    in_maps = []
    for c in range(NCORES):
        b0, b1 = c, 15 - c
        xs = np.concatenate([x[b0 * BS:(b0 + 1) * BS], x[b1 * BS:(b1 + 1) * BS]], axis=0)
        m = dict(common)
        m["x"] = np.ascontiguousarray(xs)
        m["tbl"] = make_tables(c)
        in_maps.append(m)
    res = run_bass_kernel_spmd(nc, in_maps, core_ids=list(range(NCORES)))
    out = np.empty((1, S, D), dtype=np.float32)
    for c in range(NCORES):
        b0, b1 = c, 15 - c
        yc = np.asarray(res.results[c]["y"])
        out[0, b0 * BS:(b0 + 1) * BS] = yc[:BS]
        out[0, b1 * BS:(b1 + 1) * BS] = yc[BS:]
    return out


# version 09
# speedup vs baseline: 1.1469x; 1.0796x over previous
import math
from contextlib import ExitStack

import numpy as np
import ml_dtypes

import concourse.bass as bass
import concourse.mybir as mybir
from concourse.bass_utils import run_bass_kernel_spmd

F32 = mybir.dt.float32
BF16 = mybir.dt.bfloat16
FP16 = mybir.dt.float16
I32 = mybir.dt.int32
AF = mybir.ActivationFunctionType
ALU = mybir.AluOpType
AX = mybir.AxisListType

D = 4096
S = 8192
H = 32
DH = 128
HFF = 11008
BS = 512
NCORES = 8
TOK = 1024
DC = 32
GFF = 86
NKC = 96
NPAN = 43
SCALE = 1.0 / math.sqrt(192.0)
EXPB = -2.0
EPS = 1e-5

PHASES = 6


def make_tables(core):
    b0, b1 = core, 15 - core
    pb = np.zeros((128, NKC), dtype=np.float32)
    for m in range(NKC):
        if m < 4 or m >= 92:
            real = True
        elif m < 32:
            real = (m - 4) < 4 * b0
        else:
            real = (m - 32) < 4 * b1
        pb[:, m] = 1.0 if real else 0.0
    return pb


def make_consts():
    ident_bf = np.eye(128, dtype=ml_dtypes.bfloat16)
    ident_f32 = np.eye(128, dtype=np.float32)
    masks = np.zeros((128, 4 * 512), dtype=ml_dtypes.bfloat16)
    kr = np.arange(128)[:, None]
    q = np.arange(512)[None, :]
    for w in range(4):
        masks[:, w * 512:(w + 1) * 512] = (q >= w * 128 + kr).astype(ml_dtypes.bfloat16)
    oh32 = np.zeros((32, 32 * 128), dtype=np.float16)
    for g in range(32):
        oh32[g, g * 128:(g + 1) * 128] = 1.0
    ohD = np.zeros((GFF, GFF * 128), dtype=np.float16)
    for g in range(GFF):
        ohD[g, g * 128:(g + 1) * 128] = 1.0
    ones1r = np.ones((1, 128), dtype=np.float16)
    return ident_bf, ident_f32, masks, oh32, ohD, ones1r


def build_nc():
    nc = bass.Bass("TRN2", target_bir_lowering=False)

    x_in = nc.dram_tensor("x", [TOK, D], F32, kind="ExternalInput").ap()
    wq = nc.dram_tensor("wq", [D, 6144], F32, kind="ExternalInput").ap()
    wkv = nc.dram_tensor("wkv", [D, 8256], F32, kind="ExternalInput").ap()
    wo = nc.dram_tensor("wo", [D, D], F32, kind="ExternalInput").ap()
    qwg = nc.dram_tensor("qwg", [D, HFF], I32, kind="ExternalInput").ap()
    qwu = nc.dram_tensor("qwu", [D, HFF], I32, kind="ExternalInput").ap()
    qwd = nc.dram_tensor("qwd", [HFF, D], I32, kind="ExternalInput").ap()
    zg = nc.dram_tensor("zg", [32, HFF], I32, kind="ExternalInput").ap()
    zu = nc.dram_tensor("zu", [32, HFF], I32, kind="ExternalInput").ap()
    zd = nc.dram_tensor("zd", [GFF, D], I32, kind="ExternalInput").ap()
    sg = nc.dram_tensor("sg", [32, HFF], F32, kind="ExternalInput").ap()
    su = nc.dram_tensor("su", [32, HFF], F32, kind="ExternalInput").ap()
    sd = nc.dram_tensor("sd", [GFF, D], F32, kind="ExternalInput").ap()
    tbl_in = nc.dram_tensor("tbl", [128, NKC], F32, kind="ExternalInput").ap()
    y_out = nc.dram_tensor("y", [TOK, D], F32, kind="ExternalOutput").ap()

    kvK_in = nc.dram_tensor("kvK_in", [H * 8 * 128, 128], BF16).ap()
    kvV_in = nc.dram_tensor("kvV_in", [H * TOK, DH], BF16).ap()
    kvK_out = nc.dram_tensor("kvK_out", [NCORES * H * 8 * 128, 128], BF16, addr_space="Shared").ap()
    kvV_out = nc.dram_tensor("kvV_out", [NCORES * H * TOK, DH], BF16, addr_space="Shared").ap()
    qT_dram = nc.dram_tensor("qT_dram", [H * DH, TOK], BF16).ap()
    x1_dram = nc.dram_tensor("x1_dram", [TOK, D], F32).ap()
    hact_dram = nc.dram_tensor("hact_dram", [HFF, TOK], BF16).ap()
    wd_dram = nc.dram_tensor("wd_dram", [DC, HFF * 128], BF16).ap()
    nzg_dram = nc.dram_tensor("nzg_dram", [32, HFF], BF16).ap()
    nzu_dram = nc.dram_tensor("nzu_dram", [32, HFF], BF16).ap()
    nzd_dram = nc.dram_tensor("nzd_dram", [GFF, D], BF16).ap()
    qwg_bf = nc.dram_tensor("qwg_bf", [D, HFF], BF16).ap()
    qwu_bf = nc.dram_tensor("qwu_bf", [D, HFF], BF16).ap()

    ident_bf_np, ident_f32_np, masks_np, oh32_np, ohD_np, ones1r_np = make_consts()
    ident_bf_d = nc.inline_tensor(ident_bf_np, "c_ident_bf").ap()
    ident_f32_d = nc.inline_tensor(ident_f32_np, "c_ident_f32").ap()
    masks_d = nc.inline_tensor(masks_np, "c_masks").ap()
    oh32_d = nc.inline_tensor(oh32_np, "c_oh32").ap()
    ohD_d = nc.inline_tensor(ohD_np, "c_ohD").ap()
    ones1r_d = nc.inline_tensor(ones1r_np, "c_ones1r").ap()

    es = ExitStack()

    ident_bf = nc.alloc_sbuf_tensor("ident_bf_sb", [128, 128], BF16)
    ident_f32 = nc.alloc_sbuf_tensor("ident_f32_sb", [128, 128], F32)
    masks_sb = nc.alloc_sbuf_tensor("masks_sb", [128, 2048], BF16)
    oh32_sb = nc.alloc_sbuf_tensor("oh32_sb", [32, 4096], FP16)
    ones1r_sb = nc.alloc_sbuf_tensor("ones1r_sb", [1, 128], FP16)
    tbl_sb = nc.alloc_sbuf_tensor("tbl_sb", [128, NKC], F32)
    ones_col = nc.alloc_sbuf_tensor("ones_col", [128, 1], BF16)
    zero_col = nc.alloc_sbuf_tensor("zero_col", [128, 1], BF16)
    zero_t = nc.alloc_sbuf_tensor("zero_t", [128, 128], BF16)
    expb_sb = nc.alloc_sbuf_tensor("expb_sb", [128, 1], F32)

    single = [
        "PH", "ld_c", "ld_z", "ld_s", "s_nz", "st_nz",
        "ln_sq", "ln_v1", "ln_sqr", "ln_n", "ln_t", "ln_e",
        "p2mm", "p2ev", "p2tr", "p2ve", "cc",
        "se", "ex", "ep", "pa", "siv", "p3bc", "sbe", "sat",
        "wo_mm", "wo_ev", "wo_tr", "wo_add",
        "xg_mm", "xg_ev",
        "g_bc", "g_dq", "g_mm", "s_silu", "s_hm",
        "d_bc", "d_dq", "ld_sd",
        "ag_mm", "ag_ev", "ld_ha",
        "d6_mm", "d6_ev", "d6_tr", "d6_add",
        "vv", "pcast", "dbg",
    ]
    paired = [
        "ln_ld", "ld_w", "st_k", "st_qq", "st_v", "ld_kv",
        "ld_wo", "ld_xp", "st_x1", "st_xg",
        "ld_ff", "ld_nz", "ld_qd", "ld_oh", "st_ha", "st_wd",
        "ld_wd", "ld_nzd", "ld_x1", "st_ag", "st_out",
    ]
    sems = {}
    for n in single:
        sems[n] = es.enter_context(nc.semaphore(n))
    for n in paired:
        sems[n] = [es.enter_context(nc.semaphore(n + "0")),
                   es.enter_context(nc.semaphore(n + "1"))]

    class SM:
        pass
    for n, h_ in sems.items():
        setattr(SM, n, h_)

    blk = es.enter_context(nc.Block())

    ph_target = [0]
    vvc = [0]

    def barrier():
        ph_target[0] += 5
        tgt = ph_target[0]
        for eng in ("sync", "scalar", "vector", "tensor", "gpsimd"):
            @getattr(blk, eng)
            def _(e):
                e.drain()
                e.nop().then_inc(SM.PH, 1)
                e.wait_ge(SM.PH, tgt)

    def phase0():
        @blk.sync
        def _(sy):
            sy.dma_start(out=ident_bf[:, :], in_=ident_bf_d[:, :]).then_inc(SM.ld_c, 16)
            sy.dma_start(out=ident_f32[:, :], in_=ident_f32_d[:, :]).then_inc(SM.ld_c, 16)
            sy.dma_start(out=masks_sb[:, :], in_=masks_d[:, :]).then_inc(SM.ld_c, 16)
            sy.dma_start(out=oh32_sb[:, :], in_=oh32_d[:, :]).then_inc(SM.ld_c, 16)
            sy.dma_start(out=ones1r_sb[:, :], in_=ones1r_d[:, :]).then_inc(SM.ld_c, 16)
            sy.dma_start(out=tbl_sb[:, :], in_=tbl_in[:, :]).then_inc(SM.ld_c, 16)
            sy.wait_ge(SM.ld_c, 96)

        @blk.vector
        def _(v):
            v.memset(ones_col[:, :], 1.0)
            v.memset(zero_col[:, :], 0.0)
            v.memset(zero_t[:, :], 0.0)
            v.memset(expb_sb[:, :], EXPB)

        specs = [(zg, sg, nzg_dram, 32, HFF), (zu, su, nzu_dram, 32, HFF),
                 (zd, sd, nzd_dram, GFF, D)]
        with (
            nc.sbuf_tensor("nz_z", [GFF, HFF], BF16) as z_sb,
            nc.sbuf_tensor("nz_s", [GFF, HFF], F32) as s_sb,
            nc.sbuf_tensor("nz_o", [GFF, HFF], BF16) as o_sb,
        ):
            @blk.gpsimd
            def _(g):
                for j, (z_d, s_d, nz_d, P, F) in enumerate(specs):
                    if j >= 1:
                        g.wait_ge(SM.s_nz, j)
                    g.dma_start(out=z_sb[0:P, 0:F], in_=z_d[:, :]).then_inc(SM.ld_z, 16)

            @blk.sync
            def _(sy):
                for j, (z_d, s_d, nz_d, P, F) in enumerate(specs):
                    if j >= 1:
                        sy.wait_ge(SM.s_nz, j)
                    sy.dma_start(out=s_sb[0:P, 0:F], in_=s_d[:, :]).then_inc(SM.ld_s, 16)

            @blk.vector
            def _(v):
                for j, (z_d, s_d, nz_d, P, F) in enumerate(specs):
                    v.wait_ge(SM.ld_z, 16 * (j + 1))
                    v.wait_ge(SM.ld_s, 16 * (j + 1))
                    if j >= 1:
                        v.wait_ge(SM.st_nz, 16 * j)
                    v.scalar_tensor_tensor(o_sb[0:P, 0:F], z_sb[0:P, 0:F], -1.0,
                                           s_sb[0:P, 0:F], ALU.mult, ALU.mult
                                           ).then_inc(SM.s_nz, 1)

            @blk.scalar
            def _(s):
                for j, (z_d, s_d, nz_d, P, F) in enumerate(specs):
                    s.wait_ge(SM.s_nz, j + 1)
                    s.dma_start(out=nz_d[:, :], in_=o_sb[0:P, 0:F]).then_inc(SM.st_nz, 16)
                s.wait_ge(SM.st_nz, 48)
        barrier()

    ln_ctr = {"ld0": 0, "ld1": 0, "sq": 0, "v1": 0, "sqr": 0, "n": 0, "t": 0, "e": 0}

    def emit_ln(src_dram, xnT, tag):
        c = dict(ln_ctr)
        with (
            nc.sbuf_tensor(f"ln_xt_{tag}", [128, 2 * D], F32) as xt,
            nc.sbuf_tensor(f"ln_dump_{tag}", [128, D], BF16) as dump,
            nc.sbuf_tensor(f"ln_xn_{tag}", [128, 2 * D], BF16) as xn,
            nc.sbuf_tensor(f"ln_st_{tag}", [128, 8], F32) as st,
            nc.psum_tensor(f"ln_ps_{tag}", [128, 2048], BF16) as ptr,
        ):
            @blk.sync
            def _(sy):
                for tc in range(8):
                    if tc >= 2:
                        sy.wait_ge(SM.ln_n, c["n"] + tc - 1)
                        sy.wait_ge(SM.ln_sq, c["sq"] + tc - 1)
                    sy.dma_start(out=xt[:, (tc % 2) * D:(tc % 2 + 1) * D],
                                 in_=src_dram[tc * 128:(tc + 1) * 128, :]
                                 ).then_inc(SM.ln_ld[tc % 2], 16)

            @blk.scalar
            def _(s):
                for tc in range(8):
                    xv = xt[:, (tc % 2) * D:(tc % 2 + 1) * D]
                    s.wait_ge(SM.ln_ld[tc % 2], c["ld%d" % (tc % 2)] + 16 * (tc // 2 + 1))
                    s.activation(dump[:, :], xv, AF.Square,
                                 accum_out=st[:, 1:2]).then_inc(SM.ln_sq, 1)
                    s.wait_ge(SM.ln_v1, c["v1"] + tc + 1)
                    s.activation(st[:, 7:8], st[:, 6:7], AF.Sqrt).then_inc(SM.ln_sqr, 1)

            @blk.vector
            def _(v):
                for tc in range(9):
                    if tc < 8:
                        xv = xt[:, (tc % 2) * D:(tc % 2 + 1) * D]
                        xnv = xn[:, (tc % 2) * D:(tc % 2 + 1) * D]
                        v.wait_ge(SM.ln_ld[tc % 2], c["ld%d" % (tc % 2)] + 16 * (tc // 2 + 1))
                        v.reduce_sum(st[:, 0:1], xv, axis=AX.X).then_inc(SM.vv, 1)
                        vvc[0] += 1
                        v.wait_ge(SM.ln_sq, c["sq"] + tc + 1)
                        v.wait_ge(SM.vv, vvc[0])
                        v.tensor_scalar_mul(st[:, 2:3], st[:, 0:1], 1.0 / D).then_inc(SM.vv, 1)
                        vvc[0] += 1
                        v.tensor_scalar_mul(st[:, 3:4], st[:, 1:2], 1.0 / D).then_inc(SM.vv, 1)
                        vvc[0] += 1
                        v.wait_ge(SM.vv, vvc[0])
                        v.tensor_tensor(st[:, 5:6], st[:, 2:3], st[:, 2:3], ALU.mult).then_inc(SM.vv, 1)
                        vvc[0] += 1
                        v.wait_ge(SM.vv, vvc[0])
                        v.tensor_tensor(st[:, 4:5], st[:, 3:4], st[:, 5:6], ALU.subtract).then_inc(SM.vv, 1)
                        vvc[0] += 1
                        v.wait_ge(SM.vv, vvc[0])
                        v.tensor_scalar_add(st[:, 6:7], st[:, 4:5], EPS).then_inc(SM.ln_v1, 1)
                        v.wait_ge(SM.ln_sqr, c["sqr"] + tc + 1)
                        v.reciprocal(st[:, 5:6], st[:, 7:8]).then_inc(SM.vv, 1)
                        vvc[0] += 1
                        if tc >= 2:
                            v.wait_ge(SM.ln_t, c["t"] + 4 * (tc - 1))
                        v.wait_ge(SM.vv, vvc[0])
                        v.tensor_scalar(xnv, xv, st[:, 2:3], st[:, 5:6],
                                        ALU.subtract, ALU.mult).then_inc(SM.ln_n, 1)
                    if tc >= 1:
                        for grp in range(4):
                            gidx = (tc - 1) * 4 + grp
                            v.wait_ge(SM.ln_t, c["t"] + gidx + 1)
                            v.tensor_copy(
                                bass.AP(xnT, (grp * 8) * TOK + (tc - 1) * 128,
                                        [[DC * TOK, 128], [TOK, 8], [1, 128]]),
                                ptr[:, (gidx % 2) * 1024:(gidx % 2 + 1) * 1024].rearrange(
                                    "p (j e) -> p j e", j=8)
                            ).then_inc(SM.ln_e, 1)

            @blk.tensor
            def _(t):
                for tc in range(8):
                    off = (tc % 2) * D
                    t.wait_ge(SM.ln_n, c["n"] + tc + 1)
                    for grp in range(4):
                        gidx = tc * 4 + grp
                        if gidx >= 2:
                            t.wait_ge(SM.ln_e, c["e"] + gidx - 1)
                        for jj in range(8):
                            j = grp * 8 + jj
                            ins = t.transpose(
                                ptr[:, (gidx % 2) * 1024 + jj * 128:(gidx % 2) * 1024 + (jj + 1) * 128],
                                xn[:, off + j * 128:off + (j + 1) * 128],
                                ident_bf[:, :])
                            if jj == 7:
                                ins.then_inc(SM.ln_t, 1)

        ln_ctr["ld0"] += 64
        ln_ctr["ld1"] += 64
        ln_ctr["sq"] += 8
        ln_ctr["v1"] += 8
        ln_ctr["sqr"] += 8
        ln_ctr["n"] += 8
        ln_ctr["t"] += 32
        ln_ctr["e"] += 32

    def phase12():
        es12 = ExitStack()
        xn1T = es12.enter_context(nc.sbuf_tensor("xn1T", [128, DC * TOK], BF16))
        emit_ln(x_in, xn1T, "ln1")
        barrier()

        if PHASES == 1:
            @blk.gpsimd
            def _(g):
                for j in range(8):
                    g.dma_start(out=bass.AP(y_out.tensor, j * 128 * D, [[D, 128], [1, D]]),
                                in_=xn1T[:, j * 4096:(j + 1) * 4096]).then_inc(SM.dbg, 16)
                g.wait_ge(SM.dbg, 128)
            es12.close()
            return

        with (
            nc.sbuf_tensor("wpan", [128, 2 * 4096], BF16) as wpan,
            nc.sbuf_tensor("qkst", [128, 2 * 1024], BF16) as qkst,
            nc.sbuf_tensor("vtst", [128, 2 * 1024], BF16) as vtst,
            nc.psum_tensor("pQ", [128, 2048], F32) as pQ,
            nc.psum_tensor("pVt", [128, 2048], BF16) as pVt,
        ):
            def w_src(oc):
                if oc < 32:
                    return bass.AP(wkv.tensor, oc * 128,
                                   [[8256, 128], [8256 * 128, 32], [1, 128]])
                if oc < 64:
                    return bass.AP(wkv.tensor, 4096 + (oc - 32) * 128,
                                   [[8256, 128], [8256 * 128, 32], [1, 128]])
                return bass.AP(wq.tensor, (oc - 64) * 192,
                               [[6144, 128], [6144 * 128, 32], [1, 128]])

            @blk.gpsimd
            def _(g):
                for oc in range(96):
                    if oc >= 2:
                        g.wait_ge(SM.p2mm, oc - 1)
                    g.dma_start(out=wpan[:, (oc % 2) * 4096:(oc % 2 + 1) * 4096]
                                .rearrange("p (cc e) -> p cc e", cc=32),
                                in_=w_src(oc)).then_inc(SM.ld_w[oc % 2], 16)

            @blk.tensor
            def _(t):
                for oc in range(96):
                    t.wait_ge(SM.ld_w[oc % 2], 16 * (oc // 2 + 1))
                    if oc >= 2:
                        t.wait_ge(SM.p2ev, oc - 1)
                    pb = (oc % 2) * 1024
                    for dcc in range(DC):
                        lhs = wpan[:, (oc % 2) * 4096 + dcc * 128:(oc % 2) * 4096 + (dcc + 1) * 128]
                        for tt in range(2):
                            ins = t.matmul(pQ[:, pb + tt * 512:pb + (tt + 1) * 512],
                                           lhs,
                                           xn1T[:, dcc * TOK + tt * 512:dcc * TOK + (tt + 1) * 512],
                                           start=(dcc == 0), stop=(dcc == DC - 1))
                            if dcc == DC - 1 and tt == 1:
                                ins.then_inc(SM.p2mm, 1)
                    if 33 <= oc <= 64:
                        hh = oc - 33
                        t.wait_ge(SM.p2ev, (32 + hh) + 1)
                        if hh >= 2:
                            t.wait_ge(SM.p2ve, hh - 1)
                        for tc2 in range(8):
                            ins = t.transpose(
                                pVt[:, (hh % 2) * 1024 + tc2 * 128:(hh % 2) * 1024 + (tc2 + 1) * 128],
                                qkst[:, ((32 + hh) % 2) * 1024 + tc2 * 128:
                                     ((32 + hh) % 2) * 1024 + (tc2 + 1) * 128],
                                ident_bf[:, :])
                            if tc2 == 7:
                                ins.then_inc(SM.p2tr, 1)

            @blk.scalar
            def _(s):
                for oc in range(96):
                    s.wait_ge(SM.p2mm, oc + 1)
                    if oc >= 2:
                        prev = oc - 2
                        if prev < 32:
                            s.wait_ge(SM.st_k[prev % 2], 16 * (prev // 2 + 1))
                        elif prev < 64:
                            s.wait_ge(SM.p2tr, prev - 32 + 1)
                        else:
                            s.wait_ge(SM.st_qq[prev % 2], 16 * ((prev - 64) // 2 + 1))
                    s.activation(qkst[:, (oc % 2) * 1024:(oc % 2 + 1) * 1024],
                                 pQ[:, (oc % 2) * 1024:(oc % 2 + 1) * 1024],
                                 AF.Copy).then_inc(SM.p2ev, 1)

            @blk.sync
            def _(sy):
                for oc in range(96):
                    src = qkst[:, (oc % 2) * 1024:(oc % 2 + 1) * 1024]
                    if 32 <= oc < 64:
                        hh = oc - 32
                        sy.wait_ge(SM.p2ve, hh + 1)
                        sy.dma_start(out=bass.AP(kvV_in.tensor, hh * TOK * DH,
                                                 [[DH, 128], [128 * DH, 8], [1, DH]]),
                                     in_=vtst[:, (hh % 2) * 1024:(hh % 2 + 1) * 1024]
                                     .rearrange("p (tc e) -> p tc e", tc=8)
                                     ).then_inc(SM.st_v[hh % 2], 16)
                    if oc < 32:
                        sy.wait_ge(SM.p2ev, oc + 1)
                        sy.dma_start(out=bass.AP(kvK_in.tensor, oc * 8 * 16384,
                                                 [[128, 128], [16384, 8], [1, 128]]),
                                     in_=src.rearrange("p (c e) -> p c e", c=8)
                                     ).then_inc(SM.st_k[oc % 2], 16)
                    elif oc >= 64:
                        sy.wait_ge(SM.p2ev, oc + 1)
                        sy.dma_start(out=qT_dram[(oc - 64) * 128:(oc - 63) * 128, :], in_=src
                                     ).then_inc(SM.st_qq[oc % 2], 16)

            @blk.vector
            def _(v):
                for hh in range(32):
                    v.wait_ge(SM.p2tr, hh + 1)
                    if hh >= 2:
                        v.wait_ge(SM.st_v[hh % 2], 16 * (hh // 2))
                    v.tensor_copy(vtst[:, (hh % 2) * 1024:(hh % 2 + 1) * 1024],
                                  pVt[:, (hh % 2) * 1024:(hh % 2 + 1) * 1024]
                                  ).then_inc(SM.p2ve, 1)

            @blk.gpsimd
            def _(g):
                g.wait_ge(SM.st_k[0], 16 * 16)
                g.wait_ge(SM.st_k[1], 16 * 16)
                g.wait_ge(SM.st_v[0], 16 * 16)
                g.wait_ge(SM.st_v[1], 16 * 16)
                g.collective_compute("AllGather", ALU.bypass,
                                     replica_groups=[list(range(NCORES))],
                                     ins=[kvK_in], outs=[kvK_out]).then_inc(SM.cc, 1)
                g.collective_compute("AllGather", ALU.bypass,
                                     replica_groups=[list(range(NCORES))],
                                     ins=[kvV_in], outs=[kvV_out]).then_inc(SM.cc, 1)

            @blk.sync
            def _(sy):
                sy.wait_ge(SM.st_qq[0], 16 * 16)
                sy.wait_ge(SM.st_qq[1], 16 * 16)
                sy.wait_ge(SM.cc, 2)
        es12.close()
        barrier()

        if PHASES == 2:
            @blk.gpsimd
            def _(g):
                g.dma_start(out=bass.AP(y_out.tensor, 0, [[1024, 4096], [1, 1024]]),
                            in_=bass.AP(qT_dram.tensor, 0, [[1024, 4096], [1, 1024]])
                            ).then_inc(SM.dbg, 16)
                g.wait_ge(SM.dbg, 16)

    def phase3(attnT):
        DIAG = set(range(4)) | set(range(92, 96))
        SLOTS = NKC
        FK = 2 * SLOTS * 128
        with (
            nc.sbuf_tensor("K_sb", [128, FK], BF16) as K_sb,
            nc.sbuf_tensor("V_sb", [128, FK], BF16) as V_sb,
            nc.sbuf_tensor("Q_sb", [128, 2 * TOK], BF16) as Q_sb,
            nc.sbuf_tensor("probs", [128, 4 * 512], BF16) as probs,
            nc.sbuf_tensor("invs", [1, 1024], FP16) as invs,
            nc.sbuf_tensor("invs32", [1, 1024], F32) as invs32,
            nc.sbuf_tensor("bcs", [128, 1024], F32) as bcs,
            nc.psum_tensor("s_ps", [128, 2048], F32) as s_ps,
            nc.psum_tensor("a_ps", [128, 1024], F32) as a_ps,
            nc.psum_tensor("z_ps", [33, 512], F32) as z_ps,
            nc.psum_tensor("b_ps", [128, 512], F32) as b_ps,
        ):
            @blk.gpsimd
            def _(g):
                jobs = ([(qwg, qwg_bf, j) for j in range(32)] +
                        [(qwu, qwu_bf, j) for j in range(32)])
                for n, (qw_d, qw_b, j) in enumerate(jobs):
                    g.wait_ge(SM.pa, SLOTS * (n // 2))
                    g.dma_start(out=qw_b[j * 128:(j + 1) * 128, :],
                                in_=qw_d[j * 128:(j + 1) * 128, :]
                                ).then_inc(SM.pcast, 16)

            @blk.sync
            def _(sy):
                for h in range(H):
                    if h >= 2:
                        sy.wait_ge(SM.pa, SLOTS * (h - 1))
                    hk = (h % 2) * SLOTS * 128
                    for kv_out, kv_loc, sbuf_t in ((kvK_out, kvK_in, K_sb),
                                                   (kvV_out, kvV_in, V_sb)):
                        sy.dma_start(
                            out=bass.AP(sbuf_t, hk, [[FK, 128], [128, 4], [1, 128]]),
                            in_=bass.AP(kv_loc.tensor, (h * 8) * 16384,
                                        [[128, 128], [16384, 4], [1, 128]]),
                        ).then_inc(SM.ld_kv[h % 2], 16)
                        for gb in range(7):
                            sy.dma_start(
                                out=bass.AP(sbuf_t, hk + (4 + gb * 4) * 128,
                                            [[FK, 128], [128, 4], [1, 128]]),
                                in_=bass.AP(kv_out.tensor, (gb * H * 8 + h * 8) * 16384,
                                            [[128, 128], [16384, 4], [1, 128]]),
                            ).then_inc(SM.ld_kv[h % 2], 16)
                        for gb in range(15):
                            if gb < 8:
                                soff = (gb * H * 8 + h * 8) * 16384
                            else:
                                soff = ((15 - gb) * H * 8 + h * 8 + 4) * 16384
                            sy.dma_start(
                                out=bass.AP(sbuf_t, hk + (32 + gb * 4) * 128,
                                            [[FK, 128], [128, 4], [1, 128]]),
                                in_=bass.AP(kv_out.tensor, soff,
                                            [[128, 128], [16384, 4], [1, 128]]),
                            ).then_inc(SM.ld_kv[h % 2], 16)
                        sy.dma_start(
                            out=bass.AP(sbuf_t, hk + 92 * 128, [[FK, 128], [128, 4], [1, 128]]),
                            in_=bass.AP(kv_loc.tensor, (h * 8 + 4) * 16384,
                                        [[128, 128], [16384, 4], [1, 128]]),
                        ).then_inc(SM.ld_kv[h % 2], 16)
                    sy.dma_start(out=Q_sb[:, (h % 2) * TOK:(h % 2 + 1) * TOK],
                                 in_=qT_dram[h * 128:(h + 1) * 128, :]
                                 ).then_inc(SM.ld_kv[h % 2], 16)

            @blk.tensor
            def _(t):
                for h in range(H):
                    hk = (h % 2) * SLOTS * 128
                    hq = (h % 2) * TOK
                    t.wait_ge(SM.ld_kv[h % 2], 16 * 49 * (h // 2 + 1))
                    if h >= 1:
                        t.wait_ge(SM.sat, 2 * h)
                        t.wait_ge(SM.siv, 2 * h)
                    for qb in range(2):
                        t.matmul(bass.AP(a_ps, qb * 512, [[1024, 128], [1, 512]]),
                                 zero_t[:, :], masks_sb[:, 0:512],
                                 start=True, stop=True, skip_group_check=True)
                    t.matmul(z_ps[0:33, :], zero_t[:, 0:33], masks_sb[:, 0:512],
                             start=True, stop=True, skip_group_check=True)

                    def emit_s(m):
                        qb = 0 if m < 32 else 1
                        if m >= 4:
                            t.wait_ge(SM.ex, SLOTS * h + m - 3)
                        t.matmul(s_ps[:, (m % 4) * 512:(m % 4 + 1) * 512],
                                 K_sb[:, hk + m * 128:hk + (m + 1) * 128],
                                 Q_sb[:, hq + qb * 512:hq + (qb + 1) * 512],
                                 start=True, stop=True, skip_group_check=True
                                 ).then_inc(SM.se, 1)

                    emit_s(0)
                    emit_s(1)
                    for m in range(SLOTS):
                        qb = 0 if m < 32 else 1
                        if m + 2 < SLOTS:
                            emit_s(m + 2)
                        t.wait_ge(SM.ep, SLOTS * h + m + 1)
                        t.matmul(bass.AP(z_ps, qb * 32 * 512, [[512, 1], [1, 512]]),
                                 ones_col[:, :],
                                 probs[:, (m % 4) * 512:(m % 4 + 1) * 512],
                                 start=False, stop=False, skip_group_check=True)
                        t.matmul(bass.AP(a_ps, qb * 512, [[1024, 128], [1, 512]]),
                                 V_sb[:, hk + m * 128:hk + (m + 1) * 128],
                                 probs[:, (m % 4) * 512:(m % 4 + 1) * 512],
                                 start=False, stop=False, skip_group_check=True
                                 ).then_inc(SM.pa, 1)
                    for qb in range(2):
                        t.wait_ge(SM.siv, 2 * h + qb + 1)
                        if 2 * h + qb >= 1:
                            t.wait_ge(SM.sbe, 2 * h + qb)
                        t.matmul(b_ps[:, :], ones1r_sb[:, :],
                                 invs[0:1, qb * 512:(qb + 1) * 512],
                                 start=True, stop=True, skip_group_check=True
                                 ).then_inc(SM.p3bc, 1)

            @blk.scalar
            def _(s):
                for h in range(H):
                    for m in range(SLOTS):
                        s.wait_ge(SM.se, SLOTS * h + m + 1)
                        if SLOTS * h + m >= 4:
                            s.wait_ge(SM.pa, SLOTS * h + m - 3)
                        s.activation(probs[:, (m % 4) * 512:(m % 4 + 1) * 512],
                                     s_ps[:, (m % 4) * 512:(m % 4 + 1) * 512],
                                     AF.Exp, bias=expb_sb[:, :], scale=SCALE
                                     ).then_inc(SM.ex, 1)

            @blk.vector
            def _(v):
                for h in range(H):
                    for m in range(SLOTS):
                        v.wait_ge(SM.ex, SLOTS * h + m + 1)
                        pr = probs[:, (m % 4) * 512:(m % 4 + 1) * 512]
                        if m in DIAG:
                            w = m if m < 4 else m - 92
                            v.tensor_tensor(pr, pr, masks_sb[:, w * 512:(w + 1) * 512],
                                            ALU.mult).then_inc(SM.ep, 1)
                        else:
                            v.tensor_scalar_mul(pr, pr, tbl_sb[:, m:m + 1]
                                                ).then_inc(SM.ep, 1)
                    v.wait_ge(SM.pa, SLOTS * (h + 1))
                    for qb in range(2):
                        v.reciprocal(invs32[0:1, qb * 512:(qb + 1) * 512],
                                     z_ps[32 * qb:32 * qb + 1, :]).then_inc(SM.vv, 1)
                        vvc[0] += 1
                        v.wait_ge(SM.vv, vvc[0])
                        v.tensor_copy(invs[0:1, qb * 512:(qb + 1) * 512],
                                      invs32[0:1, qb * 512:(qb + 1) * 512]
                                      ).then_inc(SM.siv, 1)
                    for qb in range(2):
                        v.wait_ge(SM.p3bc, 2 * h + qb + 1)
                        v.tensor_copy(bcs[:, qb * 512:(qb + 1) * 512], b_ps[:, :]
                                      ).then_inc(SM.sbe, 1)
                        v.wait_ge(SM.sbe, 2 * h + qb + 1)
                        v.tensor_tensor(attnT[:, h * TOK + qb * 512:h * TOK + (qb + 1) * 512],
                                        a_ps[:, qb * 512:(qb + 1) * 512],
                                        bcs[:, qb * 512:(qb + 1) * 512],
                                        ALU.mult).then_inc(SM.sat, 1)
        barrier()

    def phase4(attnT):
        with (
            nc.sbuf_tensor("wopan", [128, 2 * 4096], BF16) as wopan,
            nc.sbuf_tensor("wsb", [128, 2 * 1024], F32) as wsb,
            nc.sbuf_tensor("xp4", [128, 2 * 512], F32) as xp,
            nc.sbuf_tensor("x1p", [128, 2 * 512], F32) as x1p,
            nc.psum_tensor("pW", [128, 2048], F32) as pW,
            nc.psum_tensor("pT4", [128, 1024], F32) as pT,
        ):
            @blk.gpsimd
            def _(g):
                for dc in range(DC):
                    if dc >= 2:
                        g.wait_ge(SM.wo_mm, dc - 1)
                    g.dma_start(out=wopan[:, (dc % 2) * 4096:(dc % 2 + 1) * 4096]
                                .rearrange("p (cc e) -> p cc e", cc=32),
                                in_=bass.AP(wo.tensor, dc * 128,
                                            [[D, 128], [D * 128, 32], [1, 128]])
                                ).then_inc(SM.ld_wo[dc % 2], 16)

            @blk.sync
            def _(sy):
                for gidx in range(64):
                    dc, gg = gidx // 2, gidx % 2
                    if gidx >= 2:
                        sy.wait_ge(SM.wo_add, gidx - 1)
                    sy.dma_start(out=xp[:, (gidx % 2) * 512:(gidx % 2 + 1) * 512]
                                 .rearrange("p (tc e) -> p tc e", tc=4),
                                 in_=bass.AP(x_in.tensor, gg * 512 * D + dc * 128,
                                             [[D, 128], [128 * D, 4], [1, 128]])
                                 ).then_inc(SM.ld_xp[gidx % 2], 16)

            @blk.tensor
            def _(t):
                for dc in range(DC):
                    t.wait_ge(SM.ld_wo[dc % 2], 16 * (dc // 2 + 1))
                    if dc >= 2:
                        t.wait_ge(SM.wo_ev, dc - 1)
                    pb = (dc % 2) * 1024
                    for hc in range(32):
                        lhs = wopan[:, (dc % 2) * 4096 + hc * 128:(dc % 2) * 4096 + (hc + 1) * 128]
                        for tt in range(2):
                            ins = t.matmul(pW[:, pb + tt * 512:pb + (tt + 1) * 512],
                                           lhs,
                                           attnT[:, hc * TOK + tt * 512:hc * TOK + (tt + 1) * 512],
                                           start=(hc == 0), stop=(hc == 31))
                            if hc == 31 and tt == 1:
                                ins.then_inc(SM.wo_mm, 1)
                    t.wait_ge(SM.wo_ev, dc + 1)
                    for gg in range(2):
                        gidx = dc * 2 + gg
                        if gidx >= 2:
                            t.wait_ge(SM.wo_add, gidx - 1)
                        for tc2 in range(4):
                            ins = t.transpose(
                                pT[:, (gidx % 2) * 512 + tc2 * 128:(gidx % 2) * 512 + (tc2 + 1) * 128],
                                wsb[:, (dc % 2) * 1024 + (gg * 4 + tc2) * 128:
                                    (dc % 2) * 1024 + (gg * 4 + tc2 + 1) * 128],
                                ident_f32[:, :])
                            if tc2 == 3:
                                ins.then_inc(SM.wo_tr, 1)

            @blk.vector
            def _(v):
                for dc in range(DC):
                    v.wait_ge(SM.wo_mm, dc + 1)
                    if dc >= 2:
                        v.wait_ge(SM.wo_tr, 2 * (dc - 1))
                    v.tensor_copy(wsb[:, (dc % 2) * 1024:(dc % 2 + 1) * 1024],
                                  pW[:, (dc % 2) * 1024:(dc % 2 + 1) * 1024]
                                  ).then_inc(SM.wo_ev, 1)
                    for gg in range(2):
                        gidx = dc * 2 + gg
                        v.wait_ge(SM.wo_tr, gidx + 1)
                        v.wait_ge(SM.ld_xp[gidx % 2], 16 * (gidx // 2 + 1))
                        if gidx >= 2:
                            v.wait_ge(SM.st_x1[gidx % 2], 16 * (gidx // 2))
                        v.tensor_tensor(x1p[:, (gidx % 2) * 512:(gidx % 2 + 1) * 512],
                                        pT[:, (gidx % 2) * 512:(gidx % 2 + 1) * 512],
                                        xp[:, (gidx % 2) * 512:(gidx % 2 + 1) * 512],
                                        ALU.add).then_inc(SM.wo_add, 1)

            @blk.scalar
            def _(s):
                for gidx in range(64):
                    dc, gg = gidx // 2, gidx % 2
                    s.wait_ge(SM.wo_add, gidx + 1)
                    s.dma_start(out=bass.AP(x1_dram.tensor, gg * 512 * D + dc * 128,
                                            [[D, 128], [128 * D, 4], [1, 128]]),
                                in_=x1p[:, (gidx % 2) * 512:(gidx % 2 + 1) * 512]
                                .rearrange("p (tc e) -> p tc e", tc=4)
                                ).then_inc(SM.st_x1[gidx % 2], 16)
                s.wait_ge(SM.st_x1[0], 16 * 32)
                s.wait_ge(SM.st_x1[1], 16 * 32)
        barrier()

    def phase5():
        es5 = ExitStack()
        xn2T = es5.enter_context(nc.sbuf_tensor("xn2T", [128, DC * TOK], BF16))
        Xg_sb = es5.enter_context(nc.sbuf_tensor("Xg_sb", [32, TOK], BF16))
        sd_sb = es5.enter_context(nc.sbuf_tensor("sd_sb", [GFF, D], FP16))
        emit_ln(x1_dram, xn2T, "ln2")
        barrier()

        with (
            nc.sbuf_tensor("xgst", [1, 1024], BF16) as xgst,
            nc.psum_tensor("pxg", [1, 1024], F32) as pxg,
        ):
            @blk.tensor
            def _(t):
                for i in range(64):
                    dcc, tt = i // 2, i % 2
                    if i >= 2:
                        t.wait_ge(SM.xg_ev, i - 1)
                    t.matmul(pxg[0:1, (i % 2) * 512:(i % 2 + 1) * 512],
                             ones_col[:, :],
                             xn2T[:, dcc * TOK + tt * 512:dcc * TOK + (tt + 1) * 512],
                             start=True, stop=True, skip_group_check=True
                             ).then_inc(SM.xg_mm, 1)

            @blk.vector
            def _(v):
                for i in range(64):
                    v.wait_ge(SM.xg_mm, i + 1)
                    if i >= 2:
                        v.wait_ge(SM.st_xg[i % 2], 16 * (i // 2))
                    v.tensor_copy(xgst[0:1, (i % 2) * 512:(i % 2 + 1) * 512],
                                  pxg[0:1, (i % 2) * 512:(i % 2 + 1) * 512]
                                  ).then_inc(SM.xg_ev, 1)

            @blk.sync
            def _(sy):
                for i in range(64):
                    dcc, tt = i // 2, i % 2
                    sy.wait_ge(SM.xg_ev, i + 1)
                    sy.dma_start(out=Xg_sb[dcc:dcc + 1, tt * 512:(tt + 1) * 512],
                                 in_=xgst[0:1, (i % 2) * 512:(i % 2 + 1) * 512]
                                 ).then_inc(SM.st_xg[i % 2], 16)
                sy.wait_ge(SM.st_xg[0], 16 * 32)
                sy.wait_ge(SM.st_xg[1], 16 * 32)
        barrier()

        @blk.gpsimd
        def _(g):
            g.dma_start(out=sd_sb[:, :], in_=sd[:, :]).then_inc(SM.ld_sd, 16)

        with (
            nc.sbuf_tensor("gbuf", [128, 2 * 8192], BF16) as gbuf,
            nc.sbuf_tensor("ubuf", [128, 2 * 8192], BF16) as ubuf,
            nc.sbuf_tensor("sgp", [32, 2 * 512], FP16) as sgp,
            nc.sbuf_tensor("nzs", [32, 8 * 128], BF16) as nzs,
            nc.sbuf_tensor("sgact", [128, 1024], BF16) as sgact,
            nc.sbuf_tensor("hstage", [128, 2 * 1024], BF16) as hstage,
            nc.sbuf_tensor("qwdb", [128, 2 * 4096], BF16) as qwdb,
            nc.sbuf_tensor("ohdb", [GFF, 2 * 128], FP16) as ohdb,
            nc.psum_tensor("pG", [128, 1024], F32) as pG,
            nc.psum_tensor("pU", [128, 1024], F32) as pU,
            nc.psum_tensor("pbc", [128, 1024], F32) as pbc,
            nc.psum_tensor("pbc2", [128, 1024], F32) as pbc2,
        ):
            @blk.gpsimd
            def _(g):
                g.wait_ge(SM.pcast, 16 * 64)
                for q in range(NPAN):
                    if q >= 2:
                        g.wait_ge(SM.g_mm, 4 * (q - 1))
                    for qw_d, buf in ((qwg_bf, gbuf), (qwu_bf, ubuf)):
                        g.dma_start(out=buf[:, (q % 2) * 8192:(q % 2 + 1) * 8192]
                                    .rearrange("p (cc e) -> p cc e", cc=32),
                                    in_=bass.AP(qw_d.tensor, q * 256,
                                                [[HFF, 128], [HFF * 128, 32], [1, 256]])
                                    ).then_inc(SM.ld_ff[q % 2], 16)
                    g.dma_start(out=sgp[0:32, (q % 2) * 512:(q % 2) * 512 + 256],
                                in_=sg[:, q * 256:(q + 1) * 256]).then_inc(SM.ld_ff[q % 2], 16)
                    g.dma_start(out=sgp[0:32, (q % 2) * 512 + 256:(q % 2 + 1) * 512],
                                in_=su[:, q * 256:(q + 1) * 256]).then_inc(SM.ld_ff[q % 2], 16)
                    for hh in range(2):
                        hcd = 2 * q + hh
                        if hcd >= 2:
                            g.wait_ge(SM.st_wd[hcd % 2], 16 * (hcd // 2))
                        g.dma_start(out=qwdb[:, (hcd % 2) * 4096:(hcd % 2 + 1) * 4096],
                                    in_=qwd[hcd * 128:(hcd + 1) * 128, :]
                                    ).then_inc(SM.ld_qd[hcd % 2], 16)

            @blk.sync
            def _(sy):
                for q in range(NPAN + 1):
                    if q < NPAN:
                        if q >= 2:
                            sy.wait_ge(SM.g_mm, 4 * (q - 1))
                        for mat, nz_d in enumerate([nzg_dram, nzu_dram]):
                            for hc in range(2):
                                slot = (q % 2) * 4 + mat * 2 + hc
                                sy.dma_start(out=nzs[0:32, slot * 128:(slot + 1) * 128],
                                             in_=nz_d[:, (2 * q + hc) * 128:(2 * q + hc + 1) * 128]
                                             ).then_inc(SM.ld_nz[q % 2], 16)
                        for hh in range(2):
                            hcd = 2 * q + hh
                            if hcd >= 2:
                                sy.wait_ge(SM.d_bc, 8 * (hcd - 1))
                            sy.dma_start(out=ohdb[0:GFF, (hcd % 2) * 128:(hcd % 2 + 1) * 128],
                                         in_=bass.AP(ohD_d.tensor, hcd * 128,
                                                     [[GFF * 128, GFF], [1, 128]])
                                         ).then_inc(SM.ld_oh[hcd % 2], 16)
                    if q >= 1:
                        for hh in range(2):
                            hcd = 2 * (q - 1) + hh
                            sy.wait_ge(SM.d_dq, 8 * (hcd + 1))
                            sy.dma_start(out=bass.AP(wd_dram.tensor, hcd * 128 * 128,
                                                     [[128, 128], [HFF * 128, 32], [1, 128]]),
                                         in_=qwdb[:, (hcd % 2) * 4096:(hcd % 2 + 1) * 4096]
                                         .rearrange("p (cc e) -> p cc e", cc=32)
                                         ).then_inc(SM.st_wd[hcd % 2], 16)
                        for hc in range(2):
                            hcg = 2 * (q - 1) + hc
                            sy.wait_ge(SM.s_hm, hcg + 1)
                            sy.dma_start(out=hact_dram[hcg * 128:(hcg + 1) * 128, :],
                                         in_=hstage[:, (hcg % 2) * 1024:(hcg % 2 + 1) * 1024]
                                         ).then_inc(SM.st_ha[hcg % 2], 16)
                sy.wait_ge(SM.st_ha[0], 16 * 43)
                sy.wait_ge(SM.st_ha[1], 16 * 43)
                sy.wait_ge(SM.st_wd[0], 16 * 43)
                sy.wait_ge(SM.st_wd[1], 16 * 43)

            @blk.tensor
            def _(t):
                nbc = 0
                nbc2 = 0
                for q in range(NPAN + 1):
                    if q < NPAN:
                        t.wait_ge(SM.ld_ff[q % 2], 16 * 4 * (q // 2 + 1))
                        for dcc in range(32):
                            nbc += 1
                            if nbc >= 3:
                                t.wait_ge(SM.g_dq, 2 * (nbc - 2))
                            t.matmul(pbc[:, (nbc % 2) * 512:(nbc % 2 + 1) * 512],
                                     oh32_sb[0:32, dcc * 128:(dcc + 1) * 128],
                                     sgp[0:32, (q % 2) * 512:(q % 2 + 1) * 512],
                                     start=True, stop=True, skip_group_check=True
                                     ).then_inc(SM.g_bc, 1)
                    if q >= 1:
                        qq = q - 1
                        t.wait_ge(SM.g_dq, 64 * (qq + 1))
                        t.wait_ge(SM.ld_nz[qq % 2], 16 * 4 * (qq // 2 + 1))
                        for hc in range(2):
                            for mat, (buf, ps) in enumerate([(gbuf, pG), (ubuf, pU)]):
                                if 4 * qq + 2 * hc + mat >= 2:
                                    if mat == 0:
                                        t.wait_ge(SM.s_silu, 2 * qq + hc)
                                    else:
                                        t.wait_ge(SM.s_hm, 2 * qq + hc)
                                for dcc in range(32):
                                    for tt in range(2):
                                        t.matmul(ps[:, tt * 512:(tt + 1) * 512],
                                                 buf[:, (qq % 2) * 8192 + dcc * 256 + hc * 128:
                                                     (qq % 2) * 8192 + dcc * 256 + hc * 128 + 128],
                                                 xn2T[:, dcc * TOK + tt * 512:dcc * TOK + (tt + 1) * 512],
                                                 start=(dcc == 0), stop=False,
                                                 skip_group_check=True)
                                slot = (qq % 2) * 4 + mat * 2 + hc
                                for tt in range(2):
                                    ins = t.matmul(ps[:, tt * 512:(tt + 1) * 512],
                                                   nzs[0:32, slot * 128:(slot + 1) * 128],
                                                   Xg_sb[0:32, tt * 512:(tt + 1) * 512],
                                                   start=False, stop=True,
                                                   skip_group_check=True)
                                    if tt == 1:
                                        ins.then_inc(SM.g_mm, 1)
                    if q < NPAN:
                        t.wait_ge(SM.ld_sd, 16)
                        for hh in range(2):
                            hcd = 2 * q + hh
                            t.wait_ge(SM.ld_oh[hcd % 2], 16 * (hcd // 2 + 1))
                            for dt in range(8):
                                nbc2 += 1
                                if nbc2 >= 3:
                                    t.wait_ge(SM.d_dq, nbc2 - 2)
                                t.matmul(pbc2[:, (nbc2 % 2) * 512:(nbc2 % 2 + 1) * 512],
                                         ohdb[0:GFF, (hcd % 2) * 128:(hcd % 2 + 1) * 128],
                                         sd_sb[0:GFF, dt * 512:(dt + 1) * 512],
                                         start=True, stop=True, skip_group_check=True
                                         ).then_inc(SM.d_bc, 1)

            @blk.scalar
            def _(s):
                for qq in range(NPAN):
                    for hc in range(2):
                        hcg = 2 * qq + hc
                        s.wait_ge(SM.g_mm, 4 * qq + 2 * hc + 1)
                        if hcg >= 1:
                            s.wait_ge(SM.s_hm, hcg)
                        s.activation(sgact[:, :], pG[:, :], AF.Silu).then_inc(SM.s_silu, 1)

            @blk.vector
            def _(v):
                ndq = 0
                ndq2 = 0
                for q in range(NPAN + 1):
                    if q < NPAN:
                        v.wait_ge(SM.ld_ff[q % 2], 16 * 4 * (q // 2 + 1))
                        for dcc in range(32):
                            ndq += 1
                            v.wait_ge(SM.g_bc, ndq)
                            for mat, buf in enumerate([gbuf, ubuf]):
                                v.tensor_tensor(
                                    buf[:, (q % 2) * 8192 + dcc * 256:(q % 2) * 8192 + (dcc + 1) * 256],
                                    buf[:, (q % 2) * 8192 + dcc * 256:(q % 2) * 8192 + (dcc + 1) * 256],
                                    pbc[:, (ndq % 2) * 512 + mat * 256:(ndq % 2) * 512 + (mat + 1) * 256],
                                    ALU.mult).then_inc(SM.g_dq, 1)
                    def emit_hmult(qq, hc):
                        hcg = 2 * qq + hc
                        v.wait_ge(SM.s_silu, hcg + 1)
                        v.wait_ge(SM.g_mm, 4 * qq + 2 * hc + 2)
                        if hcg >= 2:
                            v.wait_ge(SM.st_ha[hcg % 2], 16 * (hcg // 2))
                        v.tensor_tensor(hstage[:, (hcg % 2) * 1024:(hcg % 2 + 1) * 1024],
                                        pU[:, :], sgact[:, :],
                                        ALU.mult).then_inc(SM.s_hm, 1)

                    for hh in range(2):
                        if q >= 1:
                            emit_hmult(q - 1, hh)
                        if q < NPAN:
                            hcd = 2 * q + hh
                            v.wait_ge(SM.ld_qd[hcd % 2], 16 * (hcd // 2 + 1))
                            for dt in range(8):
                                ndq2 += 1
                                v.wait_ge(SM.d_bc, ndq2)
                                v.tensor_tensor(
                                    qwdb[:, (hcd % 2) * 4096 + dt * 512:(hcd % 2) * 4096 + (dt + 1) * 512],
                                    qwdb[:, (hcd % 2) * 4096 + dt * 512:(hcd % 2) * 4096 + (dt + 1) * 512],
                                    pbc2[:, (ndq2 % 2) * 512:(ndq2 % 2 + 1) * 512],
                                    ALU.mult).then_inc(SM.d_dq, 1)
        es5.close()
        barrier()

        if PHASES == 5:
            @blk.gpsimd
            def _(g):
                g.dma_start(out=bass.AP(y_out.tensor, 0, [[1024, 4096], [1, 1024]]),
                            in_=bass.AP(hact_dram.tensor, 0, [[1024, 4096], [1, 1024]])
                            ).then_inc(SM.dbg, 16)
                g.wait_ge(SM.dbg, 16)

    def phase6():
        with (
            nc.sbuf_tensor("hact_sb", [128, GFF * 512], BF16) as hact_sb,
            nc.sbuf_tensor("wdc", [128, 2 * GFF * 128], BF16) as wdc,
            nc.sbuf_tensor("nzd6", [GFF, 2 * 128], BF16) as nzd6,
            nc.sbuf_tensor("Ag_sb", [GFF, 512], BF16) as Ag_sb,
            nc.sbuf_tensor("agst", [1, 1024], BF16) as agst,
            nc.sbuf_tensor("dsb", [128, 2 * 512], F32) as dsb,
            nc.sbuf_tensor("xp6", [128, 2 * 512], F32) as xp6,
            nc.sbuf_tensor("yp6", [128, 2 * 512], F32) as yp6,
            nc.psum_tensor("pD", [128, 1024], F32) as pD,
            nc.psum_tensor("pT6", [128, 1024], F32) as pT6,
            nc.psum_tensor("pag", [1, 1024], F32) as pag,
        ):
            @blk.sync
            def _(sy):
                for t_ in range(2):
                    if t_ == 1:
                        sy.wait_ge(SM.d6_mm, 32)
                    sy.dma_start(out=hact_sb[:].rearrange("p (gg e) -> p gg e", gg=GFF),
                                 in_=bass.AP(hact_dram.tensor, t_ * 512,
                                             [[TOK, 128], [128 * TOK, GFF], [1, 512]])
                                 ).then_inc(SM.ld_ha, 16)
                    for dc in range(DC):
                        i = t_ * 32 + dc
                        if i >= 2:
                            sy.wait_ge(SM.d6_mm, i - 1)
                        sy.dma_start(out=wdc[:, (i % 2) * GFF * 128:(i % 2 + 1) * GFF * 128],
                                     in_=bass.AP(wd_dram.tensor, dc * HFF * 128,
                                                 [[128, 128], [128 * 128, GFF], [1, 128]])
                                     ).then_inc(SM.ld_wd[i % 2], 16)
                        if i >= 2:
                            sy.wait_ge(SM.d6_add, i - 1)
                        sy.dma_start(out=xp6[:, (i % 2) * 512:(i % 2 + 1) * 512]
                                     .rearrange("p (tc e) -> p tc e", tc=4),
                                     in_=bass.AP(x1_dram.tensor, t_ * 512 * D + dc * 128,
                                                 [[D, 128], [128 * D, 4], [1, 128]])
                                     ).then_inc(SM.ld_x1[i % 2], 16)

            @blk.scalar
            def _(s):
                for t_ in range(2):
                    if t_ == 1:
                        s.wait_ge(SM.d6_mm, 32)
                    for gg in range(GFF):
                        i2 = t_ * GFF + gg
                        s.wait_ge(SM.ag_ev, i2 + 1)
                        s.dma_start(out=Ag_sb[gg:gg + 1, :],
                                    in_=agst[0:1, (i2 % 2) * 512:(i2 % 2) * 512 + 512]
                                    ).then_inc(SM.st_ag[i2 % 2], 16)
                    def emit_out(i_):
                        dc_ = i_ - t_ * 32
                        s.wait_ge(SM.d6_add, i_ + 1)
                        s.dma_start(out=bass.AP(y_out.tensor, t_ * 512 * D + dc_ * 128,
                                                [[D, 128], [128 * D, 4], [1, 128]]),
                                    in_=yp6[:, (i_ % 2) * 512:(i_ % 2 + 1) * 512]
                                    .rearrange("p (tc e) -> p tc e", tc=4)
                                    ).then_inc(SM.st_out[i_ % 2], 16)

                    for dc in range(DC):
                        i = t_ * 32 + dc
                        if i >= 2:
                            s.wait_ge(SM.d6_mm, i - 1)
                        s.dma_start(out=nzd6[0:GFF, (i % 2) * 128:(i % 2 + 1) * 128],
                                    in_=nzd_dram[:, dc * 128:(dc + 1) * 128]
                                    ).then_inc(SM.ld_nzd[i % 2], 16)
                        if dc >= 2:
                            emit_out(i - 2)
                    emit_out(t_ * 32 + 30)
                    emit_out(t_ * 32 + 31)
                s.wait_ge(SM.st_out[0], 16 * 32)
                s.wait_ge(SM.st_out[1], 16 * 32)

            @blk.tensor
            def _(t):
                def emit_tr(ip):
                    t.wait_ge(SM.d6_ev, ip + 1)
                    if ip >= 2:
                        t.wait_ge(SM.d6_add, ip - 1)
                    for tc2 in range(4):
                        ins = t.transpose(
                            pT6[:, (ip % 2) * 512 + tc2 * 128:(ip % 2) * 512 + (tc2 + 1) * 128],
                            dsb[:, (ip % 2) * 512 + tc2 * 128:(ip % 2) * 512 + (tc2 + 1) * 128],
                            ident_f32[:, :])
                        if tc2 == 3:
                            ins.then_inc(SM.d6_tr, 1)

                for t_ in range(2):
                    t.wait_ge(SM.ld_ha, 16 * (t_ + 1))
                    for gg in range(GFF):
                        i2 = t_ * GFF + gg
                        if i2 >= 2:
                            t.wait_ge(SM.ag_ev, i2 - 1)
                        t.matmul(pag[0:1, (i2 % 2) * 512:(i2 % 2) * 512 + 512],
                                 ones_col[:, :],
                                 hact_sb[:, gg * 512:(gg + 1) * 512],
                                 start=True, stop=True, skip_group_check=True
                                 ).then_inc(SM.ag_mm, 1)
                    t.wait_ge(SM.st_ag[0], 16 * 43 * (t_ + 1))
                    t.wait_ge(SM.st_ag[1], 16 * 43 * (t_ + 1))
                    for dc in range(DC):
                        i = t_ * 32 + dc
                        t.wait_ge(SM.ld_wd[i % 2], 16 * (i // 2 + 1))
                        t.wait_ge(SM.ld_nzd[i % 2], 16 * (i // 2 + 1))
                        if i >= 2:
                            t.wait_ge(SM.d6_ev, i - 1)
                        for gg in range(GFF):
                            t.matmul(pD[:, (i % 2) * 512:(i % 2 + 1) * 512],
                                     wdc[:, (i % 2) * GFF * 128 + gg * 128:
                                         (i % 2) * GFF * 128 + (gg + 1) * 128],
                                     hact_sb[:, gg * 512:(gg + 1) * 512],
                                     start=(gg == 0), stop=False, skip_group_check=True)
                        t.matmul(pD[:, (i % 2) * 512:(i % 2 + 1) * 512],
                                 nzd6[0:GFF, (i % 2) * 128:(i % 2 + 1) * 128],
                                 Ag_sb[0:GFF, :],
                                 start=False, stop=True, skip_group_check=True
                                 ).then_inc(SM.d6_mm, 1)
                        if dc >= 1:
                            emit_tr(i - 1)
                    emit_tr(t_ * 32 + 31)

            @blk.vector
            def _(v):
                for t_ in range(2):
                    for gg in range(GFF):
                        i2 = t_ * GFF + gg
                        v.wait_ge(SM.ag_mm, i2 + 1)
                        if i2 >= 2:
                            v.wait_ge(SM.st_ag[i2 % 2], 16 * (i2 // 2))
                        v.tensor_copy(agst[0:1, (i2 % 2) * 512:(i2 % 2) * 512 + 512],
                                      pag[0:1, (i2 % 2) * 512:(i2 % 2) * 512 + 512]
                                      ).then_inc(SM.ag_ev, 1)
                    for dc in range(DC):
                        i = t_ * 32 + dc
                        v.wait_ge(SM.d6_mm, i + 1)
                        if i >= 2:
                            v.wait_ge(SM.d6_tr, i - 1)
                        v.tensor_copy(dsb[:, (i % 2) * 512:(i % 2 + 1) * 512],
                                      pD[:, (i % 2) * 512:(i % 2 + 1) * 512]
                                      ).then_inc(SM.d6_ev, 1)
                        v.wait_ge(SM.d6_tr, i + 1)
                        v.wait_ge(SM.ld_x1[i % 2], 16 * (i // 2 + 1))
                        if i >= 2:
                            v.wait_ge(SM.st_out[i % 2], 16 * (i // 2))
                        v.tensor_tensor(yp6[:, (i % 2) * 512:(i % 2 + 1) * 512],
                                        pT6[:, (i % 2) * 512:(i % 2 + 1) * 512],
                                        xp6[:, (i % 2) * 512:(i % 2 + 1) * 512],
                                        ALU.add).then_inc(SM.d6_add, 1)

    phase0()
    phase12()
    if PHASES >= 3:
        es34 = ExitStack()
        attnT = es34.enter_context(nc.sbuf_tensor("attnT", [128, H * TOK], BF16))
        phase3(attnT)
        if PHASES == 3:
            @blk.gpsimd
            def _(g):
                for j in range(8):
                    g.dma_start(out=bass.AP(y_out.tensor, j * 128 * D, [[D, 128], [1, D]]),
                                in_=attnT[:, j * 4096:(j + 1) * 4096]).then_inc(SM.dbg, 16)
                g.wait_ge(SM.dbg, 128)
    if PHASES >= 4:
        phase4(attnT)
        if PHASES == 4:
            @blk.gpsimd
            def _(g):
                g.dma_start(out=y_out[:, :], in_=x1_dram[:, :]).then_inc(SM.dbg, 16)
                g.wait_ge(SM.dbg, 16)
    if PHASES >= 3:
        es34.close()
    if PHASES >= 5:
        phase5()
    if PHASES >= 6:
        phase6()
        barrier()

    es.close()
    return nc


def kernel(**inputs):
    x = np.ascontiguousarray(np.asarray(inputs["L0_attn_hidden_states"])[0], dtype=np.float32)
    common = {
        "wq": np.ascontiguousarray(inputs["L0_attn_wq"], dtype=np.float32),
        "wkv": np.ascontiguousarray(inputs["L0_attn_wkv"], dtype=np.float32),
        "wo": np.ascontiguousarray(inputs["L0_attn_wo"], dtype=np.float32),
        "qwg": np.ascontiguousarray(inputs["L0_ffn_qw_gate"], dtype=np.int32),
        "qwu": np.ascontiguousarray(inputs["L0_ffn_qw_up"], dtype=np.int32),
        "qwd": np.ascontiguousarray(inputs["L0_ffn_qw_down"], dtype=np.int32),
        "zg": np.ascontiguousarray(inputs["L0_ffn_z_gate"], dtype=np.int32),
        "zu": np.ascontiguousarray(inputs["L0_ffn_z_up"], dtype=np.int32),
        "zd": np.ascontiguousarray(inputs["L0_ffn_z_down"], dtype=np.int32),
        "sg": np.ascontiguousarray(inputs["L0_ffn_s_gate"], dtype=np.float32),
        "su": np.ascontiguousarray(inputs["L0_ffn_s_up"], dtype=np.float32),
        "sd": np.ascontiguousarray(inputs["L0_ffn_s_down"], dtype=np.float32),
    }
    nc = build_nc()
    in_maps = []
    for c in range(NCORES):
        b0, b1 = c, 15 - c
        xs = np.concatenate([x[b0 * BS:(b0 + 1) * BS], x[b1 * BS:(b1 + 1) * BS]], axis=0)
        m = dict(common)
        m["x"] = np.ascontiguousarray(xs)
        m["tbl"] = make_tables(c)
        in_maps.append(m)
    res = run_bass_kernel_spmd(nc, in_maps, core_ids=list(range(NCORES)))
    out = np.empty((1, S, D), dtype=np.float32)
    for c in range(NCORES):
        b0, b1 = c, 15 - c
        yc = np.asarray(res.results[c]["y"])
        out[0, b0 * BS:(b0 + 1) * BS] = yc[:BS]
        out[0, b1 * BS:(b1 + 1) * BS] = yc[BS:]
    return out
